# version 12
# baseline (speedup 1.0000x reference)
import numpy as np
import ml_dtypes
from contextlib import ExitStack

import concourse.bass as bass
import concourse.mybir as mybir
from concourse.bass_utils import run_bass_kernel_spmd

F32 = mybir.dt.float32
BF16 = mybir.dt.bfloat16
U8 = mybir.dt.uint8
AF = mybir.ActivationFunctionType
ALU = mybir.AluOpType
AX = mybir.AxisListType

C = 1024
T = 4096
NCC = 8
NTB = 8
TB = 512
NKT = 32
NH = 16
HD = 64
EPS = 1e-5


class Sem:
    __slots__ = ("h", "n")

    def __init__(self, h):
        self.h = h
        self.n = 0

    def inc(self, instr, k=1):
        instr.then_inc(self.h, k)
        self.n += k
        return self.n


class Prog:
    def __init__(self, nc, st):
        self.nc = nc
        self.stacks = [st]
        self.blk = st.enter_context(nc.Block())
        self.used_inputs = set()
        for name in ("sS", "sG", "sQ", "sP", "sV", "sA", "sO"):
            setattr(self, name, Sem(st.enter_context(nc.semaphore(name))))

    serial = frozenset()

    def _wrap(self, f, kind):
        if kind not in self.serial:
            return f
        snapshot = self.snap()

        def g(eng):
            self.wait_snap(eng, snapshot)
            f(eng)
        return g

    def TEN(self, f): self.blk.tensor(self._wrap(f, "t"))
    def VEC(self, f): self.blk.vector(self._wrap(f, "v"))
    def SCA(self, f): self.blk.scalar(self._wrap(f, "a"))
    def SYN(self, f): self.blk.sync(self._wrap(f, "s"))
    def GPS(self, f): self.blk.gpsimd(self._wrap(f, "g"))

    def inp(self, name, shape, dtype):
        self.used_inputs.add(name)
        return self.nc.dram_tensor(name, shape, dtype, kind="ExternalInput").ap()

    def sb(self, name, shape, dtype):
        return self.stacks[-1].enter_context(self.nc.sbuf_tensor(name, shape, dtype))

    def ps(self, name, shape, dtype=F32):
        return self.stacks[-1].enter_context(self.nc.psum_tensor(name, shape, dtype))

    def push(self):
        st = ExitStack()
        self.stacks.append(st)
        return st

    def pop(self):
        st = self.stacks.pop()
        st.close()

    def snap(self):
        return {k: getattr(self, k).n
                for k in ("sS", "sG", "sQ", "sP", "sV", "sA")}

    def wait_snap(self, eng, snapshot):
        for k, v in snapshot.items():
            if v > 0:
                eng.wait_ge(getattr(self, k).h, v)


def emit_load_x(p, env):
    x = p.inp("x", [C, T], F32)
    xv = x.rearrange("(cc p) t -> cc p t", p=128)
    x_fm = env["x_fm"]

    def sec(s):
        for cc in range(NCC):
            p.sS.inc(s.dma_start(out=x_fm[:, cc * T:(cc + 1) * T], in_=xv[cc]), 16)
    p.SYN(sec)


def emit_store_x(p, env):
    xout = p.nc.dram_tensor("xout", [C, T], F32, kind="ExternalOutput").ap()
    xov = xout.rearrange("(cc p) t -> cc p t", p=128)
    x_fm = env["x_fm"]
    bar = p.snap()

    def sec(s):
        p.wait_snap(s, bar)
        for cc in range(NCC):
            p.sO.inc(s.dma_start(out=xov[cc], in_=x_fm[:, cc * T:(cc + 1) * T]), 16)
    p.SYN(sec)


def emit_final_wait(p):
    n = p.sO.n

    def sec(s):
        s.wait_ge(p.sO.h, n)
    p.SYN(sec)


def emit_ln(p, env, tag):
    x_fm = env["x_fm"]
    ones_col = env["ones_col"]
    ones_row = env["ones_row"]
    xn_scr_v = env["xn_scr_v"]
    bar = p.snap()

    p.push()
    xbf = [p.sb(f"ln{tag}_xbf{i}", [128, TB], BF16) for i in range(3)]
    sq = [p.sb(f"ln{tag}_sq{i}", [128, TB], BF16) for i in range(3)]
    rows = p.sb(f"ln{tag}_rows", [1, TB * 6], F32)
    mean_f, m2_f, var_f, sd_f, rstd_f, lo_f = (rows[:, i * TB:(i + 1) * TB] for i in range(6))
    mean_bf = p.sb(f"ln{tag}_meanbf", [1, TB], BF16)
    rstd_bf = p.sb(f"ln{tag}_rstdbf", [1, TB], BF16)
    mlo_bf = p.sb(f"ln{tag}_mlobf", [1, TB], BF16)
    rlo_bf = p.sb(f"ln{tag}_rlobf", [1, TB], BF16)
    xn_t = [p.sb(f"ln{tag}_xnt{i}", [128, NCC * TB], BF16) for i in range(2)]
    tsub = p.sb(f"ln{tag}_tsub", [128, TB], F32)
    psS = p.ps(f"ln{tag}_psS", [1, TB])
    psQ = p.ps(f"ln{tag}_psQ", [1, TB])
    psM = p.ps(f"ln{tag}_psM", [128, TB])
    psR = p.ps(f"ln{tag}_psR", [128, TB])

    stat_mark = {}
    mean_done = {}
    var_done = {}
    rows_done = {}
    bcast_done = {}
    norm_done = {}
    store_done = {}

    pb0 = p.sP.n
    pred_stat = lambda j: pb0 + (j // NCC) * 9 + (j % NCC) + 1
    cast_marks = {}

    for tb in range(NTB):
        def a_cast(s, tb=tb):
            for cc in range(NCC):
                idx = tb * NCC + cc
                if idx == 0:
                    p.wait_snap(s, bar)
                if idx >= 3:
                    s.wait_ge(p.sP.h, pred_stat(idx - 3))
                srcv = x_fm[:, cc * T + tb * TB: cc * T + tb * TB + TB]
                b = idx % 3
                s.activation(xbf[b][:], srcv, AF.Copy)
                cast_marks[idx] = p.sA.inc(s.activation(sq[b][:], srcv, AF.Square))
        p.SCA(a_cast)

        for cc in range(NCC):
            idx = tb * NCC + cc

            def t_sec(t, tb=tb, cc=cc, idx=idx):
                t.wait_ge(p.sA.h, cast_marks[idx])
                if cc == 0 and tb > 0:
                    t.wait_ge(p.sA.h, mean_done[tb - 1])
                    t.wait_ge(p.sV.h, var_done[tb - 1])
                b = idx % 3
                t.matmul(psS[:], ones_col[:], xbf[b][:],
                         start=(cc == 0), stop=(cc == NCC - 1),
                         skip_group_check=True)
                mm = t.matmul(psQ[:], ones_col[:], sq[b][:],
                              start=(cc == 0), stop=(cc == NCC - 1),
                              skip_group_check=True)
                stat_mark[idx] = p.sP.inc(mm)
            p.TEN(t_sec)

        sn = stat_mark[tb * NCC + NCC - 1]

        def a2_sec(s, tb=tb, sn=sn):
            s.wait_ge(p.sP.h, sn)
            mean_done[tb] = p.sA.inc(s.activation(mean_f, psS[:], AF.Copy, scale=1.0 / C))
        p.SCA(a2_sec)
        a2n = p.sA.n

        def v_sec(v, tb=tb, a2n=a2n, sn=sn):
            v.wait_ge(p.sA.h, a2n)
            v.wait_ge(p.sP.h, sn)
            v.tensor_tensor(m2_f, mean_f, mean_f, ALU.mult)
            var_done[tb] = p.sV.inc(
                v.scalar_tensor_tensor(var_f, psQ[:], 1.0 / C, m2_f,
                                       ALU.mult, ALU.subtract))
        p.VEC(v_sec)
        vn = p.sV.n

        def a3_sec(s, tb=tb, vn=vn):
            s.wait_ge(p.sV.h, vn)
            p.sA.inc(s.activation(sd_f, var_f, AF.Sqrt, bias=env["eps_t"][:]))
        p.SCA(a3_sec)
        a3n = p.sA.n

        def v2_sec(v, tb=tb, a3n=a3n):
            v.wait_ge(p.sA.h, a3n)
            v.reciprocal(rstd_f, sd_f)
            v.tensor_copy(mean_bf[:], mean_f)
            v.tensor_copy(rstd_bf[:], rstd_f)
            v.tensor_tensor(lo_f, mean_f, mean_bf[:], ALU.subtract)
            v.tensor_copy(mlo_bf[:], lo_f)
            v.tensor_tensor(lo_f, rstd_f, rstd_bf[:], ALU.subtract)
            rows_done[tb] = p.sV.inc(v.tensor_copy(rlo_bf[:], lo_f))
        p.VEC(v2_sec)

        def t2_sec(t, tb=tb):
            t.wait_ge(p.sV.h, rows_done[tb])
            if tb > 0:
                t.wait_ge(p.sV.h, norm_done[(tb - 1, NCC - 1)])
            t.matmul(psM[:], ones_row[:], mean_bf[:], start=True, stop=False,
                     skip_group_check=True)
            t.matmul(psM[:], ones_row[:], mlo_bf[:], start=False, stop=True,
                     skip_group_check=True)
            t.matmul(psR[:], ones_row[:], rstd_bf[:], start=True, stop=False,
                     skip_group_check=True)
            bcast_done[tb] = p.sP.inc(
                t.matmul(psR[:], ones_row[:], rlo_bf[:], start=False, stop=True,
                         skip_group_check=True))
        p.TEN(t2_sec)

        for cc in range(NCC):
            def v3_sec(v, tb=tb, cc=cc):
                if cc == 0:
                    v.wait_ge(p.sP.h, bcast_done[tb])
                    if tb >= 2:
                        v.wait_ge(p.sS.h, store_done[tb - 2])
                src = x_fm[:, cc * T + tb * TB: cc * T + tb * TB + TB]
                v.tensor_tensor(tsub[:], src, psM[:], ALU.subtract)
                norm_done[(tb, cc)] = p.sV.inc(
                    v.tensor_tensor(xn_t[tb % 2][:, cc * TB:(cc + 1) * TB],
                                    tsub[:], psR[:], ALU.mult))
            p.VEC(v3_sec)

        def s_sec(s, tb=tb):
            s.wait_ge(p.sV.h, norm_done[(tb, NCC - 1)])
            store_done[tb] = p.sS.inc(
                s.dma_start(out=env["xn_scr_v"][:, :, tb * TB:(tb + 1) * TB],
                            in_=xn_t[tb % 2].rearrange("p (cc t) -> p cc t", cc=NCC)), 16)
        p.SYN(s_sec)

    env["xn_ready"] = dict(store_done)
    p.pop()


def emit_l0_attn(p, env):
    nc = p.nc
    x_fm = env["x_fm"]
    xn_scr_v = env["xn_scr_v"]
    ident, pswap = env["ident"], env["pswap"]
    xn_ready = env["xn_ready"]
    m_d = p.inp("cmask", [128, 4 * TB], BF16)

    wts = {nm: p.inp(f"l0{nm}", [C, C], F32) for nm in ("q1", "k1", "q2", "k2", "v", "o")}
    wview = {nm: a.rearrange("(cc p) o -> p cc o", p=128) for nm, a in wts.items()}
    ctab_d = p.inp("rope_c", [128, T], BF16)
    stab_d = p.inp("rope_s", [128, T], BF16)

    y_scr = nc.dram_tensor("y_scr", [C, T], BF16, kind="Internal").ap()
    y_scr_g = y_scr.rearrange("(g p) t -> g p t", p=128)
    y_scr_c = y_scr.rearrange("(cc p) t -> p cc t", p=128)
    q_scr = {nm: nc.dram_tensor(f"q_scr_{nm}", [128, T], BF16, kind="Internal").ap()
             for nm in ("q1", "q2")}

    names = ("q1", "k1", "q2", "k2", "v")

    p.push()
    masks = p.sb("l0a_masks", [128, 4 * TB], BF16)
    kk = {nm: p.sb(f"l0a_{nm}", [128, T], BF16) for nm in ("k1", "k2")}
    v_tm = p.sb("l0a_vtm", [128, NKT * 128], BF16)
    wbuf = {nm: p.sb(f"l0a_w_{nm}", [128, NCC * 128], BF16) for nm in names}
    xn_t = [p.sb(f"l0a_xnt{i}", [128, NCC * TB], BF16) for i in range(2)]
    cbuf = p.sb("l0a_cbuf", [128, TB], BF16)
    sbuf_t = p.sb("l0a_sbuf", [128, TB], BF16)
    ev = [p.sb(f"l0a_ev{i}", [128, TB], BF16) for i in range(2)]
    rope_t1 = p.sb("l0a_rt1", [128, TB], BF16)
    rope_t2 = p.sb("l0a_rt2", [128, TB], BF16)
    qob = [p.sb(f"l0a_qob{i}", [128, TB], BF16) for i in range(2)]
    qw_t = {nm: p.sb(f"l0a_qw_{nm}", [128, TB], BF16) for nm in ("q1", "q2")}
    s1m = [p.sb(f"l0a_s1m{i}", [128, TB], BF16) for i in range(4)]
    pat = [p.sb(f"l0a_pat{i}", [128, TB], BF16) for i in range(4)]
    ybuf = p.sb("l0a_ybuf", [128, TB], BF16)
    psyA = p.ps("l0a_psyA", [128, TB])
    psyB = p.ps("l0a_psyB", [128, TB])

    mk = {
        "ev_free": [0, 0],
        "ev_free_eng": ["V", "V"],
        "psp_free": [0, 0],
        "rope_dve": 0,
        "rope_tb": 0,
        "vtr_ev": 0,
        "qob_st": [0, 0],
        "s1ev_a": 0, "s1ev_v": 0,
        "patdve": 0,
        "av": 0,
        "yev": 0,
        "yst": 0,
        "qw_used": 0,
    }
    bar = p.snap()

    proj_tb_done = {}
    proj_g_done = {}
    q_store = {}

    for g in range(NCC):
        p.push()
        psp = [p.ps(f"l0a_psp_g{g}", [128, TB])]
        pssw = p.ps(f"l0a_pssw_g{g}", [128, TB])
        pstr = p.ps(f"l0a_pstr_g{g}", [128, 128], BF16)
        prev_pat_g = mk["patdve"]
        prev_yev_g = mk["yev"]

        def gw_sec(gq, g=g):
            if g == 0:
                p.wait_snap(gq, bar)
                p.sG.inc(gq.dma_start(out=masks[:], in_=m_d), 16)
            if g >= 1:
                gq.wait_ge(p.sP.h, proj_g_done[g - 1])
            for nm in names:
                p.sG.inc(gq.dma_start(out=wbuf[nm][:],
                                      in_=wview[nm][:, :, g * 128:(g + 1) * 128]), 16)
        p.GPS(gw_sec)
        wload_n = p.sG.n

        for tb in range(NTB):
            def s_sec(s, g=g, tb=tb):
                if g == 0 and tb == 0:
                    p.wait_snap(s, bar)
                if g == 0:
                    s.wait_ge(p.sS.h, xn_ready[tb])
                key = (g, tb - 2) if tb >= 2 else (g - 1, NTB - 2 + tb)
                if key in proj_tb_done:
                    s.wait_ge(p.sP.h, proj_tb_done[key])
                p.sS.inc(s.dma_start(
                    out=xn_t[tb % 2].rearrange("p (cc t) -> p cc t", cc=NCC),
                    in_=xn_scr_v[:, :, tb * TB:(tb + 1) * TB]), 16)
                if mk["rope_tb"]:
                    s.wait_ge(p.sV.h, mk["rope_tb"])
                p.sS.inc(s.dma_start(out=cbuf[:], in_=ctab_d[:, tb * TB:(tb + 1) * TB]), 16)
                p.sS.inc(s.dma_start(out=sbuf_t[:], in_=stab_d[:, tb * TB:(tb + 1) * TB]), 16)
            p.SYN(s_sec)
            xn_n = p.sS.n

            for pi, nm in enumerate(names):
                def t_sec(t, g=g, tb=tb, pi=pi, nm=nm, xn_n=xn_n, wload_n=wload_n,
                          prev_pat_g=prev_pat_g, prev_yev_g=prev_yev_g):
                    if pi == 0:
                        t.wait_ge(p.sS.h, xn_n)
                        t.wait_ge(p.sG.h, wload_n)
                    if g > 0 and tb == 0 and pi == 0:
                        if prev_pat_g:
                            t.wait_ge(p.sV.h, prev_pat_g)
                        if prev_yev_g:
                            t.wait_ge(p.sA.h, prev_yev_g)
                    if mk["psp_free"][0]:
                        t.wait_ge(p.sA.h, mk["psp_free"][0])
                    for cc in range(NCC):
                        mm = t.matmul(psp[0][:], wbuf[nm][:, cc * 128:(cc + 1) * 128],
                                      xn_t[tb % 2][:, cc * TB:(cc + 1) * TB],
                                      start=(cc == 0), stop=(cc == NCC - 1))
                    p.sP.inc(mm)
                p.TEN(t_sec)
                mm_n = p.sP.n

                def a_sec(s, pi=pi, mm_n=mm_n):
                    s.wait_ge(p.sP.h, mm_n)
                    eng, cnt = mk["ev_free_eng"][pi % 2], mk["ev_free"][pi % 2]
                    if cnt:
                        s.wait_ge((p.sV if eng == "V" else p.sP).h, cnt)
                    mk["psp_free"][0] = p.sA.inc(
                        s.activation(ev[pi % 2][:], psp[0][:], AF.Copy))
                p.SCA(a_sec)
                ev_n = p.sA.n

                if nm == "v":
                    for j in range(4):
                        def t3_sec(t, j=j, ev_n=ev_n, pi=pi):
                            if j == 0:
                                t.wait_ge(p.sA.h, ev_n)
                            if mk["vtr_ev"]:
                                t.wait_ge(p.sV.h, mk["vtr_ev"])
                            p.sP.inc(t.transpose(pstr[:],
                                                 ev[pi % 2][:, j * 128:(j + 1) * 128],
                                                 ident[:]))
                        p.TEN(t3_sec)
                        tr_n = p.sP.n

                        def v_sec(v, tb=tb, j=j, tr_n=tr_n):
                            v.wait_ge(p.sP.h, tr_n)
                            kt = tb * 4 + j
                            mk["vtr_ev"] = p.sV.inc(
                                v.tensor_copy(v_tm[:, kt * 128:(kt + 1) * 128], pstr[:]))
                        p.VEC(v_sec)
                    mk["ev_free"][pi % 2] = mk["vtr_ev"]
                    mk["ev_free_eng"][pi % 2] = "V"
                else:
                    def t2_sec(t, ev_n=ev_n, pi=pi):
                        t.wait_ge(p.sA.h, ev_n)
                        if mk["rope_dve"]:
                            t.wait_ge(p.sV.h, mk["rope_dve"])
                        p.sP.inc(t.matmul(pssw[:], pswap[:], ev[pi % 2][:],
                                          start=True, stop=True))
                    p.TEN(t2_sec)
                    sw_n = p.sP.n

                    is_q = nm in ("q1", "q2")
                    qi = 0 if nm == "q1" else 1

                    def v_sec(v, tb=tb, nm=nm, pi=pi, sw_n=sw_n, is_q=is_q, qi=qi):
                        v.wait_ge(p.sP.h, sw_n)
                        if is_q and mk["qob_st"][qi]:
                            v.wait_ge(p.sS.h, mk["qob_st"][qi])
                        cs = slice(tb * TB, (tb + 1) * TB)
                        v.tensor_tensor(rope_t1[:], ev[pi % 2][:], cbuf[:], ALU.mult)
                        v.tensor_tensor(rope_t2[:], pssw[:], sbuf_t[:], ALU.mult)
                        dst = qob[qi][:] if is_q else kk[nm][:, cs]
                        mk["rope_dve"] = p.sV.inc(
                            v.tensor_tensor(dst, rope_t1[:], rope_t2[:], ALU.add))
                    p.VEC(v_sec)
                    mk["ev_free"][pi % 2] = mk["rope_dve"]
                    mk["ev_free_eng"][pi % 2] = "V"

                    if is_q:
                        rn = p.sV.n

                        def sq_sec(s, g=g, tb=tb, nm=nm, qi=qi, rn=rn):
                            s.wait_ge(p.sV.h, rn)
                            mk["qob_st"][qi] = p.sS.inc(
                                s.dma_start(out=q_scr[nm][:, tb * TB:(tb + 1) * TB],
                                            in_=qob[qi][:]), 16)
                        p.SYN(sq_sec)
                        q_store[(g, nm, tb)] = p.sS.n
            mk["rope_tb"] = mk["rope_dve"]
            proj_tb_done[(g, tb)] = p.sP.n
        proj_g_done[g] = p.sP.n
        dve_fin = p.sV.n
        psp_ev_fin = mk["psp_free"][0]
        p.pop()
        p.push()
        pss = [p.ps(f"l0a_pss_g{g}_{i}", [128, TB]) for i in range(4)]

        for qw in range(NTB):
            def sw_sec(s, g=g, qw=qw):
                s.wait_ge(p.sS.h, q_store[(g, "q1", qw)])
                s.wait_ge(p.sS.h, q_store[(g, "q2", qw)])
                if mk["qw_used"]:
                    s.wait_ge(p.sP.h, mk["qw_used"])
                p.sS.inc(s.dma_start(out=qw_t["q1"][:],
                                     in_=q_scr["q1"][:, qw * TB:(qw + 1) * TB]), 16)
                p.sS.inc(s.dma_start(out=qw_t["q2"][:],
                                     in_=q_scr["q2"][:, qw * TB:(qw + 1) * TB]), 16)
            p.SYN(sw_sec)
            qw_n = p.sS.n

            nkt = 4 * qw + 4
            ndiag = 4
            noff = nkt - ndiag
            wpb = p.sP.n
            wvb = p.sV.n
            wab = p.sA.n
            patdve_pred = lambda kt: (wvb + kt + 1 if kt < noff
                                      else wvb + noff + 2 * (kt - noff) + 2)
            prev_patdve = mk["patdve"]
            prev_yst = mk["yst"]
            prev_yev = mk["yev"]

            def a_win(s, g=g, qw=qw, wpb=wpb, noff=noff, nkt=nkt,
                      prev_patdve=prev_patdve, prev_yst=prev_yst):
                for kt in range(noff):
                    pr = kt % 2
                    s.wait_ge(p.sP.h, wpb + 3 * kt + 1)
                    if kt < 2:
                        if prev_patdve:
                            s.wait_ge(p.sV.h, prev_patdve)
                    else:
                        s.wait_ge(p.sV.h, patdve_pred(kt - 2))
                    s.activation(s1m[2 * pr][:], pss[2 * pr][:], AF.Copy,
                                 scale=1.0 / 4096.0)
                    p.sA.inc(s.activation(s1m[2 * pr + 1][:], pss[2 * pr + 1][:],
                                          AF.Copy, scale=1.0 / 4096.0))
                s.wait_ge(p.sP.h, wpb + 3 * nkt)
                if prev_yst:
                    s.wait_ge(p.sS.h, prev_yst)
                s.activation(ybuf[0:64, :], psyA[0:64, :], AF.Copy)
                p.sA.inc(s.activation(ybuf[64:128, :], psyB[64:128, :], AF.Copy))
            p.SCA(a_win)
            mk["s1ev_a"] = wab + noff
            mk["yev"] = wab + noff + 1
            yev_n = mk["yev"]

            maskev_pred = lambda kt: patdve_pred(kt) - 1
            prev_av = mk["av"]

            def t_win(t, g=g, qw=qw, nkt=nkt, noff=noff, wpb=wpb, wab=wab,
                      dve_fin=dve_fin, qw_n=qw_n, prev_patdve=prev_patdve,
                      prev_yev=prev_yev, psp_ev_fin=psp_ev_fin):
                for kt in range(nkt):
                    pr = kt % 2
                    diag_r = kt - 4 * qw
                    ks = slice(kt * 128, (kt + 1) * 128)
                    if qw == 0 and kt == 0:
                        t.wait_ge(p.sV.h, dve_fin)
                        if psp_ev_fin:
                            t.wait_ge(p.sA.h, psp_ev_fin)
                    if kt == 0:
                        t.wait_ge(p.sS.h, qw_n)
                    if kt < 2:
                        if prev_patdve:
                            t.wait_ge(p.sV.h, prev_patdve)
                    else:
                        t.wait_ge(p.sV.h, patdve_pred(kt - 2))
                    t.matmul(pss[2 * pr][:], kk["k1"][0:64, ks], qw_t["q1"][0:64, :],
                             start=True, stop=True, tile_position=(0, 0))
                    mm = t.matmul(pss[2 * pr + 1][:], kk["k1"][64:128, ks],
                                  qw_t["q1"][64:128, :],
                                  start=True, stop=True, tile_position=(64, 0))
                    p.sP.inc(mm)
                    if diag_r < 0:
                        t.wait_ge(p.sA.h, wab + kt + 1)
                    else:
                        t.wait_ge(p.sV.h, maskev_pred(kt))
                    t.matmul(pss[2 * pr][:], kk["k2"][0:64, ks], qw_t["q2"][0:64, :],
                             start=True, stop=True, tile_position=(0, 0))
                    mm = t.matmul(pss[2 * pr + 1][:], kk["k2"][64:128, ks],
                                  qw_t["q2"][64:128, :],
                                  start=True, stop=True, tile_position=(64, 0))
                    mk["qw_used"] = p.sP.inc(mm)
                    t.wait_ge(p.sV.h, patdve_pred(kt))
                    if kt == 0 and prev_yev:
                        t.wait_ge(p.sA.h, prev_yev)
                    t.matmul(psyA[0:64, :], v_tm[:, kt * 128: kt * 128 + 64],
                             pat[2 * pr][:],
                             start=(kt == 0), stop=(kt == nkt - 1), tile_position=(0, 0),
                             skip_group_check=True)
                    mm = t.matmul(psyB[64:128, :],
                                  v_tm[:, kt * 128 + 64: kt * 128 + 128],
                                  pat[2 * pr + 1][:], start=(kt == 0),
                                  stop=(kt == nkt - 1),
                                  tile_position=(0, 64), skip_group_check=True)
                    mk["av"] = p.sP.inc(mm)
            p.TEN(t_win)

            def v_win(v, g=g, qw=qw, nkt=nkt, noff=noff, wpb=wpb,
                      prev_patdve=prev_patdve, prev_av=prev_av):
                for kt in range(nkt):
                    pr = kt % 2
                    diag_r = kt - 4 * qw
                    if diag_r >= 0:
                        v.wait_ge(p.sP.h, wpb + 3 * kt + 1)
                        if kt < 2 and prev_patdve:
                            v.wait_ge(p.sV.h, prev_patdve)
                        ms = masks[:, diag_r * TB:(diag_r + 1) * TB]
                        v.tensor_tensor(s1m[2 * pr][:], pss[2 * pr][:], ms, ALU.mult)
                        p.sV.inc(v.tensor_tensor(s1m[2 * pr + 1][:],
                                                 pss[2 * pr + 1][:], ms, ALU.mult))
                    v.wait_ge(p.sP.h, wpb + 3 * kt + 2)
                    if kt >= 2:
                        v.wait_ge(p.sP.h, wpb + 3 * (kt - 2) + 3)
                    elif prev_av > 0:
                        v.wait_ge(p.sP.h, prev_av)
                    v.tensor_tensor(pat[2 * pr][:], pss[2 * pr][:], s1m[2 * pr][:],
                                    ALU.mult)
                    mk["patdve"] = p.sV.inc(
                        v.tensor_tensor(pat[2 * pr + 1][:], pss[2 * pr + 1][:],
                                        s1m[2 * pr + 1][:], ALU.mult))
            p.VEC(v_win)
            y_n = p.sP.n

            def s2_sec(s, g=g, qw=qw, yev_n=yev_n):
                s.wait_ge(p.sA.h, yev_n)
                mk["yst"] = p.sS.inc(
                    s.dma_start(out=y_scr_g[g][:, qw * TB:(qw + 1) * TB], in_=ybuf[:]), 16)
            p.SYN(s2_sec)
        p.pop()

    p.pop()
    p.push()
    y_all = p.sS.n
    wout_bar = p.snap()
    wo_buf = [p.sb(f"l0a_wo{i}", [128, NCC * 128], BF16) for i in range(2)]
    y_t = [p.sb(f"l0a_yt{i}", [128, NCC * TB], BF16) for i in range(2)]
    psp2 = [p.ps(f"l0a_psp2_{i}", [128, TB]) for i in range(2)]
    wo_mm = {}
    res_add = {}
    wo_used = {}
    for oc in range(NCC):
        def g3_sec(gq, oc=oc):
            if oc == 0:
                p.wait_snap(gq, wout_bar)
            if oc >= 2:
                gq.wait_ge(p.sP.h, wo_used[oc - 2])
            p.sG.inc(gq.dma_start(out=wo_buf[oc % 2][:],
                                  in_=wview["o"][:, :, oc * 128:(oc + 1) * 128]), 16)
        p.GPS(g3_sec)
        won = p.sG.n
        for tb in range(NTB):
            idx = oc * NTB + tb

            def s_sec(s, oc=oc, tb=tb, idx=idx):
                if idx == 0:
                    p.wait_snap(s, wout_bar)
                    s.wait_ge(p.sS.h, y_all)
                if idx >= 2:
                    s.wait_ge(p.sP.h, wo_mm[idx - 2])
                p.sS.inc(s.dma_start(
                    out=y_t[tb % 2].rearrange("p (cc t) -> p cc t", cc=NCC),
                    in_=y_scr_c[:, :, tb * TB:(tb + 1) * TB]), 16)
            p.SYN(s_sec)
            yt_n = p.sS.n

            def t_sec(t, oc=oc, tb=tb, idx=idx, yt_n=yt_n, won=won):
                t.wait_ge(p.sS.h, yt_n)
                t.wait_ge(p.sG.h, won)
                if idx < 2:
                    p.wait_snap(t, wout_bar)
                else:
                    t.wait_ge(p.sV.h, res_add[idx - 2])
                for cc in range(NCC):
                    mm = t.matmul(psp2[idx % 2][:], wo_buf[oc % 2][:, cc * 128:(cc + 1) * 128],
                                  y_t[tb % 2][:, cc * TB:(cc + 1) * TB],
                                  start=(cc == 0), stop=(cc == NCC - 1))
                wo_mm[idx] = p.sP.inc(mm)
            p.TEN(t_sec)
            mm_n = p.sP.n

            def v_sec(v, oc=oc, tb=tb, idx=idx, mm_n=mm_n):
                v.wait_ge(p.sP.h, mm_n)
                xs = x_fm[:, oc * T + tb * TB: oc * T + tb * TB + TB]
                res_add[idx] = p.sV.inc(
                    v.tensor_tensor(xs, xs, psp2[idx % 2][:], ALU.add))
            p.VEC(v_sec)
        wo_used[oc] = p.sP.n
    p.pop()


def emit_l0_ffn(p, env):
    x_fm = env["x_fm"]
    xn_scr_v = env["xn_scr_v"]
    xn_ready = env["xn_ready"]

    wr_d = p.inp("l0fr", [C, C], F32).rearrange("(cc p) o -> p cc o", p=128)
    wk_d = p.inp("l0fk", [C, C], F32).rearrange("(cc p) o -> p cc o", p=128)
    wv_d = p.inp("l0fv", [C, C], F32).rearrange("(cc p) o -> p cc o", p=128)
    mur_d = p.inp("l0f_mur", [128, NCC], F32)
    muk_d = p.inp("l0f_muk", [128, NCC], F32)

    bar = p.snap()
    p.push()
    wr = p.sb("l0f_wr", [128, NCC * C], BF16)
    wk = p.sb("l0f_wk", [128, NCC * C], BF16)
    wvb = [p.sb(f"l0f_wv{i}", [128, NCC * 128], BF16) for i in range(2)]
    mur = p.sb("l0f_mur_t", [128, NCC], F32)
    muk = p.sb("l0f_muk_t", [128, NCC], F32)
    xbuf = p.sb("l0f_xbuf", [128, NCC * (TB + 1)], BF16)
    mix1 = p.sb("l0f_mix1", [128, NCC * TB], BF16)
    mix2 = p.sb("l0f_mix2", [128, NCC * TB], BF16)
    r_t = p.sb("l0f_rt", [128, NCC * TB], BF16)
    dbuf = [p.sb("l0f_d0", [128, TB], F32)]
    tbuf = [p.sb(f"l0f_t{i}", [128, TB], F32) for i in range(2)]
    krelu = p.sb("l0f_krelu", [128, TB], BF16)
    obuf = p.sb("l0f_obuf", [128, TB], F32)
    psm = [p.ps(f"l0f_ps{i}", [128, TB]) for i in range(2)]

    xb_v = xbuf.rearrange("p (cc j) -> p cc j", j=TB + 1)

    def w_sec(gq):
        p.wait_snap(gq, bar)
        p.sG.inc(gq.dma_start(out=wr.rearrange("p (cc o) -> p cc o", cc=NCC), in_=wr_d), 16)
        p.sG.inc(gq.dma_start(out=wk.rearrange("p (cc o) -> p cc o", cc=NCC), in_=wk_d), 16)
    p.GPS(w_sec)
    wres_n = p.sG.n

    def mu_sec(s):
        p.wait_snap(s, bar)
        p.sS.inc(s.dma_start(out=mur[:], in_=mur_d), 16)
        p.sS.inc(s.dma_start(out=muk[:], in_=muk_d), 16)
    p.SYN(mu_sec)
    mu_n = p.sS.n

    mk = {"ev": {}, "kv_done": {}, "wv_used": {}, "mix_done": {}, "resadd": {}}
    prev_tb_kv = 0
    prev_tb_mm = 0

    for tb in range(NTB):
        def s_sec(s, tb=tb, prev=prev_tb_kv):
            if tb == 0:
                p.wait_snap(s, bar)
                s.wait_ge(p.sS.h, mu_n)
            s.wait_ge(p.sS.h, xn_ready[tb])
            if prev:
                s.wait_ge(p.sV.h, prev)
            if tb == 0:
                p.sS.inc(s.dma_start(out=xb_v[:, :, 1:TB + 1],
                                     in_=xn_scr_v[:, :, 0:TB]), 16)
            else:
                p.sS.inc(s.dma_start(out=xb_v[:, :, 0:TB + 1],
                                     in_=xn_scr_v[:, :, tb * TB - 1: (tb + 1) * TB]), 16)
        p.SYN(s_sec)
        xb_n = p.sS.n

        if tb == 0:
            def z_sec(v):
                v.wait_ge(p.sS.h, xb_n)
                p.sV.inc(v.memset(xb_v[:, :, 0:1], 0.0))
            p.VEC(z_sec)
        zero_n = p.sV.n

        mix_marks = []
        amarks = {}
        for cc in range(NCC):
            def v_sec(v, tb=tb, cc=cc, xb_n=xb_n, zero_n=zero_n):
                if cc == 0:
                    v.wait_ge(p.sS.h, xb_n)
                    if tb == 0:
                        v.wait_ge(p.sV.h, zero_n)
                    if tb >= 1:
                        v.wait_ge(p.sP.h, prev_tb_mm)
                if (cc - 1) in amarks:
                    v.wait_ge(p.sA.h, amarks[cc - 1])
                cur = xb_v[:, cc, 1:TB + 1]
                prv = xb_v[:, cc, 0:TB]
                p.sV.inc(v.tensor_tensor(dbuf[0][:], cur, prv, ALU.subtract))
            p.VEC(v_sec)
            d_n = p.sV.n

            def a_sec(s, cc=cc, d_n=d_n):
                s.wait_ge(p.sV.h, d_n)
                s.activation(tbuf[0][:], dbuf[0][:], AF.Copy,
                             scale=mur[:, cc:cc + 1])
                amarks[cc] = p.sA.inc(s.activation(tbuf[1][:], dbuf[0][:], AF.Copy,
                                      scale=muk[:, cc:cc + 1]))
            p.SCA(a_sec)
            t_n = p.sA.n

            def v2_sec(v, cc=cc, t_n=t_n):
                v.wait_ge(p.sA.h, t_n)
                cs = slice(cc * TB, (cc + 1) * TB)
                prv = xb_v[:, cc, 0:TB]
                v.tensor_tensor(mix1[:, cs], tbuf[0][:], prv, ALU.add)
                p.sV.inc(v.tensor_tensor(mix2[:, cs], tbuf[1][:], prv, ALU.add))
            p.VEC(v2_sec)
            mix_marks.append(p.sV.n)
        mix_n = p.sV.n

        ev_mark = {}
        for oc in range(NCC):
            def t_sec(t, tb=tb, oc=oc, mix_n=mix_n):
                if oc == 0:
                    t.wait_ge(p.sV.h, mix_n)
                    if tb == 0:
                        t.wait_ge(p.sG.h, wres_n)
                if (oc - 2) in ev_mark:
                    t.wait_ge(p.sA.h, ev_mark[oc - 2])
                for cc in range(NCC):
                    mm = t.matmul(psm[oc % 2][:], wr[:, cc * C + oc * 128: cc * C + oc * 128 + 128],
                                  mix1[:, cc * TB:(cc + 1) * TB],
                                  start=(cc == 0), stop=(cc == NCC - 1))
                p.sP.inc(mm)
            p.TEN(t_sec)
            rmm_n = p.sP.n

            def a_sec(s, oc=oc, rmm_n=rmm_n):
                s.wait_ge(p.sP.h, rmm_n)
                ev_mark[oc] = p.sA.inc(
                    s.activation(r_t[:, oc * TB:(oc + 1) * TB], psm[oc % 2][:], AF.Sigmoid))
            p.SCA(a_sec)
        r_mm_all = p.sP.n

        k_ev = {}
        r_last_ev = max(ev_mark.values())
        for oc in range(NCC):
            def t_sec(t, oc=oc):
                if oc < 2:
                    t.wait_ge(p.sA.h, r_last_ev)
                if (oc - 2) in k_ev:
                    t.wait_ge(p.sA.h, k_ev[oc - 2])
                for cc in range(NCC):
                    mm = t.matmul(psm[oc % 2][:], wk[:, cc * C + oc * 128: cc * C + oc * 128 + 128],
                                  mix2[:, cc * TB:(cc + 1) * TB],
                                  start=(cc == 0), stop=(cc == NCC - 1))
                p.sP.inc(mm)
            p.TEN(t_sec)
            kmm_n = p.sP.n

            def a_sec(s, oc=oc, kmm_n=kmm_n):
                s.wait_ge(p.sP.h, kmm_n)
                k_ev[oc] = p.sA.inc(s.activation(krelu[:], psm[oc % 2][:], AF.Relu))
            p.SCA(a_sec)
            kr_n = p.sA.n

            def v_sec(v, oc=oc, kr_n=kr_n):
                v.wait_ge(p.sA.h, kr_n)
                v.wait_ge(p.sP.h, r_mm_all)
                p.sV.inc(v.tensor_tensor(mix1[:, oc * TB:(oc + 1) * TB],
                                         krelu[:], krelu[:], ALU.mult))
            p.VEC(v_sec)
        kact_n = p.sV.n

        kv_mm = {}
        for oc in range(NCC):
            def g_sec(gq, tb=tb, oc=oc):
                if (tb * NCC + oc) >= 2:
                    gq.wait_ge(p.sP.h, mk["wv_used"][(tb * NCC + oc) - 2])
                elif tb == 0 and oc == 0:
                    p.wait_snap(gq, bar)
                p.sG.inc(gq.dma_start(out=wvb[oc % 2].rearrange("p (cc o) -> p cc o", cc=NCC),
                                      in_=wv_d[:, :, oc * 128:(oc + 1) * 128]), 16)
            p.GPS(g_sec)
            wv_n = p.sG.n

            def t_sec(t, tb=tb, oc=oc, wv_n=wv_n, kact_n=kact_n):
                if oc == 0:
                    t.wait_ge(p.sV.h, kact_n)
                t.wait_ge(p.sG.h, wv_n)
                if (oc - 2) in kv_mm:
                    t.wait_ge(p.sV.h, mk["resadd"][(tb, oc - 2)])
                for cc in range(NCC):
                    mm = t.matmul(psm[oc % 2][:], wvb[oc % 2][:, cc * 128:(cc + 1) * 128],
                                  mix1[:, cc * TB:(cc + 1) * TB],
                                  start=(cc == 0), stop=(cc == NCC - 1))
                kv_mm[oc] = p.sP.inc(mm)
                mk["wv_used"][tb * NCC + oc] = kv_mm[oc]
            p.TEN(t_sec)
            kvm_n = p.sP.n

            def v_sec(v, tb=tb, oc=oc, kvm_n=kvm_n):
                v.wait_ge(p.sP.h, kvm_n)
                v.tensor_tensor(obuf[:], psm[oc % 2][:], r_t[:, oc * TB:(oc + 1) * TB],
                                ALU.mult)
                xs = x_fm[:, oc * T + tb * TB: oc * T + tb * TB + TB]
                mk["resadd"][(tb, oc)] = p.sV.inc(
                    v.tensor_tensor(xs, xs, obuf[:], ALU.add))
            p.VEC(v_sec)
        prev_tb_kv = p.sV.n
        prev_tb_mm = p.sP.n
    p.pop()


def emit_l1_attn(p, env):
    x_fm = env["x_fm"]
    xn_scr_v = env["xn_scr_v"]
    xn_ready = env["xn_ready"]
    ident = env["ident"]
    idsh = env["idsh"]
    ones_col = env["ones_col"]
    ones_row = env["ones_row"]

    wq_d = p.inp("l1q", [C, C], F32).rearrange("(cc p) o -> p cc o", p=128)
    wk_d = p.inp("l1k", [C, C], F32).rearrange("(cc p) o -> p cc o", p=128)
    wv_d = p.inp("l1v", [C, C], F32).rearrange("(cc p) o -> p cc o", p=128)
    wview = {"q": wq_d, "k": wk_d, "v": wv_d}

    bar = p.snap()
    p.push()
    qf = p.sb("l1a_qf", [128, T], BF16)
    kf = p.sb("l1a_kf", [128, T], BF16)
    v_tm = p.sb("l1a_vtm", [128, NKT * 130], BF16)
    v_tm_v = v_tm.rearrange("p (tt c) -> p tt c", c=130)
    wbuf = {nm: p.sb(f"l1a_w_{nm}", [128, NCC * 128], BF16) for nm in ("q", "k", "v")}
    xn_t = [p.sb(f"l1a_xnt{i}", [128, NCC * TB], BF16) for i in range(2)]
    ev = [p.sb(f"l1a_ev{i}", [128, TB], BF16) for i in range(2)]
    sqb = p.sb("l1a_sq", [128, T], BF16)
    nmrow = p.sb("l1a_nmrow", [1, 16], F32)
    smalls = p.sb("l1a_smalls", [1, 16], F32)
    neg_bf = p.sb("l1a_negbf", [1, 2], BF16)
    negc = [p.sb(f"l1a_negc{i}", [128, 1], F32) for i in range(2)]
    eA = [p.sb(f"l1a_eA{i}", [128, TB], BF16) for i in range(2)]
    eB = [p.sb(f"l1a_eB{i}", [128, TB], BF16) for i in range(2)]
    rec_f = p.sb("l1a_recf", [1, TB], F32)
    rec_bf = p.sb("l1a_recbf", [1, TB], BF16)
    rb_sb = p.sb("l1a_rb", [64, TB], F32)
    ybfA = p.sb("l1a_ybfA", [64, TB], BF16)
    ybfB = p.sb("l1a_ybfB", [64, TB], BF16)
    psp = p.ps("l1a_psp", [128, TB])
    pstr = p.ps("l1a_pstr", [128, 128], BF16)
    psA = p.ps("l1a_psA", [128, TB])
    psB = p.ps("l1a_psB", [128, TB])
    psYA = p.ps("l1a_psYA", [65, TB])
    psYB = p.ps("l1a_psYB", [65, TB])
    psN = p.ps("l1a_psN", [1, TB])
    psU = p.ps("l1a_psU", [128, TB])

    mk = {"psp_free": 0, "ev_free": [0, 0], "ev_eng": ["V", "V"], "vtr_ev": 0,
          "sA_ev": 0, "sB_ev": 0, "avA": 0, "avB": 0, "y_done": 0,
          "scores_done": 0, "psN_free": 0, "psC1_free": 0}

    proj_tb_done = {}
    proj_g_done = {}

    for g in range(NCC):
        def gw_sec(gq, g=g):
            if g == 0:
                p.wait_snap(gq, bar)
            if g >= 1:
                gq.wait_ge(p.sP.h, proj_g_done[g - 1])
            for nm in ("q", "k", "v"):
                p.sG.inc(gq.dma_start(out=wbuf[nm][:],
                                      in_=wview[nm][:, :, g * 128:(g + 1) * 128]), 16)
        p.GPS(gw_sec)
        wload_n = p.sG.n

        def ones_sec(v, g=g):
            if mk["scores_done"]:
                v.wait_ge(p.sP.h, mk["scores_done"])
            p.sV.inc(v.memset(v_tm_v[:, :, 64:130:65], 1.0))
        p.VEC(ones_sec)
        vones_n = p.sV.n

        for tb in range(NTB):
            def s_sec(s, g=g, tb=tb):
                if g == 0 and tb == 0:
                    p.wait_snap(s, bar)
                if g == 0:
                    s.wait_ge(p.sS.h, xn_ready[tb])
                key = (g, tb - 2) if tb >= 2 else (g - 1, NTB - 2 + tb)
                if key in proj_tb_done:
                    s.wait_ge(p.sP.h, proj_tb_done[key])
                p.sS.inc(s.dma_start(
                    out=xn_t[tb % 2].rearrange("p (cc t) -> p cc t", cc=NCC),
                    in_=xn_scr_v[:, :, tb * TB:(tb + 1) * TB]), 16)
            p.SYN(s_sec)
            xn_n = p.sS.n

            for pi, nm in enumerate(("q", "k", "v")):
                def t_sec(t, tb=tb, pi=pi, nm=nm, xn_n=xn_n, wload_n=wload_n):
                    if pi == 0:
                        t.wait_ge(p.sS.h, xn_n)
                        t.wait_ge(p.sG.h, wload_n)
                    if mk["psp_free"]:
                        t.wait_ge(p.sA.h, mk["psp_free"])
                    for cc in range(NCC):
                        mm = t.matmul(psp[:], wbuf[nm][:, cc * 128:(cc + 1) * 128],
                                      xn_t[tb % 2][:, cc * TB:(cc + 1) * TB],
                                      start=(cc == 0), stop=(cc == NCC - 1))
                    p.sP.inc(mm)
                p.TEN(t_sec)
                mm_n = p.sP.n

                if nm in ("q", "k"):
                    def a_sec(s, tb=tb, nm=nm, mm_n=mm_n):
                        s.wait_ge(p.sP.h, mm_n)
                        dst = (qf if nm == "q" else kf)[:, tb * TB:(tb + 1) * TB]
                        mk["psp_free"] = p.sA.inc(s.activation(dst, psp[:], AF.Copy))
                    p.SCA(a_sec)
                else:
                    def a_sec(s, pi=pi, mm_n=mm_n):
                        s.wait_ge(p.sP.h, mm_n)
                        eng, cnt = mk["ev_eng"][pi % 2], mk["ev_free"][pi % 2]
                        if cnt:
                            s.wait_ge((p.sV if eng == "V" else p.sP).h, cnt)
                        mk["psp_free"] = p.sA.inc(
                            s.activation(ev[pi % 2][:], psp[:], AF.Copy))
                    p.SCA(a_sec)
                    ev_n = p.sA.n
                    for j in range(4):
                        def t3_sec(t, j=j, ev_n=ev_n, pi=pi):
                            if j == 0:
                                t.wait_ge(p.sA.h, ev_n)
                            if mk["vtr_ev"]:
                                t.wait_ge(p.sV.h, mk["vtr_ev"])
                            p.sP.inc(t.transpose(pstr[:],
                                                 ev[pi % 2][:, j * 128:(j + 1) * 128],
                                                 ident[:]))
                        p.TEN(t3_sec)
                        tr_n = p.sP.n

                        def v_sec(v, tb=tb, j=j, tr_n=tr_n, vones_n=vones_n):
                            v.wait_ge(p.sP.h, tr_n)
                            if tb == 0 and j == 0:
                                v.wait_ge(p.sV.h, vones_n)
                            kt = tb * 4 + j
                            dst = v_tm_v[:, kt, :].rearrange(
                                "p (two r) -> p two r", r=65)[:, :, 0:64]
                            mk["vtr_ev"] = p.sV.inc(
                                v.tensor_copy(dst,
                                    pstr[:].rearrange("p (two r) -> p two r", two=2)))
                        p.VEC(v_sec)
                    mk["ev_free"][pi % 2] = mk["vtr_ev"]
                    mk["ev_eng"][pi % 2] = "V"
            proj_tb_done[(g, tb)] = p.sP.n
        proj_g_done[g] = p.sP.n

        def sq_sec(s, g=g):
            s.wait_ge(p.sA.h, mk["psp_free"])
            s.wait_ge(p.sP.h, proj_g_done[g])
            p.sA.inc(s.activation(sqb[:], qf[:], AF.Square))
        p.SCA(sq_sec)
        sqq_n = p.sA.n

        norm_res = {}
        for which, src_n in (("q", sqq_n), ("k", None)):
            if which == "k":
                def sq2_sec(s, g=g):
                    s.wait_ge(p.sV.h, norm_res[("q", 1)])
                    p.sA.inc(s.activation(sqb[:], kf[:], AF.Square))
                p.SCA(sq2_sec)
                src_n = p.sA.n
            for hi, base in ((0, 0), (1, 64)):
                for tb in range(NTB):
                    def tn_sec(t, hi=hi, base=base, tb=tb, src_n=src_n):
                        t.wait_ge(p.sA.h, src_n)
                        if mk["psN_free"]:
                            t.wait_ge(p.sV.h, mk["psN_free"])
                        p.sP.inc(t.matmul(psN[:], ones_col[base:base + 64, 0:1],
                                          sqb[base:base + 64, tb * TB:(tb + 1) * TB],
                                          start=True, stop=True,
                                          tile_position=(base, 0)))
                    p.TEN(tn_sec)
                    tn_n = p.sP.n

                    def vr_sec(v, which=which, hi=hi, tb=tb, tn_n=tn_n):
                        v.wait_ge(p.sP.h, tn_n)
                        mk["psN_free"] = p.sV.inc(
                            v.reduce_max(nmrow[:, tb:tb + 1], psN[:], axis=AX.X))
                    p.VEC(vr_sec)

                def vm_sec(v, which=which, hi=hi):
                    idx = {"q": 0, "k": 2}[which] + hi
                    norm_res[(which, hi)] = p.sV.inc(
                        v.reduce_max(smalls[:, idx:idx + 1], nmrow[:, 0:NTB], axis=AX.X))
                p.VEC(vm_sec)
        qkmax_n = p.sV.n

        def vc_sec(v, qkmax_n=qkmax_n):
            v.wait_ge(p.sV.h, qkmax_n)
            v.tensor_tensor(smalls[:, 4:5], smalls[:, 0:1], smalls[:, 2:3], ALU.mult)
            p.sV.inc(v.tensor_tensor(smalls[:, 5:6], smalls[:, 1:2], smalls[:, 3:4],
                                     ALU.mult))
        p.VEC(vc_sec)
        prod_n = p.sV.n

        def ac_sec(s, prod_n=prod_n):
            s.wait_ge(p.sV.h, prod_n)
            p.sA.inc(s.activation(smalls[:, 6:8], smalls[:, 4:6], AF.Sqrt))
        p.SCA(ac_sec)
        csr_n = p.sA.n

        def vc2_sec(v, csr_n=csr_n):
            v.wait_ge(p.sA.h, csr_n)
            v.tensor_scalar_mul(smalls[:, 8:10], smalls[:, 6:8], -0.125)
            p.sV.inc(v.tensor_copy(neg_bf[:], smalls[:, 8:10]))
        p.VEC(vc2_sec)
        nbf_n = p.sV.n

        def tc_sec(t, nbf_n=nbf_n):
            t.wait_ge(p.sV.h, nbf_n)
            if mk["psC1_free"]:
                t.wait_ge(p.sV.h, mk["psC1_free"])
            p.sP.inc(t.matmul(psU[:, 0:2], ones_row[:], neg_bf[:],
                              start=True, stop=True))
        p.TEN(tc_sec)
        tc_n = p.sP.n

        def vc3_sec(v, tc_n=tc_n):
            v.wait_ge(p.sP.h, tc_n)
            v.tensor_copy(negc[0][:], psU[:, 0:1])
            mk["psC1_free"] = p.sV.inc(v.tensor_copy(negc[1][:], psU[:, 1:2]))
        p.VEC(vc3_sec)
        negc_n = p.sV.n

        for qw in range(NTB):
            pbase = p.sP.n
            abase = p.sA.n

            def a_win(s, g=g, qw=qw, pbase=pbase):
                for kt in range(NKT):
                    s.wait_ge(p.sP.h, pbase + 2 * kt + 1)
                    s.activation(eA[kt % 2][:], psA[:], AF.Exp, scale=0.125,
                                 bias=negc[0][:])
                    op = s.activation(eB[kt % 2][:], psB[:], AF.Exp, scale=0.125,
                                      bias=negc[1][:])
                    p.sA.inc(op)
            p.SCA(a_win)
            mk["sA_ev"] = p.sA.n

            def t_win(t, g=g, qw=qw, negc_n=negc_n, pbase=pbase, abase=abase):
                for kt in range(NKT):
                    if qw == 0 and kt == 0:
                        t.wait_ge(p.sV.h, negc_n)
                    if kt >= 1:
                        t.wait_ge(p.sA.h, abase + kt)
                    qs = slice(qw * TB, (qw + 1) * TB)
                    ks = slice(kt * 128, (kt + 1) * 128)
                    t.matmul(psA[:], kf[0:64, ks], qf[0:64, qs],
                             start=True, stop=True, tile_position=(0, 0))
                    mm = t.matmul(psB[:], kf[64:128, ks], qf[64:128, qs],
                                  start=True, stop=True, tile_position=(64, 0))
                    p.sP.inc(mm)
                    t.wait_ge(p.sA.h, abase + kt + 1)
                    if kt == 0 and mk["y_done"]:
                        t.wait_ge(p.sV.h, mk["y_done"])
                    t.matmul(psYA[:], v_tm[:, kt * 130: kt * 130 + 65], eA[kt % 2][:],
                             start=(kt == 0), stop=(kt == NKT - 1), skip_group_check=True)
                    mm = t.matmul(psYB[:], v_tm[:, kt * 130 + 65: kt * 130 + 130],
                                  eB[kt % 2][:], start=(kt == 0), stop=(kt == NKT - 1),
                                  skip_group_check=True)
                    mk["avB"] = p.sP.inc(mm)
            p.TEN(t_win)
            y_n = p.sP.n

            for hi, psY, yb in ((0, psYA, ybfA), (1, psYB, ybfB)):
                def vr_sec(v, psY=psY, y_n=y_n):
                    v.wait_ge(p.sP.h, y_n)
                    v.reciprocal(rec_f[:], psY[64:65, :])
                    p.sV.inc(v.tensor_copy(rec_bf[:], rec_f[:]))
                p.VEC(vr_sec)
                rb_n = p.sV.n

                def tb_sec(t, rb_n=rb_n):
                    t.wait_ge(p.sV.h, rb_n)
                    if mk["psC1_free"]:
                        t.wait_ge(p.sV.h, mk["psC1_free"])
                    p.sP.inc(t.matmul(psU[0:64, :], ones_row[0:1, 0:64], rec_bf[:],
                                      start=True, stop=True))
                p.TEN(tb_sec)
                tb_n = p.sP.n

                def vy_sec(v, psY=psY, yb=yb, tb_n=tb_n):
                    v.wait_ge(p.sP.h, tb_n)
                    v.tensor_copy(rb_sb[:], psU[0:64, :])
                    mk["psC1_free"] = p.sV.n + 1
                    p.sV.inc(v.tensor_tensor(yb[:], psY[0:64, :], rb_sb[:], ALU.mult))
                p.VEC(vy_sec)
            mults_n = p.sV.n

            def tsh_sec(t, mults_n=mults_n):
                t.wait_ge(p.sV.h, mults_n)
                t.matmul(psU[:], ident[0:64, :], ybfA[:], start=True, stop=False)
                p.sP.inc(t.matmul(psU[:], idsh[0:64, :], ybfB[:],
                                  start=False, stop=True))
            p.TEN(tsh_sec)
            tsh_n = p.sP.n

            def vadd_sec(v, g=g, qw=qw, tsh_n=tsh_n):
                v.wait_ge(p.sP.h, tsh_n)
                xs = x_fm[:, g * T + qw * TB: g * T + qw * TB + TB]
                mk["psC1_free"] = p.sV.inc(v.tensor_tensor(xs, xs, psU[:], ALU.add))
                mk["y_done"] = mk["psC1_free"]
            p.VEC(vadd_sec)
        mk["scores_done"] = p.sP.n
    p.pop()


def emit_spill_x(p, env):
    import os as _os
    x_scr = p.nc.dram_tensor("x_scr", [C, T], F32, kind="Internal").ap()
    env["x_scr_v"] = x_scr.rearrange("(cc p) t -> cc p t", p=128)
    x_fm = env["x_fm"]
    bar = p.snap()
    dbgx = _os.environ.get("DBGX", "0") == "1"
    if dbgx:
        x3d = p.nc.dram_tensor("x3dbg", [C, T], F32, kind="ExternalOutput").ap()
        x3v = x3d.rearrange("(cc p) t -> cc p t", p=128)

    def sec(s):
        p.wait_snap(s, bar)
        for cc in range(NCC):
            p.sS.inc(s.dma_start(out=env["x_scr_v"][cc],
                                 in_=x_fm[:, cc * T:(cc + 1) * T]), 16)
            if dbgx:
                p.sO.inc(s.dma_start(out=x3v[cc],
                                     in_=x_fm[:, cc * T:(cc + 1) * T]), 16)
    p.SYN(sec)
    env["x_spilled"] = p.sS.n


def emit_l1_ffn(p, env):
    xn_scr_v = env["xn_scr_v"]
    xn_ready = env["xn_ready"]
    x_scr_v = env["x_scr_v"]
    xout = p.nc.dram_tensor("xout", [C, T], F32, kind="ExternalOutput").ap()
    xout_v = xout.rearrange("(cc p) t -> cc p t", p=128)

    F1, F2 = 4096, 4096
    NF1, NF2 = 32, 32

    k0_d = p.inp("l1f_k0t", [32, 128, 16 * 128], U8)
    k1_d = p.inp("l1f_k1t", [32, 128, 32 * 128], U8)
    k2_d = p.inp("l1f_k2t", [8, 128, 32 * 128], U8)
    s0_d = p.inp("l1f_s0", [128, 1], F32)
    s1_d = p.inp("l1f_s1", [128, 1], F32)
    s2_d = p.inp("l1f_s2", [128, 1], F32)

    bar = p.snap()
    import os as _os
    p.serial = frozenset(_os.environ.get("SERIAL3", "s"))
    fixraw = _os.environ.get("FIXRAW", "0") == "1"
    fixact = _os.environ.get("FIXACT", "0") == "1"
    dbg3 = _os.environ.get("DBG3", "0") == "1"
    if dbg3:
        d_wab = p.nc.dram_tensor("d_wab", [NTB * 32, 128, 2048], BF16, kind="ExternalOutput").ap()
        d_w0e = p.nc.dram_tensor("d_w0e", [NTB * 32, 128, C], BF16, kind="ExternalOutput").ap()
        d_xn = p.nc.dram_tensor("d_xn", [NTB, 128, NCC * TB], BF16, kind="ExternalOutput").ap()
        d_h1 = p.nc.dram_tensor("d_h1", [NTB, 128, 32 * TB], BF16, kind="ExternalOutput").ap()
        d_h2 = p.nc.dram_tensor("d_h2", [NTB, 128, 32 * TB], BF16, kind="ExternalOutput").ap()
    p.push()
    wprobe = p.sb("l1f_wprobe", [128, 8], BF16)
    aprobe = p.sb("l1f_aprobe", [128, 8], BF16)
    h1 = p.sb("l1f_h1", [128, NF1 * TB], BF16)
    h2 = p.sb("l1f_h2", [128, NF2 * TB], BF16)
    xn_t = p.sb("l1f_xnt", [128, NCC * TB], BF16)
    wab = [p.sb(f"l1f_wab{i}", [128, 2 * NCC * 128], U8) for i in range(2)]
    w0e = [p.sb(f"l1f_w0e{i}", [128, C], BF16) for i in range(2)]
    wu = [p.sb(f"l1f_wu{i}", [128, F1], U8) for i in range(2)]
    we = [p.sb(f"l1f_we{i}", [128, F1], BF16) for i in range(2)]
    sc_t = p.sb("l1f_sc", [128, 4], F32)
    xblk = [p.sb(f"l1f_xb{i}", [128, TB], F32) for i in range(2)]
    ybuf = [p.sb(f"l1f_yb{i}", [128, TB], F32) for i in range(2)]
    psm = [p.ps(f"l1f_ps{i}", [128, TB]) for i in range(2)]

    def sld_sec(s):
        p.wait_snap(s, bar)
        p.sS.inc(s.dma_start(out=sc_t[:, 0:1], in_=s0_d), 16)
        p.sS.inc(s.dma_start(out=sc_t[:, 1:2], in_=s1_d), 16)
        p.sS.inc(s.dma_start(out=sc_t[:, 2:3], in_=s2_d), 16)
    p.SYN(sld_sec)
    sld_n = p.sS.n

    def sa_sec(s):
        s.wait_ge(p.sS.h, sld_n)
        s.activation(sc_t[:, 0:1], sc_t[:, 0:1], AF.Copy, scale=4.0)
        p.sA.inc(s.activation(sc_t[:, 1:2], sc_t[:, 1:2], AF.Copy, scale=2.0))
    p.SCA(sa_sec)
    scales_n = p.sA.n

    mk = {"conv": {}, "mm": {}, "ev": {}, "l1mm_tb": {}, "l2mm_tb": {}, "l3mm_tb": {},
          "stt": {}, "yst": {}, "xb_used": {}, "wconv": {}, "wmm": {}}
    gidx = 0
    wj = 0

    for tb in range(NTB):
        ts0 = tb * TB

        def s_sec(s, tb=tb):
            if tb == 0:
                p.wait_snap(s, bar)
                s.wait_ge(p.sS.h, env["x_spilled"])
            s.wait_ge(p.sS.h, xn_ready[tb])
            if tb >= 1:
                s.wait_ge(p.sP.h, mk["l1mm_tb"][tb - 1])
            p.sS.inc(s.dma_start(
                out=xn_t.rearrange("p (cc t) -> p cc t", cc=NCC),
                in_=xn_scr_v[:, :, ts0:ts0 + TB]), 16)
        p.SYN(s_sec)
        xn_n = p.sS.n

        if dbg3:
            def dxn_sec(s, tb=tb, xn_n=xn_n):
                s.wait_ge(p.sG.h, xn_n)
                p.sO.inc(s.dma_start(out=d_xn[tb], in_=xn_t[:]), 16)
            p.SYN(dxn_sec)

        l1_pb = p.sP.n
        l1_ab = p.sA.n
        for fc in range(NF1):
            ci = tb * NF1 + fc

            def g_sec(g, tb=tb, fc=fc, ci=ci):
                if tb == 0 and fc == 0:
                    p.wait_snap(g, bar)
                if (ci - 2) in mk["conv"]:
                    g.wait_ge(p.sV.h, mk["conv"][ci - 2])
                p.sS.inc(g.dma_start(out=wab[fc % 2][:], in_=k0_d[fc]), 16)
            p.SYN(g_sec)
            gl_n = p.sS.n

            def v_sec(v, fc=fc, ci=ci, gl_n=gl_n):
                v.wait_ge(p.sS.h, gl_n)
                if (ci - 2) in mk["mm"]:
                    v.wait_ge(p.sP.h, mk["mm"][ci - 2])
                tt = v.tensor_tensor(w0e[fc % 2][:], wab[fc % 2][:, 0:NCC * 128],
                                     wab[fc % 2][:, NCC * 128:], ALU.subtract)
                if fixraw:
                    tt = v.tensor_copy(wprobe[:], w0e[fc % 2][:, 0:8])
                mk["conv"][ci] = p.sV.inc(tt)
            p.VEC(v_sec)
            cv_n = p.sV.n

            if dbg3:
                def dw_sec(s, tb=tb, fc=fc, ci=ci, gl_n=gl_n, cv_n=cv_n):
                    s.wait_ge(p.sG.h, gl_n)
                    s.wait_ge(p.sV.h, cv_n)
                    p.sO.inc(s.dma_start(out=d_wab[ci], in_=wab[fc % 2][:]), 16)
                    p.sO.inc(s.dma_start(out=d_w0e[ci], in_=w0e[fc % 2][:]), 16)
                    mk["dw"] = p.sO.n
                p.SYN(dw_sec)

            def t_sec(t, tb=tb, fc=fc, ci=ci, cv_n=cv_n, xn_n=xn_n, gi=gidx,
                      l1_ab=l1_ab):
                t.wait_ge(p.sV.h, cv_n)
                if fc == 0:
                    t.wait_ge(p.sS.h, xn_n)
                if fc >= 2:
                    t.wait_ge(p.sA.h, l1_ab + fc - 1)
                elif (gi - 2) in mk["ev"]:
                    eng, cnt = mk["ev"][gi - 2]
                    t.wait_ge((p.sA if eng == "A" else p.sV).h, cnt)
                for cc in range(NCC):
                    mm = t.matmul(psm[gi % 2][:],
                                  w0e[fc % 2][:, cc * 128:(cc + 1) * 128],
                                  xn_t[:, cc * TB:(cc + 1) * TB],
                                  start=(cc == 0), stop=(cc == NCC - 1))
                mk["mm"][ci] = p.sP.inc(mm)
            p.TEN(t_sec)
            gidx += 1

        gi0 = gidx - NF1

        def a_l1(s, tb=tb, l1_pb=l1_pb, gi0=gi0, scales_n=scales_n):
            for fc in range(NF1):
                s.wait_ge(p.sP.h, l1_pb + fc + 1)
                if tb == 0 and fc == 0:
                    s.wait_ge(p.sA.h, scales_n)
                p.sA.inc(s.activation(h1[:, fc * TB:(fc + 1) * TB],
                                      psm[(gi0 + fc) % 2][:],
                                      AF.Relu, scale=sc_t[:, 0:1]))
        p.SCA(a_l1)
        for fc in range(NF1):
            mk["ev"][gi0 + fc] = ("A", l1_ab + fc + 1)
        l1ev_n = p.sA.n
        mk["l1mm_tb"][tb] = p.sP.n

        if dbg3:
            def dh1_sec(s, tb=tb, l1ev_n=l1ev_n):
                s.wait_ge(p.sA.h, l1ev_n)
                p.sO.inc(s.dma_start(out=d_h1[tb], in_=h1[:]), 16)
            p.SYN(dh1_sec)

        l2_pb = p.sP.n
        l2_ab = p.sA.n
        for fc in range(NF2):
            ci = 1000 + tb * NF2 + fc

            def g_sec(g, fc=fc, wj=wj):
                if (wj - 2) in mk["wconv"]:
                    g.wait_ge(p.sV.h, mk["wconv"][wj - 2])
                p.sS.inc(g.dma_start(out=wu[wj % 2][:], in_=k1_d[fc]), 16)
            p.SYN(g_sec)
            gl_n = p.sS.n

            def v_sec(v, wj=wj, gl_n=gl_n):
                v.wait_ge(p.sS.h, gl_n)
                if (wj - 2) in mk["wmm"]:
                    v.wait_ge(p.sP.h, mk["wmm"][wj - 2])
                tt = v.tensor_scalar(we[wj % 2][:], wu[wj % 2][:], 2.0, -1.0,
                                     ALU.mult, ALU.add)
                if fixraw:
                    tt = v.tensor_copy(wprobe[:], we[wj % 2][:, 0:8])
                mk["wconv"][wj] = p.sV.inc(tt)
            p.VEC(v_sec)
            cv_n = p.sV.n

            def t_sec(t, fc=fc, wj=wj, cv_n=cv_n, gi=gidx, l1ev_n=l1ev_n,
                      l2_ab=l2_ab):
                t.wait_ge(p.sV.h, cv_n)
                if fc == 0:
                    t.wait_ge(p.sA.h, l1ev_n)
                if fc >= 2:
                    t.wait_ge(p.sA.h, l2_ab + fc - 1)
                elif (gi - 2) in mk["ev"]:
                    eng, cnt = mk["ev"][gi - 2]
                    t.wait_ge((p.sA if eng == "A" else p.sV).h, cnt)
                for cc in range(NF1):
                    mm = t.matmul(psm[gi % 2][:],
                                  we[wj % 2][:, cc * 128:(cc + 1) * 128],
                                  h1[:, cc * TB:(cc + 1) * TB],
                                  start=(cc == 0), stop=(cc == NF1 - 1))
                mk["wmm"][wj] = p.sP.inc(mm)
            p.TEN(t_sec)
            gidx += 1
            wj += 1

        gi0 = gidx - NF2

        def a_l2(s, tb=tb, l2_pb=l2_pb, gi0=gi0):
            for fc in range(NF2):
                s.wait_ge(p.sP.h, l2_pb + fc + 1)
                p.sA.inc(s.activation(h2[:, fc * TB:(fc + 1) * TB],
                                      psm[(gi0 + fc) % 2][:],
                                      AF.Relu, scale=sc_t[:, 1:2]))
        p.SCA(a_l2)
        for fc in range(NF2):
            mk["ev"][gi0 + fc] = ("A", l2_ab + fc + 1)
        l2ev_n = p.sA.n
        mk["l2mm_tb"][tb] = p.sP.n

        if dbg3:
            def dh2_sec(s, tb=tb, l2ev_n=l2ev_n):
                s.wait_ge(p.sA.h, l2ev_n)
                p.sO.inc(s.dma_start(out=d_h2[tb], in_=h2[:]), 16)
            p.SYN(dh2_sec)

        for oc in range(NCC):
            ci = 2000 + tb * NCC + oc

            def g_sec(g, oc=oc, wj=wj):
                if (wj - 2) in mk["wconv"]:
                    g.wait_ge(p.sV.h, mk["wconv"][wj - 2])
                p.sS.inc(g.dma_start(out=wu[wj % 2][:], in_=k2_d[oc]), 16)
            p.SYN(g_sec)
            gl_n = p.sS.n

            def sx_sec(s, tb=tb, oc=oc, ci=ci):
                if (ci - 2) in mk["xb_used"]:
                    s.wait_ge(p.sV.h, mk["xb_used"][ci - 2])
                p.sS.inc(s.dma_start(out=xblk[oc % 2][:],
                                     in_=x_scr_v[oc][:, tb * TB:(tb + 1) * TB]), 16)
            p.SYN(sx_sec)
            xb_n = p.sS.n

            def v_sec(v, wj=wj, gl_n=gl_n):
                v.wait_ge(p.sS.h, gl_n)
                if (wj - 2) in mk["wmm"]:
                    v.wait_ge(p.sP.h, mk["wmm"][wj - 2])
                tt = v.tensor_scalar(we[wj % 2][:], wu[wj % 2][:], 2.0, -1.0,
                                     ALU.mult, ALU.add)
                if fixraw:
                    tt = v.tensor_copy(wprobe[:], we[wj % 2][:, 0:8])
                mk["wconv"][wj] = p.sV.inc(tt)
            p.VEC(v_sec)
            cv_n = p.sV.n

            def t_sec(t, oc=oc, wj=wj, cv_n=cv_n, gi=gidx, l2ev_n=l2ev_n):
                t.wait_ge(p.sV.h, cv_n)
                if oc == 0:
                    t.wait_ge(p.sA.h, l2ev_n)
                if (gi - 2) in mk["ev"]:
                    eng, cnt = mk["ev"][gi - 2]
                    t.wait_ge((p.sA if eng == "A" else p.sV).h, cnt)
                for cc in range(NF2):
                    mm = t.matmul(psm[gi % 2][:],
                                  we[wj % 2][:, cc * 128:(cc + 1) * 128],
                                  h2[:, cc * TB:(cc + 1) * TB],
                                  start=(cc == 0), stop=(cc == NF2 - 1))
                mk["wmm"][wj] = p.sP.inc(mm)
            p.TEN(t_sec)
            mm_n = p.sP.n

            def vy_sec(v, tb=tb, oc=oc, ci=ci, mm_n=mm_n, xb_n=xb_n, gi=gidx):
                v.wait_ge(p.sP.h, mm_n)
                v.wait_ge(p.sS.h, xb_n)
                if (ci - 2) in mk["yst"]:
                    v.wait_ge(p.sO.h, mk["yst"][ci - 2])
                out = p.sV.inc(
                    v.scalar_tensor_tensor(ybuf[oc % 2][:], psm[gi % 2][:],
                                           sc_t[:, 2:3], xblk[oc % 2][:],
                                           ALU.mult, ALU.add))
                mk["xb_used"][ci] = out
                mk["stt"][ci] = out
                mk["ev"][gi] = ("V", out)
            p.VEC(vy_sec)
            stt_n = p.sV.n

            def sy_sec(s, tb=tb, oc=oc, ci=ci, stt_n=stt_n):
                s.wait_ge(p.sV.h, stt_n)
                mk["yst"][ci] = p.sO.inc(
                    s.dma_start(out=xout_v[oc][:, tb * TB:(tb + 1) * TB],
                                in_=ybuf[oc % 2][:]), 16)
            p.SYN(sy_sec)
            gidx += 1
            wj += 1
    p.serial = frozenset()
    p.pop()


def build_program(blocks):
    nc = bass.Bass("TRN2", target_bir_lowering=False)
    st = ExitStack()
    with st:
        p = Prog(nc, st)
        env = {}

        ident = p.sb("ident", [128, 128], BF16)
        pswap = p.sb("pswap", [128, 128], BF16)
        ones_col = p.sb("ones_col", [128, 1], BF16)
        ones_row = p.sb("ones_row", [1, 128], BF16)
        ones_row_f = p.sb("ones_row_f", [1, 128], F32)
        env.update(ident=ident, pswap=pswap, ones_col=ones_col, ones_row=ones_row)

        id_d = p.inp("ident_in", [128, 128], BF16)
        ps_d = p.inp("pswap_in", [128, 128], BF16)
        idsh = p.sb("idsh", [128, 128], BF16)
        env["idsh"] = idsh
        ish_d = p.inp("idsh_in", [128, 128], BF16)

        def const_sec(gq):
            p.sG.inc(gq.dma_start(out=ident[:], in_=id_d), 16)
            p.sG.inc(gq.dma_start(out=pswap[:], in_=ps_d), 16)
            p.sG.inc(gq.dma_start(out=idsh[:], in_=ish_d), 16)
        p.GPS(const_sec)

        eps_t = p.sb("eps_t", [1, 1], F32)
        env["eps_t"] = eps_t

        def ones_sec(v):
            v.memset(ones_col[:], 1.0)
            v.memset(eps_t[:], EPS)
            v.memset(ones_row_f[:], 1.0)
            p.sV.inc(v.memset(ones_row[:], 1.0))
        p.VEC(ones_sec)
        env["ones_row_f"] = ones_row_f

        xn_scr = nc.dram_tensor("xn_scr", [C, T], BF16, kind="Internal").ap()
        env["xn_scr_v"] = xn_scr.rearrange("(cc p) t -> p cc t", p=128)

        p.push()
        x_fm = p.sb("x_fm", [128, NCC * T], F32)
        env["x_fm"] = x_fm
        emit_load_x(p, env)

        if 0 in blocks:
            emit_ln(p, env, "0")
            emit_l0_attn(p, env)

        if 1 in blocks:
            emit_ln(p, env, "1")
            emit_l0_ffn(p, env)

        if 2 in blocks:
            emit_ln(p, env, "2")
            emit_l1_attn(p, env)

        if 3 in blocks:
            emit_ln(p, env, "3")
            emit_spill_x(p, env)
            p.pop()
            emit_l1_ffn(p, env)
        else:
            emit_store_x(p, env)
            p.pop()
        emit_final_wait(p)

    return nc, sorted(p.used_inputs)


def host_constants():
    bf = ml_dtypes.bfloat16
    ident = np.eye(128, dtype=np.float32)
    pswap = np.zeros((128, 128), np.float32)
    for j in range(128):
        pswap[j ^ 32, j] = 1.0
    inv_freq = (1.0 / (10000.0 ** (np.arange(0, HD, 2, dtype=np.float32) / HD))).astype(np.float32)
    tpos = np.arange(T, dtype=np.float32)
    freqs = tpos[:, None] * inv_freq[None, :]
    cos_t = np.cos(freqs).astype(bf).astype(np.float32)
    sin_t = np.sin(freqs).astype(bf).astype(np.float32)
    Ctab = np.zeros((128, T), np.float32)
    Stab = np.zeros((128, T), np.float32)
    for pr in range(128):
        i = pr % 32
        Ctab[pr] = cos_t[:, i]
        Stab[pr] = sin_t[:, i] * (1.0 if (pr % 64) < 32 else -1.0)
    cmask = np.zeros((128, 4 * TB), np.float32)
    qf = np.arange(TB)
    for r in range(4):
        for kp in range(128):
            cmask[kp, r * TB:(r + 1) * TB] = np.where(qf >= r * 128 + kp, 1.0 / 4096.0, 0.0)
    idsh = np.zeros((128, 128), np.float32)
    for i in range(64):
        idsh[i, i + 64] = 1.0
    return {"ident_in": ident.astype(bf), "pswap_in": pswap.astype(bf),
            "idsh_in": idsh.astype(bf),
            "rope_c": Ctab.astype(bf), "rope_s": Stab.astype(bf),
            "cmask": cmask.astype(bf)}


def host_prep(inputs):
    f32 = np.float32
    shared = dict(host_constants())
    tr = lambda a: np.ascontiguousarray(np.asarray(a, dtype=f32).T)
    shared["l0q1"] = tr(inputs["L0_attn_Wq1"])
    shared["l0k1"] = tr(inputs["L0_attn_Wk1"])
    shared["l0q2"] = tr(inputs["L0_attn_Wq2"])
    shared["l0k2"] = tr(inputs["L0_attn_Wk2"])
    shared["l0v"] = tr(inputs["L0_attn_Wv"])
    shared["l0o"] = tr(inputs["L0_attn_Wout"])
    shared["l0fr"] = tr(inputs["L0_ffn_W_r"])
    shared["l0fk"] = tr(inputs["L0_ffn_W_k"])
    shared["l0fv"] = tr(inputs["L0_ffn_W_v"])
    shared["l0f_mur"] = np.ascontiguousarray(
        np.asarray(inputs["L0_ffn_mu_r"], f32).reshape(NCC, 128).T)
    shared["l0f_muk"] = np.ascontiguousarray(
        np.asarray(inputs["L0_ffn_mu_k"], f32).reshape(NCC, 128).T)
    cat = lambda w: np.ascontiguousarray(
        np.asarray(w, f32).transpose(1, 0, 2).reshape(C, C))
    shared["l1q"] = cat(inputs["L1_attn_W_Q"])
    shared["l1k"] = cat(inputs["L1_attn_W_K"])
    shared["l1v"] = cat(inputs["L1_attn_W_V"])
    k0 = np.ascontiguousarray(np.asarray(inputs["L1_ffn_k0"])).view(np.uint8)
    k1_ = np.ascontiguousarray(np.asarray(inputs["L1_ffn_k1"])).view(np.uint8)
    k2_ = np.ascontiguousarray(np.asarray(inputs["L1_ffn_k2"])).view(np.uint8)

    def wtile(k, ng, nf):
        return np.ascontiguousarray(
            k.reshape(ng, 128, nf, 128).transpose(2, 1, 0, 3).reshape(nf, 128, ng * 128))
    shared["l1f_k0t"] = wtile(k0, 16, 32)
    shared["l1f_k1t"] = wtile(k1_, 32, 32)
    shared["l1f_k2t"] = wtile(k2_, 32, 8)
    for i in (0, 1, 2):
        shared[f"l1f_s{i}"] = np.ascontiguousarray(np.broadcast_to(
            np.asarray(inputs[f"L1_ffn_s{i}"], f32).reshape(1, 1), (128, 1)))
    return shared


LAUNCH_GROUPS = [[0, 1, 2, 3]]


def kernel(**inputs):
    x = np.asarray(inputs["L0_attn_x"], dtype=np.float32)
    shared = host_prep(inputs)
    state = [np.ascontiguousarray(x[b].T) for b in range(8)]

    for grp in LAUNCH_GROUPS:
        nc, needed = build_program(grp)
        in_maps = []
        for b in range(8):
            m = {}
            for k in needed:
                m[k] = state[b] if k == "x" else shared[k]
            in_maps.append(m)
        res = run_bass_kernel_spmd(nc, in_maps, core_ids=list(range(8)))
        state = [np.asarray(res.results[b]["xout"]) for b in range(8)]

    out = np.stack([s.T for s in state]).astype(np.float32)
    return np.ascontiguousarray(out)


# version 14
# speedup vs baseline: 1.0024x; 1.0024x over previous
import numpy as np
import ml_dtypes
from contextlib import ExitStack

import concourse.bass as bass
import concourse.mybir as mybir
from concourse.bass_utils import run_bass_kernel_spmd

F32 = mybir.dt.float32
BF16 = mybir.dt.bfloat16
U8 = mybir.dt.uint8
AF = mybir.ActivationFunctionType
ALU = mybir.AluOpType
AX = mybir.AxisListType

C = 1024
T = 4096
NCC = 8
NTB = 8
TB = 512
NKT = 32
NH = 16
HD = 64
EPS = 1e-5


class Sem:
    __slots__ = ("h", "n")

    def __init__(self, h):
        self.h = h
        self.n = 0

    def inc(self, instr, k=1):
        instr.then_inc(self.h, k)
        self.n += k
        return self.n


class Prog:
    def __init__(self, nc, st):
        self.nc = nc
        self.stacks = [st]
        self.blk = st.enter_context(nc.Block())
        self.used_inputs = set()
        for name in ("sS", "sG", "sQ", "sP", "sV", "sA", "sO"):
            setattr(self, name, Sem(st.enter_context(nc.semaphore(name))))

    serial = frozenset()

    def _wrap(self, f, kind):
        if kind not in self.serial:
            return f
        snapshot = self.snap()

        def g(eng):
            self.wait_snap(eng, snapshot)
            f(eng)
        return g

    def TEN(self, f): self.blk.tensor(self._wrap(f, "t"))
    def VEC(self, f): self.blk.vector(self._wrap(f, "v"))
    def SCA(self, f): self.blk.scalar(self._wrap(f, "a"))
    def SYN(self, f): self.blk.sync(self._wrap(f, "s"))
    def GPS(self, f): self.blk.gpsimd(self._wrap(f, "g"))

    def inp(self, name, shape, dtype):
        self.used_inputs.add(name)
        return self.nc.dram_tensor(name, shape, dtype, kind="ExternalInput").ap()

    def sb(self, name, shape, dtype):
        return self.stacks[-1].enter_context(self.nc.sbuf_tensor(name, shape, dtype))

    def ps(self, name, shape, dtype=F32):
        return self.stacks[-1].enter_context(self.nc.psum_tensor(name, shape, dtype))

    def push(self):
        st = ExitStack()
        self.stacks.append(st)
        return st

    def pop(self):
        st = self.stacks.pop()
        st.close()

    def snap(self):
        return {k: getattr(self, k).n
                for k in ("sS", "sG", "sQ", "sP", "sV", "sA")}

    def wait_snap(self, eng, snapshot):
        for k, v in snapshot.items():
            if v > 0:
                eng.wait_ge(getattr(self, k).h, v)


def emit_load_x(p, env):
    x = p.inp("x", [C, T], F32)
    xv = x.rearrange("(cc p) t -> cc p t", p=128)
    x_fm = env["x_fm"]

    def sec(s):
        for cc in range(NCC):
            p.sS.inc(s.dma_start(out=x_fm[:, cc * T:(cc + 1) * T], in_=xv[cc]), 16)
    p.SYN(sec)


def emit_store_x(p, env):
    xout = p.nc.dram_tensor("xout", [C, T], F32, kind="ExternalOutput").ap()
    xov = xout.rearrange("(cc p) t -> cc p t", p=128)
    x_fm = env["x_fm"]
    bar = p.snap()

    def sec(s):
        p.wait_snap(s, bar)
        for cc in range(NCC):
            p.sO.inc(s.dma_start(out=xov[cc], in_=x_fm[:, cc * T:(cc + 1) * T]), 16)
    p.SYN(sec)


def emit_final_wait(p):
    n = p.sO.n

    def sec(s):
        s.wait_ge(p.sO.h, n)
    p.SYN(sec)


def emit_ln(p, env, tag):
    x_fm = env["x_fm"]
    ones_col = env["ones_col"]
    ones_row = env["ones_row"]
    xn_scr_v = env["xn_scr_v"]
    bar = p.snap()

    p.push()
    xbf = [p.sb(f"ln{tag}_xbf{i}", [128, TB], BF16) for i in range(3)]
    sq = [p.sb(f"ln{tag}_sq{i}", [128, TB], BF16) for i in range(3)]
    rows = p.sb(f"ln{tag}_rows", [1, TB * 6], F32)
    mean_f, m2_f, var_f, sd_f, rstd_f, lo_f = (rows[:, i * TB:(i + 1) * TB] for i in range(6))
    mean_bf = p.sb(f"ln{tag}_meanbf", [1, TB], BF16)
    rstd_bf = p.sb(f"ln{tag}_rstdbf", [1, TB], BF16)
    mlo_bf = p.sb(f"ln{tag}_mlobf", [1, TB], BF16)
    rlo_bf = p.sb(f"ln{tag}_rlobf", [1, TB], BF16)
    xn_t = [p.sb(f"ln{tag}_xnt{i}", [128, NCC * TB], BF16) for i in range(2)]
    tsub = p.sb(f"ln{tag}_tsub", [128, TB], F32)
    psS = p.ps(f"ln{tag}_psS", [1, TB])
    psQ = p.ps(f"ln{tag}_psQ", [1, TB])
    psM = p.ps(f"ln{tag}_psM", [128, TB])
    psR = p.ps(f"ln{tag}_psR", [128, TB])

    stat_mark = {}
    mean_done = {}
    var_done = {}
    rows_done = {}
    bcast_done = {}
    norm_done = {}
    store_done = {}

    pb0 = p.sP.n
    pred_stat = lambda j: pb0 + (j // NCC) * 9 + (j % NCC) + 1
    cast_marks = {}

    for tb in range(NTB):
        def a_cast(s, tb=tb):
            for cc in range(NCC):
                idx = tb * NCC + cc
                if idx == 0:
                    p.wait_snap(s, bar)
                if idx >= 3:
                    s.wait_ge(p.sP.h, pred_stat(idx - 3))
                srcv = x_fm[:, cc * T + tb * TB: cc * T + tb * TB + TB]
                b = idx % 3
                s.activation(xbf[b][:], srcv, AF.Copy)
                cast_marks[idx] = p.sA.inc(s.activation(sq[b][:], srcv, AF.Square))
        p.SCA(a_cast)

        for cc in range(NCC):
            idx = tb * NCC + cc

            def t_sec(t, tb=tb, cc=cc, idx=idx):
                t.wait_ge(p.sA.h, cast_marks[idx])
                if cc == 0 and tb > 0:
                    t.wait_ge(p.sA.h, mean_done[tb - 1])
                    t.wait_ge(p.sV.h, var_done[tb - 1])
                b = idx % 3
                t.matmul(psS[:], ones_col[:], xbf[b][:],
                         start=(cc == 0), stop=(cc == NCC - 1),
                         skip_group_check=True)
                mm = t.matmul(psQ[:], ones_col[:], sq[b][:],
                              start=(cc == 0), stop=(cc == NCC - 1),
                              skip_group_check=True)
                stat_mark[idx] = p.sP.inc(mm)
            p.TEN(t_sec)

        sn = stat_mark[tb * NCC + NCC - 1]

        def a2_sec(s, tb=tb, sn=sn):
            s.wait_ge(p.sP.h, sn)
            mean_done[tb] = p.sA.inc(s.activation(mean_f, psS[:], AF.Copy, scale=1.0 / C))
        p.SCA(a2_sec)
        a2n = p.sA.n

        def v_sec(v, tb=tb, a2n=a2n, sn=sn):
            v.wait_ge(p.sA.h, a2n)
            v.wait_ge(p.sP.h, sn)
            v.tensor_tensor(m2_f, mean_f, mean_f, ALU.mult)
            var_done[tb] = p.sV.inc(
                v.scalar_tensor_tensor(var_f, psQ[:], 1.0 / C, m2_f,
                                       ALU.mult, ALU.subtract))
        p.VEC(v_sec)
        vn = p.sV.n

        def a3_sec(s, tb=tb, vn=vn):
            s.wait_ge(p.sV.h, vn)
            p.sA.inc(s.activation(sd_f, var_f, AF.Sqrt, bias=env["eps_t"][:]))
        p.SCA(a3_sec)
        a3n = p.sA.n

        def v2_sec(v, tb=tb, a3n=a3n):
            v.wait_ge(p.sA.h, a3n)
            v.reciprocal(rstd_f, sd_f)
            v.tensor_copy(mean_bf[:], mean_f)
            v.tensor_copy(rstd_bf[:], rstd_f)
            v.tensor_tensor(lo_f, mean_f, mean_bf[:], ALU.subtract)
            v.tensor_copy(mlo_bf[:], lo_f)
            v.tensor_tensor(lo_f, rstd_f, rstd_bf[:], ALU.subtract)
            rows_done[tb] = p.sV.inc(v.tensor_copy(rlo_bf[:], lo_f))
        p.VEC(v2_sec)

        def t2_sec(t, tb=tb):
            t.wait_ge(p.sV.h, rows_done[tb])
            if tb > 0:
                t.wait_ge(p.sV.h, norm_done[(tb - 1, NCC - 1)])
            t.matmul(psM[:], ones_row[:], mean_bf[:], start=True, stop=False,
                     skip_group_check=True)
            t.matmul(psM[:], ones_row[:], mlo_bf[:], start=False, stop=True,
                     skip_group_check=True)
            t.matmul(psR[:], ones_row[:], rstd_bf[:], start=True, stop=False,
                     skip_group_check=True)
            bcast_done[tb] = p.sP.inc(
                t.matmul(psR[:], ones_row[:], rlo_bf[:], start=False, stop=True,
                         skip_group_check=True))
        p.TEN(t2_sec)

        for cc in range(NCC):
            def v3_sec(v, tb=tb, cc=cc):
                if cc == 0:
                    v.wait_ge(p.sP.h, bcast_done[tb])
                    if tb >= 2:
                        v.wait_ge(p.sS.h, store_done[tb - 2])
                src = x_fm[:, cc * T + tb * TB: cc * T + tb * TB + TB]
                v.tensor_tensor(tsub[:], src, psM[:], ALU.subtract)
                norm_done[(tb, cc)] = p.sV.inc(
                    v.tensor_tensor(xn_t[tb % 2][:, cc * TB:(cc + 1) * TB],
                                    tsub[:], psR[:], ALU.mult))
            p.VEC(v3_sec)

        def s_sec(s, tb=tb):
            s.wait_ge(p.sV.h, norm_done[(tb, NCC - 1)])
            store_done[tb] = p.sS.inc(
                s.dma_start(out=env["xn_scr_v"][:, :, tb * TB:(tb + 1) * TB],
                            in_=xn_t[tb % 2].rearrange("p (cc t) -> p cc t", cc=NCC)), 16)
        p.SYN(s_sec)

    env["xn_ready"] = dict(store_done)
    p.pop()


def emit_l0_attn(p, env):
    nc = p.nc
    x_fm = env["x_fm"]
    xn_scr_v = env["xn_scr_v"]
    ident, pswap = env["ident"], env["pswap"]
    xn_ready = env["xn_ready"]
    m_d = p.inp("cmask", [128, 4 * TB], BF16)

    wts = {nm: p.inp(f"l0{nm}", [C, C], F32) for nm in ("q1", "k1", "q2", "k2", "v", "o")}
    wview = {nm: a.rearrange("(cc p) o -> p cc o", p=128) for nm, a in wts.items()}
    ctab_d = p.inp("rope_c", [128, T], BF16)
    stab_d = p.inp("rope_s", [128, T], BF16)

    y_scr = nc.dram_tensor("y_scr", [C, T], BF16, kind="Internal").ap()
    y_scr_g = y_scr.rearrange("(g p) t -> g p t", p=128)
    y_scr_c = y_scr.rearrange("(cc p) t -> p cc t", p=128)
    q_scr = {nm: nc.dram_tensor(f"q_scr_{nm}", [128, T], BF16, kind="Internal").ap()
             for nm in ("q1", "q2")}

    names = ("q1", "k1", "q2", "k2", "v")

    p.push()
    masks = p.sb("l0a_masks", [128, 4 * TB], BF16)
    kk = {nm: p.sb(f"l0a_{nm}", [128, T], BF16) for nm in ("k1", "k2")}
    v_tm = p.sb("l0a_vtm", [128, NKT * 128], BF16)
    wbuf = {nm: p.sb(f"l0a_w_{nm}", [128, NCC * 128], BF16) for nm in names}
    xn_t = [p.sb(f"l0a_xnt{i}", [128, NCC * TB], BF16) for i in range(2)]
    cbuf = p.sb("l0a_cbuf", [128, TB], BF16)
    sbuf_t = p.sb("l0a_sbuf", [128, TB], BF16)
    ev = [p.sb(f"l0a_ev{i}", [128, TB], BF16) for i in range(2)]
    rope_t1 = p.sb("l0a_rt1", [128, TB], BF16)
    rope_t2 = p.sb("l0a_rt2", [128, TB], BF16)
    qob = [p.sb(f"l0a_qob{i}", [128, TB], BF16) for i in range(2)]
    qw_t = {nm: p.sb(f"l0a_qw_{nm}", [128, TB], BF16) for nm in ("q1", "q2")}
    s1m = [p.sb(f"l0a_s1m{i}", [128, TB], BF16) for i in range(4)]
    pat = [p.sb(f"l0a_pat{i}", [128, TB], BF16) for i in range(4)]
    ybuf = p.sb("l0a_ybuf", [128, TB], BF16)
    psyA = p.ps("l0a_psyA", [128, TB])
    psyB = p.ps("l0a_psyB", [128, TB])

    mk = {
        "ev_free": [0, 0],
        "ev_free_eng": ["V", "V"],
        "psp_free": [0, 0],
        "rope_dve": 0,
        "rope_tb": 0,
        "vtr_ev": 0,
        "qob_st": [0, 0],
        "s1ev_a": 0, "s1ev_v": 0,
        "patdve": 0,
        "av": 0,
        "yev": 0,
        "yst": 0,
        "qw_used": 0,
    }
    bar = p.snap()

    proj_tb_done = {}
    proj_g_done = {}
    q_store = {}

    for g in range(NCC):
        p.push()
        psp = [p.ps(f"l0a_psp_g{g}", [128, TB])]
        pssw = p.ps(f"l0a_pssw_g{g}", [128, TB])
        pstr = p.ps(f"l0a_pstr_g{g}", [128, 128], BF16)
        prev_pat_g = mk["patdve"]
        prev_yev_g = mk["yev"]

        def gw_sec(gq, g=g):
            if g == 0:
                p.wait_snap(gq, bar)
                p.sG.inc(gq.dma_start(out=masks[:], in_=m_d), 16)
            if g >= 1:
                gq.wait_ge(p.sP.h, proj_g_done[g - 1])
            for nm in names:
                p.sG.inc(gq.dma_start(out=wbuf[nm][:],
                                      in_=wview[nm][:, :, g * 128:(g + 1) * 128]), 16)
        p.GPS(gw_sec)
        wload_n = p.sG.n

        for tb in range(NTB):
            def s_sec(s, g=g, tb=tb):
                if g == 0 and tb == 0:
                    p.wait_snap(s, bar)
                if g == 0:
                    s.wait_ge(p.sS.h, xn_ready[tb])
                key = (g, tb - 2) if tb >= 2 else (g - 1, NTB - 2 + tb)
                if key in proj_tb_done:
                    s.wait_ge(p.sP.h, proj_tb_done[key])
                p.sS.inc(s.dma_start(
                    out=xn_t[tb % 2].rearrange("p (cc t) -> p cc t", cc=NCC),
                    in_=xn_scr_v[:, :, tb * TB:(tb + 1) * TB]), 16)
                if mk["rope_tb"]:
                    s.wait_ge(p.sV.h, mk["rope_tb"])
                p.sS.inc(s.dma_start(out=cbuf[:], in_=ctab_d[:, tb * TB:(tb + 1) * TB]), 16)
                p.sS.inc(s.dma_start(out=sbuf_t[:], in_=stab_d[:, tb * TB:(tb + 1) * TB]), 16)
            p.SYN(s_sec)
            xn_n = p.sS.n

            for pi, nm in enumerate(names):
                def t_sec(t, g=g, tb=tb, pi=pi, nm=nm, xn_n=xn_n, wload_n=wload_n,
                          prev_pat_g=prev_pat_g, prev_yev_g=prev_yev_g):
                    if pi == 0:
                        t.wait_ge(p.sS.h, xn_n)
                        t.wait_ge(p.sG.h, wload_n)
                    if g > 0 and tb == 0 and pi == 0:
                        if prev_pat_g:
                            t.wait_ge(p.sV.h, prev_pat_g)
                        if prev_yev_g:
                            t.wait_ge(p.sA.h, prev_yev_g)
                    if mk["psp_free"][0]:
                        t.wait_ge(p.sA.h, mk["psp_free"][0])
                    for cc in range(NCC):
                        mm = t.matmul(psp[0][:], wbuf[nm][:, cc * 128:(cc + 1) * 128],
                                      xn_t[tb % 2][:, cc * TB:(cc + 1) * TB],
                                      start=(cc == 0), stop=(cc == NCC - 1))
                    p.sP.inc(mm)
                p.TEN(t_sec)
                mm_n = p.sP.n

                def a_sec(s, pi=pi, mm_n=mm_n):
                    s.wait_ge(p.sP.h, mm_n)
                    eng, cnt = mk["ev_free_eng"][pi % 2], mk["ev_free"][pi % 2]
                    if cnt:
                        s.wait_ge((p.sV if eng == "V" else p.sP).h, cnt)
                    mk["psp_free"][0] = p.sA.inc(
                        s.activation(ev[pi % 2][:], psp[0][:], AF.Copy))
                p.SCA(a_sec)
                ev_n = p.sA.n

                if nm == "v":
                    for j in range(4):
                        def t3_sec(t, j=j, ev_n=ev_n, pi=pi):
                            if j == 0:
                                t.wait_ge(p.sA.h, ev_n)
                            if mk["vtr_ev"]:
                                t.wait_ge(p.sV.h, mk["vtr_ev"])
                            p.sP.inc(t.transpose(pstr[:],
                                                 ev[pi % 2][:, j * 128:(j + 1) * 128],
                                                 ident[:]))
                        p.TEN(t3_sec)
                        tr_n = p.sP.n

                        def v_sec(v, tb=tb, j=j, tr_n=tr_n):
                            v.wait_ge(p.sP.h, tr_n)
                            kt = tb * 4 + j
                            mk["vtr_ev"] = p.sV.inc(
                                v.tensor_copy(v_tm[:, kt * 128:(kt + 1) * 128], pstr[:]))
                        p.VEC(v_sec)
                    mk["ev_free"][pi % 2] = mk["vtr_ev"]
                    mk["ev_free_eng"][pi % 2] = "V"
                else:
                    def t2_sec(t, ev_n=ev_n, pi=pi):
                        t.wait_ge(p.sA.h, ev_n)
                        if mk["rope_dve"]:
                            t.wait_ge(p.sV.h, mk["rope_dve"])
                        p.sP.inc(t.matmul(pssw[:], pswap[:], ev[pi % 2][:],
                                          start=True, stop=True))
                    p.TEN(t2_sec)
                    sw_n = p.sP.n

                    is_q = nm in ("q1", "q2")
                    qi = 0 if nm == "q1" else 1

                    def v_sec(v, tb=tb, nm=nm, pi=pi, sw_n=sw_n, is_q=is_q, qi=qi):
                        v.wait_ge(p.sP.h, sw_n)
                        if is_q and mk["qob_st"][qi]:
                            v.wait_ge(p.sS.h, mk["qob_st"][qi])
                        cs = slice(tb * TB, (tb + 1) * TB)
                        v.tensor_tensor(rope_t1[:], ev[pi % 2][:], cbuf[:], ALU.mult)
                        v.tensor_tensor(rope_t2[:], pssw[:], sbuf_t[:], ALU.mult)
                        dst = qob[qi][:] if is_q else kk[nm][:, cs]
                        mk["rope_dve"] = p.sV.inc(
                            v.tensor_tensor(dst, rope_t1[:], rope_t2[:], ALU.add))
                    p.VEC(v_sec)
                    mk["ev_free"][pi % 2] = mk["rope_dve"]
                    mk["ev_free_eng"][pi % 2] = "V"

                    if is_q:
                        rn = p.sV.n

                        def sq_sec(s, g=g, tb=tb, nm=nm, qi=qi, rn=rn):
                            s.wait_ge(p.sV.h, rn)
                            mk["qob_st"][qi] = p.sS.inc(
                                s.dma_start(out=q_scr[nm][:, tb * TB:(tb + 1) * TB],
                                            in_=qob[qi][:]), 16)
                        p.SYN(sq_sec)
                        q_store[(g, nm, tb)] = p.sS.n
            mk["rope_tb"] = mk["rope_dve"]
            proj_tb_done[(g, tb)] = p.sP.n
        proj_g_done[g] = p.sP.n
        dve_fin = p.sV.n
        psp_ev_fin = mk["psp_free"][0]
        p.pop()
        p.push()
        pss = [p.ps(f"l0a_pss_g{g}_{i}", [128, TB]) for i in range(4)]

        for qw in range(NTB):
            def sw_sec(s, g=g, qw=qw):
                s.wait_ge(p.sS.h, q_store[(g, "q1", qw)])
                s.wait_ge(p.sS.h, q_store[(g, "q2", qw)])
                if mk["qw_used"]:
                    s.wait_ge(p.sP.h, mk["qw_used"])
                p.sS.inc(s.dma_start(out=qw_t["q1"][:],
                                     in_=q_scr["q1"][:, qw * TB:(qw + 1) * TB]), 16)
                p.sS.inc(s.dma_start(out=qw_t["q2"][:],
                                     in_=q_scr["q2"][:, qw * TB:(qw + 1) * TB]), 16)
            p.SYN(sw_sec)
            qw_n = p.sS.n

            nkt = 4 * qw + 4
            ndiag = 4
            noff = nkt - ndiag
            wpb = p.sP.n
            wvb = p.sV.n
            wab = p.sA.n
            patdve_pred = lambda kt: (wvb + kt + 1 if kt < noff
                                      else wvb + noff + 2 * (kt - noff) + 2)
            prev_patdve = mk["patdve"]
            prev_yst = mk["yst"]
            prev_yev = mk["yev"]

            def a_win(s, g=g, qw=qw, wpb=wpb, noff=noff, nkt=nkt,
                      prev_patdve=prev_patdve, prev_yst=prev_yst):
                for kt in range(noff):
                    pr = kt % 2
                    s.wait_ge(p.sP.h, wpb + 3 * kt + 1)
                    if kt < 2:
                        if prev_patdve:
                            s.wait_ge(p.sV.h, prev_patdve)
                    else:
                        s.wait_ge(p.sV.h, patdve_pred(kt - 2))
                    s.activation(s1m[2 * pr][:], pss[2 * pr][:], AF.Copy,
                                 scale=1.0 / 4096.0)
                    p.sA.inc(s.activation(s1m[2 * pr + 1][:], pss[2 * pr + 1][:],
                                          AF.Copy, scale=1.0 / 4096.0))
                s.wait_ge(p.sP.h, wpb + 3 * nkt)
                if prev_yst:
                    s.wait_ge(p.sS.h, prev_yst)
                s.activation(ybuf[0:64, :], psyA[0:64, :], AF.Copy)
                p.sA.inc(s.activation(ybuf[64:128, :], psyB[64:128, :], AF.Copy))
            p.SCA(a_win)
            mk["s1ev_a"] = wab + noff
            mk["yev"] = wab + noff + 1
            yev_n = mk["yev"]

            maskev_pred = lambda kt: patdve_pred(kt) - 1
            prev_av = mk["av"]

            def t_win(t, g=g, qw=qw, nkt=nkt, noff=noff, wpb=wpb, wab=wab,
                      dve_fin=dve_fin, qw_n=qw_n, prev_patdve=prev_patdve,
                      prev_yev=prev_yev, psp_ev_fin=psp_ev_fin):
                for kt in range(nkt):
                    pr = kt % 2
                    diag_r = kt - 4 * qw
                    ks = slice(kt * 128, (kt + 1) * 128)
                    if qw == 0 and kt == 0:
                        t.wait_ge(p.sV.h, dve_fin)
                        if psp_ev_fin:
                            t.wait_ge(p.sA.h, psp_ev_fin)
                    if kt == 0:
                        t.wait_ge(p.sS.h, qw_n)
                    if kt < 2:
                        if prev_patdve:
                            t.wait_ge(p.sV.h, prev_patdve)
                    else:
                        t.wait_ge(p.sV.h, patdve_pred(kt - 2))
                    t.matmul(pss[2 * pr][:], kk["k1"][0:64, ks], qw_t["q1"][0:64, :],
                             start=True, stop=True, tile_position=(0, 0))
                    mm = t.matmul(pss[2 * pr + 1][:], kk["k1"][64:128, ks],
                                  qw_t["q1"][64:128, :],
                                  start=True, stop=True, tile_position=(64, 0))
                    p.sP.inc(mm)
                    if diag_r < 0:
                        t.wait_ge(p.sA.h, wab + kt + 1)
                    else:
                        t.wait_ge(p.sV.h, maskev_pred(kt))
                    t.matmul(pss[2 * pr][:], kk["k2"][0:64, ks], qw_t["q2"][0:64, :],
                             start=True, stop=True, tile_position=(0, 0))
                    mm = t.matmul(pss[2 * pr + 1][:], kk["k2"][64:128, ks],
                                  qw_t["q2"][64:128, :],
                                  start=True, stop=True, tile_position=(64, 0))
                    mk["qw_used"] = p.sP.inc(mm)
                    t.wait_ge(p.sV.h, patdve_pred(kt))
                    if kt == 0 and prev_yev:
                        t.wait_ge(p.sA.h, prev_yev)
                    t.matmul(psyA[0:64, :], v_tm[:, kt * 128: kt * 128 + 64],
                             pat[2 * pr][:],
                             start=(kt == 0), stop=(kt == nkt - 1), tile_position=(0, 0),
                             skip_group_check=True)
                    mm = t.matmul(psyB[64:128, :],
                                  v_tm[:, kt * 128 + 64: kt * 128 + 128],
                                  pat[2 * pr + 1][:], start=(kt == 0),
                                  stop=(kt == nkt - 1),
                                  tile_position=(0, 64), skip_group_check=True)
                    mk["av"] = p.sP.inc(mm)
            p.TEN(t_win)

            def v_win(v, g=g, qw=qw, nkt=nkt, noff=noff, wpb=wpb,
                      prev_patdve=prev_patdve, prev_av=prev_av):
                for kt in range(nkt):
                    pr = kt % 2
                    diag_r = kt - 4 * qw
                    if diag_r >= 0:
                        v.wait_ge(p.sP.h, wpb + 3 * kt + 1)
                        if kt < 2 and prev_patdve:
                            v.wait_ge(p.sV.h, prev_patdve)
                        ms = masks[:, diag_r * TB:(diag_r + 1) * TB]
                        v.tensor_tensor(s1m[2 * pr][:], pss[2 * pr][:], ms, ALU.mult)
                        p.sV.inc(v.tensor_tensor(s1m[2 * pr + 1][:],
                                                 pss[2 * pr + 1][:], ms, ALU.mult))
                    v.wait_ge(p.sP.h, wpb + 3 * kt + 2)
                    if kt >= 2:
                        v.wait_ge(p.sP.h, wpb + 3 * (kt - 2) + 3)
                    elif prev_av > 0:
                        v.wait_ge(p.sP.h, prev_av)
                    v.tensor_tensor(pat[2 * pr][:], pss[2 * pr][:], s1m[2 * pr][:],
                                    ALU.mult)
                    mk["patdve"] = p.sV.inc(
                        v.tensor_tensor(pat[2 * pr + 1][:], pss[2 * pr + 1][:],
                                        s1m[2 * pr + 1][:], ALU.mult))
            p.VEC(v_win)
            y_n = p.sP.n

            def s2_sec(s, g=g, qw=qw, yev_n=yev_n):
                s.wait_ge(p.sA.h, yev_n)
                mk["yst"] = p.sS.inc(
                    s.dma_start(out=y_scr_g[g][:, qw * TB:(qw + 1) * TB], in_=ybuf[:]), 16)
            p.SYN(s2_sec)
        p.pop()

    p.pop()
    p.push()
    y_all = p.sS.n
    wout_bar = p.snap()
    wo_buf = [p.sb(f"l0a_wo{i}", [128, NCC * 128], BF16) for i in range(2)]
    y_t = [p.sb(f"l0a_yt{i}", [128, NCC * TB], BF16) for i in range(2)]
    psp2 = [p.ps(f"l0a_psp2_{i}", [128, TB]) for i in range(2)]
    wo_mm = {}
    res_add = {}
    wo_used = {}
    for oc in range(NCC):
        def g3_sec(gq, oc=oc):
            if oc == 0:
                p.wait_snap(gq, wout_bar)
            if oc >= 2:
                gq.wait_ge(p.sP.h, wo_used[oc - 2])
            p.sG.inc(gq.dma_start(out=wo_buf[oc % 2][:],
                                  in_=wview["o"][:, :, oc * 128:(oc + 1) * 128]), 16)
        p.GPS(g3_sec)
        won = p.sG.n
        for tb in range(NTB):
            idx = oc * NTB + tb

            def s_sec(s, oc=oc, tb=tb, idx=idx):
                if idx == 0:
                    p.wait_snap(s, wout_bar)
                    s.wait_ge(p.sS.h, y_all)
                if idx >= 2:
                    s.wait_ge(p.sP.h, wo_mm[idx - 2])
                p.sS.inc(s.dma_start(
                    out=y_t[tb % 2].rearrange("p (cc t) -> p cc t", cc=NCC),
                    in_=y_scr_c[:, :, tb * TB:(tb + 1) * TB]), 16)
            p.SYN(s_sec)
            yt_n = p.sS.n

            def t_sec(t, oc=oc, tb=tb, idx=idx, yt_n=yt_n, won=won):
                t.wait_ge(p.sS.h, yt_n)
                t.wait_ge(p.sG.h, won)
                if idx < 2:
                    p.wait_snap(t, wout_bar)
                else:
                    t.wait_ge(p.sV.h, res_add[idx - 2])
                for cc in range(NCC):
                    mm = t.matmul(psp2[idx % 2][:], wo_buf[oc % 2][:, cc * 128:(cc + 1) * 128],
                                  y_t[tb % 2][:, cc * TB:(cc + 1) * TB],
                                  start=(cc == 0), stop=(cc == NCC - 1))
                wo_mm[idx] = p.sP.inc(mm)
            p.TEN(t_sec)
            mm_n = p.sP.n

            def v_sec(v, oc=oc, tb=tb, idx=idx, mm_n=mm_n):
                v.wait_ge(p.sP.h, mm_n)
                xs = x_fm[:, oc * T + tb * TB: oc * T + tb * TB + TB]
                res_add[idx] = p.sV.inc(
                    v.tensor_tensor(xs, xs, psp2[idx % 2][:], ALU.add))
            p.VEC(v_sec)
        wo_used[oc] = p.sP.n
    p.pop()


def emit_l0_ffn(p, env):
    x_fm = env["x_fm"]
    xn_scr_v = env["xn_scr_v"]
    xn_ready = env["xn_ready"]

    wr_d = p.inp("l0fr", [C, C], F32).rearrange("(cc p) o -> p cc o", p=128)
    wk_d = p.inp("l0fk", [C, C], F32).rearrange("(cc p) o -> p cc o", p=128)
    wv_d = p.inp("l0fv", [C, C], F32).rearrange("(cc p) o -> p cc o", p=128)
    mur_d = p.inp("l0f_mur", [128, NCC], F32)
    muk_d = p.inp("l0f_muk", [128, NCC], F32)

    bar = p.snap()
    p.push()
    wr = p.sb("l0f_wr", [128, NCC * C], BF16)
    wk = p.sb("l0f_wk", [128, NCC * C], BF16)
    wvb = [p.sb(f"l0f_wv{i}", [128, NCC * 128], BF16) for i in range(2)]
    mur = p.sb("l0f_mur_t", [128, NCC], F32)
    muk = p.sb("l0f_muk_t", [128, NCC], F32)
    xbuf = p.sb("l0f_xbuf", [128, NCC * (TB + 1)], BF16)
    mix1 = p.sb("l0f_mix1", [128, NCC * TB], BF16)
    mix2 = p.sb("l0f_mix2", [128, NCC * TB], BF16)
    r_t = p.sb("l0f_rt", [128, NCC * TB], BF16)
    dbuf = [p.sb("l0f_d0", [128, TB], F32)]
    tbuf = [p.sb(f"l0f_t{i}", [128, TB], F32) for i in range(2)]
    krelu = p.sb("l0f_krelu", [128, TB], BF16)
    obuf = p.sb("l0f_obuf", [128, TB], F32)
    psm = [p.ps(f"l0f_ps{i}", [128, TB]) for i in range(2)]

    xb_v = xbuf.rearrange("p (cc j) -> p cc j", j=TB + 1)

    def w_sec(gq):
        p.wait_snap(gq, bar)
        p.sG.inc(gq.dma_start(out=wr.rearrange("p (cc o) -> p cc o", cc=NCC), in_=wr_d), 16)
        p.sG.inc(gq.dma_start(out=wk.rearrange("p (cc o) -> p cc o", cc=NCC), in_=wk_d), 16)
    p.GPS(w_sec)
    wres_n = p.sG.n

    def mu_sec(s):
        p.wait_snap(s, bar)
        p.sS.inc(s.dma_start(out=mur[:], in_=mur_d), 16)
        p.sS.inc(s.dma_start(out=muk[:], in_=muk_d), 16)
    p.SYN(mu_sec)
    mu_n = p.sS.n

    mk = {"ev": {}, "kv_done": {}, "wv_used": {}, "mix_done": {}, "resadd": {}}
    prev_tb_kv = 0
    prev_tb_mm = 0

    for tb in range(NTB):
        def s_sec(s, tb=tb, prev=prev_tb_kv):
            if tb == 0:
                p.wait_snap(s, bar)
                s.wait_ge(p.sS.h, mu_n)
            s.wait_ge(p.sS.h, xn_ready[tb])
            if prev:
                s.wait_ge(p.sV.h, prev)
            if tb == 0:
                p.sS.inc(s.dma_start(out=xb_v[:, :, 1:TB + 1],
                                     in_=xn_scr_v[:, :, 0:TB]), 16)
            else:
                p.sS.inc(s.dma_start(out=xb_v[:, :, 0:TB + 1],
                                     in_=xn_scr_v[:, :, tb * TB - 1: (tb + 1) * TB]), 16)
        p.SYN(s_sec)
        xb_n = p.sS.n

        if tb == 0:
            def z_sec(v):
                v.wait_ge(p.sS.h, xb_n)
                p.sV.inc(v.memset(xb_v[:, :, 0:1], 0.0))
            p.VEC(z_sec)
        zero_n = p.sV.n

        mix_marks = []
        amarks = {}
        for cc in range(NCC):
            def v_sec(v, tb=tb, cc=cc, xb_n=xb_n, zero_n=zero_n):
                if cc == 0:
                    v.wait_ge(p.sS.h, xb_n)
                    if tb == 0:
                        v.wait_ge(p.sV.h, zero_n)
                    if tb >= 1:
                        v.wait_ge(p.sP.h, prev_tb_mm)
                if (cc - 1) in amarks:
                    v.wait_ge(p.sA.h, amarks[cc - 1])
                cur = xb_v[:, cc, 1:TB + 1]
                prv = xb_v[:, cc, 0:TB]
                p.sV.inc(v.tensor_tensor(dbuf[0][:], cur, prv, ALU.subtract))
            p.VEC(v_sec)
            d_n = p.sV.n

            def a_sec(s, cc=cc, d_n=d_n):
                s.wait_ge(p.sV.h, d_n)
                s.activation(tbuf[0][:], dbuf[0][:], AF.Copy,
                             scale=mur[:, cc:cc + 1])
                amarks[cc] = p.sA.inc(s.activation(tbuf[1][:], dbuf[0][:], AF.Copy,
                                      scale=muk[:, cc:cc + 1]))
            p.SCA(a_sec)
            t_n = p.sA.n

            def v2_sec(v, cc=cc, t_n=t_n):
                v.wait_ge(p.sA.h, t_n)
                cs = slice(cc * TB, (cc + 1) * TB)
                prv = xb_v[:, cc, 0:TB]
                v.tensor_tensor(mix1[:, cs], tbuf[0][:], prv, ALU.add)
                p.sV.inc(v.tensor_tensor(mix2[:, cs], tbuf[1][:], prv, ALU.add))
            p.VEC(v2_sec)
            mix_marks.append(p.sV.n)
        mix_n = p.sV.n

        ev_mark = {}
        for oc in range(NCC):
            def t_sec(t, tb=tb, oc=oc, mix_n=mix_n):
                if oc == 0:
                    t.wait_ge(p.sV.h, mix_n)
                    if tb == 0:
                        t.wait_ge(p.sG.h, wres_n)
                if (oc - 2) in ev_mark:
                    t.wait_ge(p.sA.h, ev_mark[oc - 2])
                for cc in range(NCC):
                    mm = t.matmul(psm[oc % 2][:], wr[:, cc * C + oc * 128: cc * C + oc * 128 + 128],
                                  mix1[:, cc * TB:(cc + 1) * TB],
                                  start=(cc == 0), stop=(cc == NCC - 1))
                p.sP.inc(mm)
            p.TEN(t_sec)
            rmm_n = p.sP.n

            def a_sec(s, oc=oc, rmm_n=rmm_n):
                s.wait_ge(p.sP.h, rmm_n)
                ev_mark[oc] = p.sA.inc(
                    s.activation(r_t[:, oc * TB:(oc + 1) * TB], psm[oc % 2][:], AF.Sigmoid))
            p.SCA(a_sec)
        r_mm_all = p.sP.n

        k_ev = {}
        r_last_ev = max(ev_mark.values())
        for oc in range(NCC):
            def t_sec(t, oc=oc):
                if oc < 2:
                    t.wait_ge(p.sA.h, r_last_ev)
                if (oc - 2) in k_ev:
                    t.wait_ge(p.sA.h, k_ev[oc - 2])
                for cc in range(NCC):
                    mm = t.matmul(psm[oc % 2][:], wk[:, cc * C + oc * 128: cc * C + oc * 128 + 128],
                                  mix2[:, cc * TB:(cc + 1) * TB],
                                  start=(cc == 0), stop=(cc == NCC - 1))
                p.sP.inc(mm)
            p.TEN(t_sec)
            kmm_n = p.sP.n

            def a_sec(s, oc=oc, kmm_n=kmm_n):
                s.wait_ge(p.sP.h, kmm_n)
                k_ev[oc] = p.sA.inc(s.activation(krelu[:], psm[oc % 2][:], AF.Relu))
            p.SCA(a_sec)
            kr_n = p.sA.n

            def v_sec(v, oc=oc, kr_n=kr_n):
                v.wait_ge(p.sA.h, kr_n)
                v.wait_ge(p.sP.h, r_mm_all)
                p.sV.inc(v.tensor_tensor(mix1[:, oc * TB:(oc + 1) * TB],
                                         krelu[:], krelu[:], ALU.mult))
            p.VEC(v_sec)
        kact_n = p.sV.n

        kv_mm = {}
        for oc in range(NCC):
            def g_sec(gq, tb=tb, oc=oc):
                if (tb * NCC + oc) >= 2:
                    gq.wait_ge(p.sP.h, mk["wv_used"][(tb * NCC + oc) - 2])
                elif tb == 0 and oc == 0:
                    p.wait_snap(gq, bar)
                p.sG.inc(gq.dma_start(out=wvb[oc % 2].rearrange("p (cc o) -> p cc o", cc=NCC),
                                      in_=wv_d[:, :, oc * 128:(oc + 1) * 128]), 16)
            p.GPS(g_sec)
            wv_n = p.sG.n

            def t_sec(t, tb=tb, oc=oc, wv_n=wv_n, kact_n=kact_n):
                if oc == 0:
                    t.wait_ge(p.sV.h, kact_n)
                t.wait_ge(p.sG.h, wv_n)
                if (oc - 2) in kv_mm:
                    t.wait_ge(p.sV.h, mk["resadd"][(tb, oc - 2)])
                for cc in range(NCC):
                    mm = t.matmul(psm[oc % 2][:], wvb[oc % 2][:, cc * 128:(cc + 1) * 128],
                                  mix1[:, cc * TB:(cc + 1) * TB],
                                  start=(cc == 0), stop=(cc == NCC - 1))
                kv_mm[oc] = p.sP.inc(mm)
                mk["wv_used"][tb * NCC + oc] = kv_mm[oc]
            p.TEN(t_sec)
            kvm_n = p.sP.n

            def v_sec(v, tb=tb, oc=oc, kvm_n=kvm_n):
                v.wait_ge(p.sP.h, kvm_n)
                v.tensor_tensor(obuf[:], psm[oc % 2][:], r_t[:, oc * TB:(oc + 1) * TB],
                                ALU.mult)
                xs = x_fm[:, oc * T + tb * TB: oc * T + tb * TB + TB]
                mk["resadd"][(tb, oc)] = p.sV.inc(
                    v.tensor_tensor(xs, xs, obuf[:], ALU.add))
            p.VEC(v_sec)
        prev_tb_kv = p.sV.n
        prev_tb_mm = p.sP.n
    p.pop()


def emit_l1_attn(p, env):
    x_fm = env["x_fm"]
    xn_scr_v = env["xn_scr_v"]
    xn_ready = env["xn_ready"]
    ident = env["ident"]
    idsh = env["idsh"]
    ones_col = env["ones_col"]
    ones_row = env["ones_row"]

    wq_d = p.inp("l1q", [C, C], F32).rearrange("(cc p) o -> p cc o", p=128)
    wk_d = p.inp("l1k", [C, C], F32).rearrange("(cc p) o -> p cc o", p=128)
    wv_d = p.inp("l1v", [C, C], F32).rearrange("(cc p) o -> p cc o", p=128)
    wview = {"q": wq_d, "k": wk_d, "v": wv_d}

    bar = p.snap()
    p.push()
    qf = p.sb("l1a_qf", [128, T], BF16)
    kf = p.sb("l1a_kf", [128, T], BF16)
    v_tm = p.sb("l1a_vtm", [128, NKT * 130], BF16)
    v_tm_v = v_tm.rearrange("p (tt c) -> p tt c", c=130)
    wbuf = {nm: p.sb(f"l1a_w_{nm}", [128, NCC * 128], BF16) for nm in ("q", "k", "v")}
    xn_t = [p.sb(f"l1a_xnt{i}", [128, NCC * TB], BF16) for i in range(2)]
    ev = [p.sb(f"l1a_ev{i}", [128, TB], BF16) for i in range(2)]
    sqb = p.sb("l1a_sq", [128, T], BF16)
    nmrow = p.sb("l1a_nmrow", [1, 16], F32)
    smalls = p.sb("l1a_smalls", [1, 16], F32)
    neg_bf = p.sb("l1a_negbf", [1, 2], BF16)
    negc = [p.sb(f"l1a_negc{i}", [128, 1], F32) for i in range(3)]
    eA = [p.sb(f"l1a_eA{i}", [128, TB], BF16) for i in range(2)]
    eB = [p.sb(f"l1a_eB{i}", [128, TB], BF16) for i in range(2)]
    rec_f = p.sb("l1a_recf", [1, TB], F32)
    rec_bf = p.sb("l1a_recbf", [1, TB], BF16)
    rb_sb = p.sb("l1a_rb", [64, TB], F32)
    ybfA = p.sb("l1a_ybfA", [64, TB], BF16)
    ybfB = p.sb("l1a_ybfB", [64, TB], BF16)
    psp = p.ps("l1a_psp", [128, TB])
    pstr = p.ps("l1a_pstr", [128, 128], BF16)
    psA = p.ps("l1a_psA", [128, TB])
    psB = p.ps("l1a_psB", [128, TB])
    psYA = p.ps("l1a_psYA", [65, TB])
    psYB = p.ps("l1a_psYB", [65, TB])
    psN = p.ps("l1a_psN", [1, TB])
    psU = p.ps("l1a_psU", [128, TB])

    mk = {"psp_free": 0, "ev_free": [0, 0], "ev_eng": ["V", "V"], "vtr_ev": 0,
          "sA_ev": 0, "sB_ev": 0, "avA": 0, "avB": 0, "y_done": 0,
          "scores_done": 0, "psN_free": 0, "psC1_free": 0}

    proj_tb_done = {}
    proj_g_done = {}

    for g in range(NCC):
        def gw_sec(gq, g=g):
            if g == 0:
                p.wait_snap(gq, bar)
            if g >= 1:
                gq.wait_ge(p.sP.h, proj_g_done[g - 1])
            for nm in ("q", "k", "v"):
                p.sG.inc(gq.dma_start(out=wbuf[nm][:],
                                      in_=wview[nm][:, :, g * 128:(g + 1) * 128]), 16)
        p.GPS(gw_sec)
        wload_n = p.sG.n

        def ones_sec(v, g=g):
            if mk["scores_done"]:
                v.wait_ge(p.sP.h, mk["scores_done"])
            p.sV.inc(v.memset(v_tm_v[:, :, 64:130:65], 1.0))
        p.VEC(ones_sec)
        vones_n = p.sV.n

        for tb in range(NTB):
            def s_sec(s, g=g, tb=tb):
                if g == 0 and tb == 0:
                    p.wait_snap(s, bar)
                if g == 0:
                    s.wait_ge(p.sS.h, xn_ready[tb])
                key = (g, tb - 2) if tb >= 2 else (g - 1, NTB - 2 + tb)
                if key in proj_tb_done:
                    s.wait_ge(p.sP.h, proj_tb_done[key])
                p.sS.inc(s.dma_start(
                    out=xn_t[tb % 2].rearrange("p (cc t) -> p cc t", cc=NCC),
                    in_=xn_scr_v[:, :, tb * TB:(tb + 1) * TB]), 16)
            p.SYN(s_sec)
            xn_n = p.sS.n

            for pi, nm in enumerate(("q", "k", "v")):
                def t_sec(t, tb=tb, pi=pi, nm=nm, xn_n=xn_n, wload_n=wload_n):
                    if pi == 0:
                        t.wait_ge(p.sS.h, xn_n)
                        t.wait_ge(p.sG.h, wload_n)
                    if mk["psp_free"]:
                        t.wait_ge(p.sA.h, mk["psp_free"])
                    for cc in range(NCC):
                        mm = t.matmul(psp[:], wbuf[nm][:, cc * 128:(cc + 1) * 128],
                                      xn_t[tb % 2][:, cc * TB:(cc + 1) * TB],
                                      start=(cc == 0), stop=(cc == NCC - 1))
                    p.sP.inc(mm)
                p.TEN(t_sec)
                mm_n = p.sP.n

                if nm in ("q", "k"):
                    def a_sec(s, tb=tb, nm=nm, mm_n=mm_n):
                        s.wait_ge(p.sP.h, mm_n)
                        dst = (qf if nm == "q" else kf)[:, tb * TB:(tb + 1) * TB]
                        mk["psp_free"] = p.sA.inc(s.activation(dst, psp[:], AF.Copy))
                    p.SCA(a_sec)
                else:
                    def a_sec(s, pi=pi, mm_n=mm_n):
                        s.wait_ge(p.sP.h, mm_n)
                        eng, cnt = mk["ev_eng"][pi % 2], mk["ev_free"][pi % 2]
                        if cnt:
                            s.wait_ge((p.sV if eng == "V" else p.sP).h, cnt)
                        mk["psp_free"] = p.sA.inc(
                            s.activation(ev[pi % 2][:], psp[:], AF.Copy))
                    p.SCA(a_sec)
                    ev_n = p.sA.n
                    for j in range(4):
                        def t3_sec(t, j=j, ev_n=ev_n, pi=pi):
                            if j == 0:
                                t.wait_ge(p.sA.h, ev_n)
                            if mk["vtr_ev"]:
                                t.wait_ge(p.sV.h, mk["vtr_ev"])
                            p.sP.inc(t.transpose(pstr[:],
                                                 ev[pi % 2][:, j * 128:(j + 1) * 128],
                                                 ident[:]))
                        p.TEN(t3_sec)
                        tr_n = p.sP.n

                        def v_sec(v, tb=tb, j=j, tr_n=tr_n, vones_n=vones_n):
                            v.wait_ge(p.sP.h, tr_n)
                            if tb == 0 and j == 0:
                                v.wait_ge(p.sV.h, vones_n)
                            kt = tb * 4 + j
                            dst = v_tm_v[:, kt, :].rearrange(
                                "p (two r) -> p two r", r=65)[:, :, 0:64]
                            mk["vtr_ev"] = p.sV.inc(
                                v.tensor_copy(dst,
                                    pstr[:].rearrange("p (two r) -> p two r", two=2)))
                        p.VEC(v_sec)
                    mk["ev_free"][pi % 2] = mk["vtr_ev"]
                    mk["ev_eng"][pi % 2] = "V"
            proj_tb_done[(g, tb)] = p.sP.n
        proj_g_done[g] = p.sP.n

        def sq_sec(s, g=g):
            s.wait_ge(p.sA.h, mk["psp_free"])
            s.wait_ge(p.sP.h, proj_g_done[g])
            p.sA.inc(s.activation(sqb[:], qf[:], AF.Square))
        p.SCA(sq_sec)
        sqq_n = p.sA.n

        norm_res = {}
        for which, src_n in (("q", sqq_n), ("k", None)):
            if which == "k":
                def sq2_sec(s, g=g):
                    s.wait_ge(p.sV.h, norm_res[("q", 1)])
                    p.sA.inc(s.activation(sqb[:], kf[:], AF.Square))
                p.SCA(sq2_sec)
                src_n = p.sA.n
            for hi, base in ((0, 0), (1, 64)):
                for tb in range(NTB):
                    def tn_sec(t, hi=hi, base=base, tb=tb, src_n=src_n):
                        t.wait_ge(p.sA.h, src_n)
                        if mk["psN_free"]:
                            t.wait_ge(p.sV.h, mk["psN_free"])
                        p.sP.inc(t.matmul(psN[:], ones_col[base:base + 64, 0:1],
                                          sqb[base:base + 64, tb * TB:(tb + 1) * TB],
                                          start=True, stop=True,
                                          tile_position=(base, 0)))
                    p.TEN(tn_sec)
                    tn_n = p.sP.n

                    def vr_sec(v, which=which, hi=hi, tb=tb, tn_n=tn_n):
                        v.wait_ge(p.sP.h, tn_n)
                        mk["psN_free"] = p.sV.inc(
                            v.reduce_max(nmrow[:, tb:tb + 1], psN[:], axis=AX.X))
                    p.VEC(vr_sec)

                def vm_sec(v, which=which, hi=hi):
                    idx = {"q": 0, "k": 2}[which] + hi
                    norm_res[(which, hi)] = p.sV.inc(
                        v.reduce_max(smalls[:, idx:idx + 1], nmrow[:, 0:NTB], axis=AX.X))
                p.VEC(vm_sec)
        qkmax_n = p.sV.n

        def vc_sec(v, qkmax_n=qkmax_n):
            v.wait_ge(p.sV.h, qkmax_n)
            v.tensor_tensor(smalls[:, 4:5], smalls[:, 0:1], smalls[:, 2:3], ALU.mult)
            p.sV.inc(v.tensor_tensor(smalls[:, 5:6], smalls[:, 1:2], smalls[:, 3:4],
                                     ALU.mult))
        p.VEC(vc_sec)
        prod_n = p.sV.n

        def ac_sec(s, prod_n=prod_n):
            s.wait_ge(p.sV.h, prod_n)
            p.sA.inc(s.activation(smalls[:, 6:8], smalls[:, 4:6], AF.Sqrt))
        p.SCA(ac_sec)
        csr_n = p.sA.n

        def vc2_sec(v, csr_n=csr_n):
            v.wait_ge(p.sA.h, csr_n)
            v.tensor_scalar_mul(smalls[:, 8:10], smalls[:, 6:8], -0.125)
            p.sV.inc(v.tensor_copy(neg_bf[:], smalls[:, 8:10]))
        p.VEC(vc2_sec)
        nbf_n = p.sV.n

        def tc_sec(t, nbf_n=nbf_n):
            t.wait_ge(p.sV.h, nbf_n)
            if mk["psC1_free"]:
                t.wait_ge(p.sV.h, mk["psC1_free"])
            p.sP.inc(t.matmul(psU[:, 0:2], ones_row[:], neg_bf[:],
                              start=True, stop=True))
        p.TEN(tc_sec)
        tc_n = p.sP.n

        def vc3_sec(v, tc_n=tc_n):
            v.wait_ge(p.sP.h, tc_n)
            v.tensor_copy(negc[0][:], psU[:, 0:1])
            v.tensor_copy(negc[1][:], psU[:, 1:2])
            mk["psC1_free"] = p.sV.inc(
                v.tensor_tensor(negc[2][:], negc[0][:], negc[1][:], ALU.min))
        p.VEC(vc3_sec)
        negc_n = p.sV.n

        for qw in range(NTB):
            pbase = p.sP.n
            abase = p.sA.n

            def a_win(s, g=g, qw=qw, pbase=pbase):
                for kt in range(NKT):
                    s.wait_ge(p.sP.h, pbase + 2 * kt + 1)
                    s.activation(eA[kt % 2][:], psA[:], AF.Exp, scale=0.125,
                                 bias=negc[0][:])
                    p.sA.inc(s.activation(eB[kt % 2][:], psB[:], AF.Exp, scale=0.125,
                                          bias=negc[1][:]))
            p.SCA(a_win)
            mk["sA_ev"] = p.sA.n

            def t_win(t, g=g, qw=qw, negc_n=negc_n, pbase=pbase, abase=abase):
                for kt in range(NKT):
                    if qw == 0 and kt == 0:
                        t.wait_ge(p.sV.h, negc_n)
                    if kt >= 1:
                        t.wait_ge(p.sA.h, abase + kt)
                    qs = slice(qw * TB, (qw + 1) * TB)
                    ks = slice(kt * 128, (kt + 1) * 128)
                    t.matmul(psA[:], kf[0:64, ks], qf[0:64, qs],
                             start=True, stop=True, tile_position=(0, 0))
                    mm = t.matmul(psB[:], kf[64:128, ks], qf[64:128, qs],
                                  start=True, stop=True, tile_position=(64, 0))
                    p.sP.inc(mm)
                    t.wait_ge(p.sA.h, abase + kt + 1)
                    if kt == 0 and mk["y_done"]:
                        t.wait_ge(p.sV.h, mk["y_done"])
                    t.matmul(psYA[:], v_tm[:, kt * 130: kt * 130 + 65], eA[kt % 2][:],
                             start=(kt == 0), stop=(kt == NKT - 1), skip_group_check=True)
                    mm = t.matmul(psYB[:], v_tm[:, kt * 130 + 65: kt * 130 + 130],
                                  eB[kt % 2][:], start=(kt == 0), stop=(kt == NKT - 1),
                                  skip_group_check=True)
                    mk["avB"] = p.sP.inc(mm)
            p.TEN(t_win)
            y_n = p.sP.n

            for hi, psY, yb in ((0, psYA, ybfA), (1, psYB, ybfB)):
                def vr_sec(v, psY=psY, y_n=y_n):
                    v.wait_ge(p.sP.h, y_n)
                    v.reciprocal(rec_f[:], psY[64:65, :])
                    p.sV.inc(v.tensor_copy(rec_bf[:], rec_f[:]))
                p.VEC(vr_sec)
                rb_n = p.sV.n

                def tb_sec(t, rb_n=rb_n):
                    t.wait_ge(p.sV.h, rb_n)
                    if mk["psC1_free"]:
                        t.wait_ge(p.sV.h, mk["psC1_free"])
                    p.sP.inc(t.matmul(psU[0:64, :], ones_row[0:1, 0:64], rec_bf[:],
                                      start=True, stop=True))
                p.TEN(tb_sec)
                tb_n = p.sP.n

                def vy_sec(v, psY=psY, yb=yb, tb_n=tb_n):
                    v.wait_ge(p.sP.h, tb_n)
                    v.tensor_copy(rb_sb[:], psU[0:64, :])
                    mk["psC1_free"] = p.sV.n + 1
                    p.sV.inc(v.tensor_tensor(yb[:], psY[0:64, :], rb_sb[:], ALU.mult))
                p.VEC(vy_sec)
            mults_n = p.sV.n

            def tsh_sec(t, mults_n=mults_n):
                t.wait_ge(p.sV.h, mults_n)
                t.matmul(psU[:], ident[0:64, :], ybfA[:], start=True, stop=False)
                p.sP.inc(t.matmul(psU[:], idsh[0:64, :], ybfB[:],
                                  start=False, stop=True))
            p.TEN(tsh_sec)
            tsh_n = p.sP.n

            def vadd_sec(v, g=g, qw=qw, tsh_n=tsh_n):
                v.wait_ge(p.sP.h, tsh_n)
                xs = x_fm[:, g * T + qw * TB: g * T + qw * TB + TB]
                mk["psC1_free"] = p.sV.inc(v.tensor_tensor(xs, xs, psU[:], ALU.add))
                mk["y_done"] = mk["psC1_free"]
            p.VEC(vadd_sec)
        mk["scores_done"] = p.sP.n
    p.pop()


def emit_spill_x(p, env):
    import os as _os
    x_scr = p.nc.dram_tensor("x_scr", [C, T], F32, kind="Internal").ap()
    env["x_scr_v"] = x_scr.rearrange("(cc p) t -> cc p t", p=128)
    x_fm = env["x_fm"]
    bar = p.snap()
    dbgx = _os.environ.get("DBGX", "0") == "1"
    if dbgx:
        x3d = p.nc.dram_tensor("x3dbg", [C, T], F32, kind="ExternalOutput").ap()
        x3v = x3d.rearrange("(cc p) t -> cc p t", p=128)

    def sec(s):
        p.wait_snap(s, bar)
        for cc in range(NCC):
            p.sS.inc(s.dma_start(out=env["x_scr_v"][cc],
                                 in_=x_fm[:, cc * T:(cc + 1) * T]), 16)
            if dbgx:
                p.sO.inc(s.dma_start(out=x3v[cc],
                                     in_=x_fm[:, cc * T:(cc + 1) * T]), 16)
    p.SYN(sec)
    env["x_spilled"] = p.sS.n


def emit_l1_ffn(p, env):
    xn_scr_v = env["xn_scr_v"]
    xn_ready = env["xn_ready"]
    x_scr_v = env["x_scr_v"]
    xout = p.nc.dram_tensor("xout", [C, T], F32, kind="ExternalOutput").ap()
    xout_v = xout.rearrange("(cc p) t -> cc p t", p=128)

    F1, F2 = 4096, 4096
    NF1, NF2 = 32, 32

    k0_d = p.inp("l1f_k0t", [32, 128, 16 * 128], U8)
    k1_d = p.inp("l1f_k1t", [32, 128, 32 * 128], U8)
    k2_d = p.inp("l1f_k2t", [8, 128, 32 * 128], U8)
    s0_d = p.inp("l1f_s0", [128, 1], F32)
    s1_d = p.inp("l1f_s1", [128, 1], F32)
    s2_d = p.inp("l1f_s2", [128, 1], F32)

    bar = p.snap()
    import os as _os
    p.serial = frozenset(_os.environ.get("SERIAL3", "s"))
    fixraw = _os.environ.get("FIXRAW", "0") == "1"
    fixact = _os.environ.get("FIXACT", "0") == "1"
    dbg3 = _os.environ.get("DBG3", "0") == "1"
    if dbg3:
        d_wab = p.nc.dram_tensor("d_wab", [NTB * 32, 128, 2048], BF16, kind="ExternalOutput").ap()
        d_w0e = p.nc.dram_tensor("d_w0e", [NTB * 32, 128, C], BF16, kind="ExternalOutput").ap()
        d_xn = p.nc.dram_tensor("d_xn", [NTB, 128, NCC * TB], BF16, kind="ExternalOutput").ap()
        d_h1 = p.nc.dram_tensor("d_h1", [NTB, 128, 32 * TB], BF16, kind="ExternalOutput").ap()
        d_h2 = p.nc.dram_tensor("d_h2", [NTB, 128, 32 * TB], BF16, kind="ExternalOutput").ap()
    p.push()
    wprobe = p.sb("l1f_wprobe", [128, 8], BF16)
    aprobe = p.sb("l1f_aprobe", [128, 8], BF16)
    h1 = p.sb("l1f_h1", [128, NF1 * TB], BF16)
    h2 = p.sb("l1f_h2", [128, NF2 * TB], BF16)
    xn_t = p.sb("l1f_xnt", [128, NCC * TB], BF16)
    wab = [p.sb(f"l1f_wab{i}", [128, 2 * NCC * 128], U8) for i in range(2)]
    w0e = [p.sb(f"l1f_w0e{i}", [128, C], BF16) for i in range(2)]
    wu = [p.sb(f"l1f_wu{i}", [128, F1], U8) for i in range(2)]
    we = [p.sb(f"l1f_we{i}", [128, F1], BF16) for i in range(2)]
    sc_t = p.sb("l1f_sc", [128, 4], F32)
    xblk = [p.sb(f"l1f_xb{i}", [128, TB], F32) for i in range(2)]
    ybuf = [p.sb(f"l1f_yb{i}", [128, TB], F32) for i in range(2)]
    psm = [p.ps(f"l1f_ps{i}", [128, TB]) for i in range(2)]

    def sld_sec(s):
        p.wait_snap(s, bar)
        p.sS.inc(s.dma_start(out=sc_t[:, 0:1], in_=s0_d), 16)
        p.sS.inc(s.dma_start(out=sc_t[:, 1:2], in_=s1_d), 16)
        p.sS.inc(s.dma_start(out=sc_t[:, 2:3], in_=s2_d), 16)
    p.SYN(sld_sec)
    sld_n = p.sS.n

    def sa_sec(s):
        s.wait_ge(p.sS.h, sld_n)
        s.activation(sc_t[:, 0:1], sc_t[:, 0:1], AF.Copy, scale=4.0)
        p.sA.inc(s.activation(sc_t[:, 1:2], sc_t[:, 1:2], AF.Copy, scale=2.0))
    p.SCA(sa_sec)
    scales_n = p.sA.n

    mk = {"conv": {}, "mm": {}, "ev": {}, "l1mm_tb": {}, "l2mm_tb": {}, "l3mm_tb": {},
          "stt": {}, "yst": {}, "xb_used": {}, "wconv": {}, "wmm": {}}
    gidx = 0
    wj = 0

    for tb in range(NTB):
        ts0 = tb * TB

        def s_sec(s, tb=tb):
            if tb == 0:
                p.wait_snap(s, bar)
                s.wait_ge(p.sS.h, env["x_spilled"])
            s.wait_ge(p.sS.h, xn_ready[tb])
            if tb >= 1:
                s.wait_ge(p.sP.h, mk["l1mm_tb"][tb - 1])
            p.sS.inc(s.dma_start(
                out=xn_t.rearrange("p (cc t) -> p cc t", cc=NCC),
                in_=xn_scr_v[:, :, ts0:ts0 + TB]), 16)
        p.SYN(s_sec)
        xn_n = p.sS.n

        if dbg3:
            def dxn_sec(s, tb=tb, xn_n=xn_n):
                s.wait_ge(p.sG.h, xn_n)
                p.sO.inc(s.dma_start(out=d_xn[tb], in_=xn_t[:]), 16)
            p.SYN(dxn_sec)

        l1_pb = p.sP.n
        l1_ab = p.sA.n
        for fc in range(NF1):
            ci = tb * NF1 + fc

            def g_sec(g, tb=tb, fc=fc, ci=ci):
                if tb == 0 and fc == 0:
                    p.wait_snap(g, bar)
                if (ci - 2) in mk["conv"]:
                    g.wait_ge(p.sV.h, mk["conv"][ci - 2])
                p.sS.inc(g.dma_start(out=wab[fc % 2][:], in_=k0_d[fc]), 16)
            p.SYN(g_sec)
            gl_n = p.sS.n

            def v_sec(v, fc=fc, ci=ci, gl_n=gl_n):
                v.wait_ge(p.sS.h, gl_n)
                if (ci - 2) in mk["mm"]:
                    v.wait_ge(p.sP.h, mk["mm"][ci - 2])
                tt = v.tensor_tensor(w0e[fc % 2][:], wab[fc % 2][:, 0:NCC * 128],
                                     wab[fc % 2][:, NCC * 128:], ALU.subtract)
                if fixraw:
                    tt = v.tensor_copy(wprobe[:], w0e[fc % 2][:, 0:8])
                mk["conv"][ci] = p.sV.inc(tt)
            p.VEC(v_sec)
            cv_n = p.sV.n

            if dbg3:
                def dw_sec(s, tb=tb, fc=fc, ci=ci, gl_n=gl_n, cv_n=cv_n):
                    s.wait_ge(p.sG.h, gl_n)
                    s.wait_ge(p.sV.h, cv_n)
                    p.sO.inc(s.dma_start(out=d_wab[ci], in_=wab[fc % 2][:]), 16)
                    p.sO.inc(s.dma_start(out=d_w0e[ci], in_=w0e[fc % 2][:]), 16)
                    mk["dw"] = p.sO.n
                p.SYN(dw_sec)

            def t_sec(t, tb=tb, fc=fc, ci=ci, cv_n=cv_n, xn_n=xn_n, gi=gidx,
                      l1_ab=l1_ab):
                t.wait_ge(p.sV.h, cv_n)
                if fc == 0:
                    t.wait_ge(p.sS.h, xn_n)
                if fc >= 2:
                    t.wait_ge(p.sA.h, l1_ab + fc - 1)
                elif (gi - 2) in mk["ev"]:
                    eng, cnt = mk["ev"][gi - 2]
                    t.wait_ge((p.sA if eng == "A" else p.sV).h, cnt)
                for cc in range(NCC):
                    mm = t.matmul(psm[gi % 2][:],
                                  w0e[fc % 2][:, cc * 128:(cc + 1) * 128],
                                  xn_t[:, cc * TB:(cc + 1) * TB],
                                  start=(cc == 0), stop=(cc == NCC - 1))
                mk["mm"][ci] = p.sP.inc(mm)
            p.TEN(t_sec)
            gidx += 1

        gi0 = gidx - NF1

        def a_l1(s, tb=tb, l1_pb=l1_pb, gi0=gi0, scales_n=scales_n):
            for fc in range(NF1):
                s.wait_ge(p.sP.h, l1_pb + fc + 1)
                if tb == 0 and fc == 0:
                    s.wait_ge(p.sA.h, scales_n)
                p.sA.inc(s.activation(h1[:, fc * TB:(fc + 1) * TB],
                                      psm[(gi0 + fc) % 2][:],
                                      AF.Relu, scale=sc_t[:, 0:1]))
        p.SCA(a_l1)
        for fc in range(NF1):
            mk["ev"][gi0 + fc] = ("A", l1_ab + fc + 1)
        l1ev_n = p.sA.n
        mk["l1mm_tb"][tb] = p.sP.n

        if dbg3:
            def dh1_sec(s, tb=tb, l1ev_n=l1ev_n):
                s.wait_ge(p.sA.h, l1ev_n)
                p.sO.inc(s.dma_start(out=d_h1[tb], in_=h1[:]), 16)
            p.SYN(dh1_sec)

        l2_pb = p.sP.n
        l2_ab = p.sA.n
        for fc in range(NF2):
            ci = 1000 + tb * NF2 + fc

            def g_sec(g, fc=fc, wj=wj):
                if (wj - 2) in mk["wconv"]:
                    g.wait_ge(p.sV.h, mk["wconv"][wj - 2])
                p.sS.inc(g.dma_start(out=wu[wj % 2][:], in_=k1_d[fc]), 16)
            p.SYN(g_sec)
            gl_n = p.sS.n

            def v_sec(v, wj=wj, gl_n=gl_n):
                v.wait_ge(p.sS.h, gl_n)
                if (wj - 2) in mk["wmm"]:
                    v.wait_ge(p.sP.h, mk["wmm"][wj - 2])
                tt = v.tensor_scalar(we[wj % 2][:], wu[wj % 2][:], 2.0, -1.0,
                                     ALU.mult, ALU.add)
                if fixraw:
                    tt = v.tensor_copy(wprobe[:], we[wj % 2][:, 0:8])
                mk["wconv"][wj] = p.sV.inc(tt)
            p.VEC(v_sec)
            cv_n = p.sV.n

            def t_sec(t, fc=fc, wj=wj, cv_n=cv_n, gi=gidx, l1ev_n=l1ev_n,
                      l2_ab=l2_ab):
                t.wait_ge(p.sV.h, cv_n)
                if fc == 0:
                    t.wait_ge(p.sA.h, l1ev_n)
                if fc >= 2:
                    t.wait_ge(p.sA.h, l2_ab + fc - 1)
                elif (gi - 2) in mk["ev"]:
                    eng, cnt = mk["ev"][gi - 2]
                    t.wait_ge((p.sA if eng == "A" else p.sV).h, cnt)
                for cc in range(NF1):
                    mm = t.matmul(psm[gi % 2][:],
                                  we[wj % 2][:, cc * 128:(cc + 1) * 128],
                                  h1[:, cc * TB:(cc + 1) * TB],
                                  start=(cc == 0), stop=(cc == NF1 - 1))
                mk["wmm"][wj] = p.sP.inc(mm)
            p.TEN(t_sec)
            gidx += 1
            wj += 1

        gi0 = gidx - NF2

        def a_l2(s, tb=tb, l2_pb=l2_pb, gi0=gi0):
            for fc in range(NF2):
                s.wait_ge(p.sP.h, l2_pb + fc + 1)
                p.sA.inc(s.activation(h2[:, fc * TB:(fc + 1) * TB],
                                      psm[(gi0 + fc) % 2][:],
                                      AF.Relu, scale=sc_t[:, 1:2]))
        p.SCA(a_l2)
        for fc in range(NF2):
            mk["ev"][gi0 + fc] = ("A", l2_ab + fc + 1)
        l2ev_n = p.sA.n
        mk["l2mm_tb"][tb] = p.sP.n

        if dbg3:
            def dh2_sec(s, tb=tb, l2ev_n=l2ev_n):
                s.wait_ge(p.sA.h, l2ev_n)
                p.sO.inc(s.dma_start(out=d_h2[tb], in_=h2[:]), 16)
            p.SYN(dh2_sec)

        for oc in range(NCC):
            ci = 2000 + tb * NCC + oc

            def g_sec(g, oc=oc, wj=wj):
                if (wj - 2) in mk["wconv"]:
                    g.wait_ge(p.sV.h, mk["wconv"][wj - 2])
                p.sS.inc(g.dma_start(out=wu[wj % 2][:], in_=k2_d[oc]), 16)
            p.SYN(g_sec)
            gl_n = p.sS.n

            def sx_sec(s, tb=tb, oc=oc, ci=ci):
                if (ci - 2) in mk["xb_used"]:
                    s.wait_ge(p.sV.h, mk["xb_used"][ci - 2])
                p.sS.inc(s.dma_start(out=xblk[oc % 2][:],
                                     in_=x_scr_v[oc][:, tb * TB:(tb + 1) * TB]), 16)
            p.SYN(sx_sec)
            xb_n = p.sS.n

            def v_sec(v, wj=wj, gl_n=gl_n):
                v.wait_ge(p.sS.h, gl_n)
                if (wj - 2) in mk["wmm"]:
                    v.wait_ge(p.sP.h, mk["wmm"][wj - 2])
                tt = v.tensor_scalar(we[wj % 2][:], wu[wj % 2][:], 2.0, -1.0,
                                     ALU.mult, ALU.add)
                if fixraw:
                    tt = v.tensor_copy(wprobe[:], we[wj % 2][:, 0:8])
                mk["wconv"][wj] = p.sV.inc(tt)
            p.VEC(v_sec)
            cv_n = p.sV.n

            def t_sec(t, oc=oc, wj=wj, cv_n=cv_n, gi=gidx, l2ev_n=l2ev_n):
                t.wait_ge(p.sV.h, cv_n)
                if oc == 0:
                    t.wait_ge(p.sA.h, l2ev_n)
                if (gi - 2) in mk["ev"]:
                    eng, cnt = mk["ev"][gi - 2]
                    t.wait_ge((p.sA if eng == "A" else p.sV).h, cnt)
                for cc in range(NF2):
                    mm = t.matmul(psm[gi % 2][:],
                                  we[wj % 2][:, cc * 128:(cc + 1) * 128],
                                  h2[:, cc * TB:(cc + 1) * TB],
                                  start=(cc == 0), stop=(cc == NF2 - 1))
                mk["wmm"][wj] = p.sP.inc(mm)
            p.TEN(t_sec)
            mm_n = p.sP.n

            def vy_sec(v, tb=tb, oc=oc, ci=ci, mm_n=mm_n, xb_n=xb_n, gi=gidx):
                v.wait_ge(p.sP.h, mm_n)
                v.wait_ge(p.sS.h, xb_n)
                if (ci - 2) in mk["yst"]:
                    v.wait_ge(p.sO.h, mk["yst"][ci - 2])
                out = p.sV.inc(
                    v.scalar_tensor_tensor(ybuf[oc % 2][:], psm[gi % 2][:],
                                           sc_t[:, 2:3], xblk[oc % 2][:],
                                           ALU.mult, ALU.add))
                mk["xb_used"][ci] = out
                mk["stt"][ci] = out
                mk["ev"][gi] = ("V", out)
            p.VEC(vy_sec)
            stt_n = p.sV.n

            def sy_sec(s, tb=tb, oc=oc, ci=ci, stt_n=stt_n):
                s.wait_ge(p.sV.h, stt_n)
                mk["yst"][ci] = p.sO.inc(
                    s.dma_start(out=xout_v[oc][:, tb * TB:(tb + 1) * TB],
                                in_=ybuf[oc % 2][:]), 16)
            p.SYN(sy_sec)
            gidx += 1
            wj += 1
    p.serial = frozenset()
    p.pop()


def build_program(blocks):
    nc = bass.Bass("TRN2", target_bir_lowering=False)
    st = ExitStack()
    with st:
        p = Prog(nc, st)
        env = {}

        ident = p.sb("ident", [128, 128], BF16)
        pswap = p.sb("pswap", [128, 128], BF16)
        ones_col = p.sb("ones_col", [128, 1], BF16)
        ones_row = p.sb("ones_row", [1, 128], BF16)
        ones_row_f = p.sb("ones_row_f", [1, 128], F32)
        env.update(ident=ident, pswap=pswap, ones_col=ones_col, ones_row=ones_row)

        id_d = p.inp("ident_in", [128, 128], BF16)
        ps_d = p.inp("pswap_in", [128, 128], BF16)
        idsh = p.sb("idsh", [128, 128], BF16)
        env["idsh"] = idsh
        ish_d = p.inp("idsh_in", [128, 128], BF16)

        def const_sec(gq):
            p.sG.inc(gq.dma_start(out=ident[:], in_=id_d), 16)
            p.sG.inc(gq.dma_start(out=pswap[:], in_=ps_d), 16)
            p.sG.inc(gq.dma_start(out=idsh[:], in_=ish_d), 16)
        p.GPS(const_sec)

        eps_t = p.sb("eps_t", [1, 1], F32)
        env["eps_t"] = eps_t

        def ones_sec(v):
            v.memset(ones_col[:], 1.0)
            v.memset(eps_t[:], EPS)
            v.memset(ones_row_f[:], 1.0)
            p.sV.inc(v.memset(ones_row[:], 1.0))
        p.VEC(ones_sec)
        env["ones_row_f"] = ones_row_f

        xn_scr = nc.dram_tensor("xn_scr", [C, T], BF16, kind="Internal").ap()
        env["xn_scr_v"] = xn_scr.rearrange("(cc p) t -> p cc t", p=128)

        p.push()
        x_fm = p.sb("x_fm", [128, NCC * T], F32)
        env["x_fm"] = x_fm
        emit_load_x(p, env)

        if 0 in blocks:
            emit_ln(p, env, "0")
            emit_l0_attn(p, env)

        if 1 in blocks:
            emit_ln(p, env, "1")
            emit_l0_ffn(p, env)

        if 2 in blocks:
            emit_ln(p, env, "2")
            emit_l1_attn(p, env)

        if 3 in blocks:
            emit_ln(p, env, "3")
            emit_spill_x(p, env)
            p.pop()
            emit_l1_ffn(p, env)
        else:
            emit_store_x(p, env)
            p.pop()
        emit_final_wait(p)

    return nc, sorted(p.used_inputs)


def host_constants():
    bf = ml_dtypes.bfloat16
    ident = np.eye(128, dtype=np.float32)
    pswap = np.zeros((128, 128), np.float32)
    for j in range(128):
        pswap[j ^ 32, j] = 1.0
    inv_freq = (1.0 / (10000.0 ** (np.arange(0, HD, 2, dtype=np.float32) / HD))).astype(np.float32)
    tpos = np.arange(T, dtype=np.float32)
    freqs = tpos[:, None] * inv_freq[None, :]
    cos_t = np.cos(freqs).astype(bf).astype(np.float32)
    sin_t = np.sin(freqs).astype(bf).astype(np.float32)
    Ctab = np.zeros((128, T), np.float32)
    Stab = np.zeros((128, T), np.float32)
    for pr in range(128):
        i = pr % 32
        Ctab[pr] = cos_t[:, i]
        Stab[pr] = sin_t[:, i] * (1.0 if (pr % 64) < 32 else -1.0)
    cmask = np.zeros((128, 4 * TB), np.float32)
    qf = np.arange(TB)
    for r in range(4):
        for kp in range(128):
            cmask[kp, r * TB:(r + 1) * TB] = np.where(qf >= r * 128 + kp, 1.0 / 4096.0, 0.0)
    idsh = np.zeros((128, 128), np.float32)
    for i in range(64):
        idsh[i, i + 64] = 1.0
    return {"ident_in": ident.astype(bf), "pswap_in": pswap.astype(bf),
            "idsh_in": idsh.astype(bf),
            "rope_c": Ctab.astype(bf), "rope_s": Stab.astype(bf),
            "cmask": cmask.astype(bf)}


def host_prep(inputs):
    f32 = np.float32
    shared = dict(host_constants())
    tr = lambda a: np.ascontiguousarray(np.asarray(a, dtype=f32).T)
    shared["l0q1"] = tr(inputs["L0_attn_Wq1"])
    shared["l0k1"] = tr(inputs["L0_attn_Wk1"])
    shared["l0q2"] = tr(inputs["L0_attn_Wq2"])
    shared["l0k2"] = tr(inputs["L0_attn_Wk2"])
    shared["l0v"] = tr(inputs["L0_attn_Wv"])
    shared["l0o"] = tr(inputs["L0_attn_Wout"])
    shared["l0fr"] = tr(inputs["L0_ffn_W_r"])
    shared["l0fk"] = tr(inputs["L0_ffn_W_k"])
    shared["l0fv"] = tr(inputs["L0_ffn_W_v"])
    shared["l0f_mur"] = np.ascontiguousarray(
        np.asarray(inputs["L0_ffn_mu_r"], f32).reshape(NCC, 128).T)
    shared["l0f_muk"] = np.ascontiguousarray(
        np.asarray(inputs["L0_ffn_mu_k"], f32).reshape(NCC, 128).T)
    cat = lambda w: np.ascontiguousarray(
        np.asarray(w, f32).transpose(1, 0, 2).reshape(C, C))
    shared["l1q"] = cat(inputs["L1_attn_W_Q"])
    shared["l1k"] = cat(inputs["L1_attn_W_K"])
    shared["l1v"] = cat(inputs["L1_attn_W_V"])
    k0 = np.ascontiguousarray(np.asarray(inputs["L1_ffn_k0"])).view(np.uint8)
    k1_ = np.ascontiguousarray(np.asarray(inputs["L1_ffn_k1"])).view(np.uint8)
    k2_ = np.ascontiguousarray(np.asarray(inputs["L1_ffn_k2"])).view(np.uint8)

    def wtile(k, ng, nf):
        return np.ascontiguousarray(
            k.reshape(ng, 128, nf, 128).transpose(2, 1, 0, 3).reshape(nf, 128, ng * 128))
    shared["l1f_k0t"] = wtile(k0, 16, 32)
    shared["l1f_k1t"] = wtile(k1_, 32, 32)
    shared["l1f_k2t"] = wtile(k2_, 32, 8)
    for i in (0, 1, 2):
        shared[f"l1f_s{i}"] = np.ascontiguousarray(np.broadcast_to(
            np.asarray(inputs[f"L1_ffn_s{i}"], f32).reshape(1, 1), (128, 1)))
    return shared


LAUNCH_GROUPS = [[0, 1, 2, 3]]


def kernel(**inputs):
    x = np.asarray(inputs["L0_attn_x"], dtype=np.float32)
    shared = host_prep(inputs)
    state = [np.ascontiguousarray(x[b].T) for b in range(8)]

    for grp in LAUNCH_GROUPS:
        nc, needed = build_program(grp)
        in_maps = []
        for b in range(8):
            m = {}
            for k in needed:
                m[k] = state[b] if k == "x" else shared[k]
            in_maps.append(m)
        res = run_bass_kernel_spmd(nc, in_maps, core_ids=list(range(8)))
        state = [np.asarray(res.results[b]["xout"]) for b in range(8)]

    out = np.stack([s.T for s in state]).astype(np.float32)
    return np.ascontiguousarray(out)


# version 15
# speedup vs baseline: 1.0107x; 1.0083x over previous
import numpy as np
import ml_dtypes
from contextlib import ExitStack

import concourse.bass as bass
import concourse.mybir as mybir
from concourse.bass_utils import run_bass_kernel_spmd

F32 = mybir.dt.float32
BF16 = mybir.dt.bfloat16
U8 = mybir.dt.uint8
AF = mybir.ActivationFunctionType
ALU = mybir.AluOpType
AX = mybir.AxisListType

C = 1024
T = 4096
NCC = 8
NTB = 8
TB = 512
NKT = 32
NH = 16
HD = 64
EPS = 1e-5


class Sem:
    __slots__ = ("h", "n")

    def __init__(self, h):
        self.h = h
        self.n = 0

    def inc(self, instr, k=1):
        instr.then_inc(self.h, k)
        self.n += k
        return self.n


class Prog:
    def __init__(self, nc, st):
        self.nc = nc
        self.stacks = [st]
        self.blk = st.enter_context(nc.Block())
        self.used_inputs = set()
        for name in ("sS", "sG", "sQ", "sP", "sV", "sA", "sO"):
            setattr(self, name, Sem(st.enter_context(nc.semaphore(name))))

    serial = frozenset()

    def _wrap(self, f, kind):
        if kind not in self.serial:
            return f
        snapshot = self.snap()

        def g(eng):
            self.wait_snap(eng, snapshot)
            f(eng)
        return g

    def TEN(self, f): self.blk.tensor(self._wrap(f, "t"))
    def VEC(self, f): self.blk.vector(self._wrap(f, "v"))
    def SCA(self, f): self.blk.scalar(self._wrap(f, "a"))
    def SYN(self, f): self.blk.sync(self._wrap(f, "s"))
    def GPS(self, f): self.blk.gpsimd(self._wrap(f, "g"))

    def inp(self, name, shape, dtype):
        self.used_inputs.add(name)
        return self.nc.dram_tensor(name, shape, dtype, kind="ExternalInput").ap()

    def sb(self, name, shape, dtype):
        return self.stacks[-1].enter_context(self.nc.sbuf_tensor(name, shape, dtype))

    def ps(self, name, shape, dtype=F32):
        return self.stacks[-1].enter_context(self.nc.psum_tensor(name, shape, dtype))

    def push(self):
        st = ExitStack()
        self.stacks.append(st)
        return st

    def pop(self):
        st = self.stacks.pop()
        st.close()

    def snap(self):
        return {k: getattr(self, k).n
                for k in ("sS", "sG", "sQ", "sP", "sV", "sA")}

    def wait_snap(self, eng, snapshot):
        for k, v in snapshot.items():
            if v > 0:
                eng.wait_ge(getattr(self, k).h, v)


def emit_load_x(p, env):
    x = p.inp("x", [C, T], F32)
    xv = x.rearrange("(cc p) t -> cc p t", p=128)
    x_fm = env["x_fm"]

    def sec(s):
        for cc in range(NCC):
            p.sS.inc(s.dma_start(out=x_fm[:, cc * T:(cc + 1) * T], in_=xv[cc]), 16)
    p.SYN(sec)


def emit_store_x(p, env):
    xout = p.nc.dram_tensor("xout", [C, T], F32, kind="ExternalOutput").ap()
    xov = xout.rearrange("(cc p) t -> cc p t", p=128)
    x_fm = env["x_fm"]
    bar = p.snap()

    def sec(s):
        p.wait_snap(s, bar)
        for cc in range(NCC):
            p.sO.inc(s.dma_start(out=xov[cc], in_=x_fm[:, cc * T:(cc + 1) * T]), 16)
    p.SYN(sec)


def emit_final_wait(p):
    n = p.sO.n

    def sec(s):
        s.wait_ge(p.sO.h, n)
    p.SYN(sec)


def emit_ln(p, env, tag):
    x_fm = env["x_fm"]
    ones_col = env["ones_col"]
    ones_row = env["ones_row"]
    xn_scr_v = env["xn_scr_v"]
    bar = p.snap()

    p.push()
    xbf = [p.sb(f"ln{tag}_xbf{i}", [128, TB], BF16) for i in range(3)]
    sq = [p.sb(f"ln{tag}_sq{i}", [128, TB], BF16) for i in range(3)]
    rows = p.sb(f"ln{tag}_rows", [1, TB * 6], F32)
    mean_f, m2_f, var_f, sd_f, rstd_f, lo_f = (rows[:, i * TB:(i + 1) * TB] for i in range(6))
    mean_bf = p.sb(f"ln{tag}_meanbf", [1, TB], BF16)
    rstd_bf = p.sb(f"ln{tag}_rstdbf", [1, TB], BF16)
    mlo_bf = p.sb(f"ln{tag}_mlobf", [1, TB], BF16)
    rlo_bf = p.sb(f"ln{tag}_rlobf", [1, TB], BF16)
    xn_t = [p.sb(f"ln{tag}_xnt{i}", [128, NCC * TB], BF16) for i in range(2)]
    tsub = p.sb(f"ln{tag}_tsub", [128, TB], F32)
    psS = p.ps(f"ln{tag}_psS", [1, TB])
    psQ = p.ps(f"ln{tag}_psQ", [1, TB])
    psM = p.ps(f"ln{tag}_psM", [128, TB])
    psR = p.ps(f"ln{tag}_psR", [128, TB])

    stat_mark = {}
    mean_done = {}
    var_done = {}
    rows_done = {}
    bcast_done = {}
    norm_done = {}
    store_done = {}

    pb0 = p.sP.n
    pred_stat = lambda j: pb0 + (j // NCC) * 9 + (j % NCC) + 1
    cast_marks = {}

    for tb in range(NTB):
        def a_cast(s, tb=tb):
            for cc in range(NCC):
                idx = tb * NCC + cc
                if idx == 0:
                    p.wait_snap(s, bar)
                if idx >= 3:
                    s.wait_ge(p.sP.h, pred_stat(idx - 3))
                srcv = x_fm[:, cc * T + tb * TB: cc * T + tb * TB + TB]
                b = idx % 3
                s.activation(xbf[b][:], srcv, AF.Copy)
                cast_marks[idx] = p.sA.inc(s.activation(sq[b][:], srcv, AF.Square))
        p.SCA(a_cast)

        for cc in range(NCC):
            idx = tb * NCC + cc

            def t_sec(t, tb=tb, cc=cc, idx=idx):
                t.wait_ge(p.sA.h, cast_marks[idx])
                if cc == 0 and tb > 0:
                    t.wait_ge(p.sA.h, mean_done[tb - 1])
                    t.wait_ge(p.sV.h, var_done[tb - 1])
                b = idx % 3
                t.matmul(psS[:], ones_col[:], xbf[b][:],
                         start=(cc == 0), stop=(cc == NCC - 1),
                         skip_group_check=True)
                mm = t.matmul(psQ[:], ones_col[:], sq[b][:],
                              start=(cc == 0), stop=(cc == NCC - 1),
                              skip_group_check=True)
                stat_mark[idx] = p.sP.inc(mm)
            p.TEN(t_sec)

        sn = stat_mark[tb * NCC + NCC - 1]

        def a2_sec(s, tb=tb, sn=sn):
            s.wait_ge(p.sP.h, sn)
            mean_done[tb] = p.sA.inc(s.activation(mean_f, psS[:], AF.Copy, scale=1.0 / C))
        p.SCA(a2_sec)
        a2n = p.sA.n

        def v_sec(v, tb=tb, a2n=a2n, sn=sn):
            v.wait_ge(p.sA.h, a2n)
            v.wait_ge(p.sP.h, sn)
            v.tensor_tensor(m2_f, mean_f, mean_f, ALU.mult)
            var_done[tb] = p.sV.inc(
                v.scalar_tensor_tensor(var_f, psQ[:], 1.0 / C, m2_f,
                                       ALU.mult, ALU.subtract))
        p.VEC(v_sec)
        vn = p.sV.n

        def a3_sec(s, tb=tb, vn=vn):
            s.wait_ge(p.sV.h, vn)
            p.sA.inc(s.activation(sd_f, var_f, AF.Sqrt, bias=env["eps_t"][:]))
        p.SCA(a3_sec)
        a3n = p.sA.n

        def v2_sec(v, tb=tb, a3n=a3n):
            v.wait_ge(p.sA.h, a3n)
            v.reciprocal(rstd_f, sd_f)
            v.tensor_copy(mean_bf[:], mean_f)
            v.tensor_copy(rstd_bf[:], rstd_f)
            v.tensor_tensor(lo_f, mean_f, mean_bf[:], ALU.subtract)
            v.tensor_copy(mlo_bf[:], lo_f)
            v.tensor_tensor(lo_f, rstd_f, rstd_bf[:], ALU.subtract)
            rows_done[tb] = p.sV.inc(v.tensor_copy(rlo_bf[:], lo_f))
        p.VEC(v2_sec)

        def t2_sec(t, tb=tb):
            t.wait_ge(p.sV.h, rows_done[tb])
            if tb > 0:
                t.wait_ge(p.sV.h, norm_done[(tb - 1, NCC - 1)])
            t.matmul(psM[:], ones_row[:], mean_bf[:], start=True, stop=False,
                     skip_group_check=True)
            t.matmul(psM[:], ones_row[:], mlo_bf[:], start=False, stop=True,
                     skip_group_check=True)
            t.matmul(psR[:], ones_row[:], rstd_bf[:], start=True, stop=False,
                     skip_group_check=True)
            bcast_done[tb] = p.sP.inc(
                t.matmul(psR[:], ones_row[:], rlo_bf[:], start=False, stop=True,
                         skip_group_check=True))
        p.TEN(t2_sec)

        for cc in range(NCC):
            def v3_sec(v, tb=tb, cc=cc):
                if cc == 0:
                    v.wait_ge(p.sP.h, bcast_done[tb])
                    if tb >= 2:
                        v.wait_ge(p.sS.h, store_done[tb - 2])
                src = x_fm[:, cc * T + tb * TB: cc * T + tb * TB + TB]
                v.tensor_tensor(tsub[:], src, psM[:], ALU.subtract)
                norm_done[(tb, cc)] = p.sV.inc(
                    v.tensor_tensor(xn_t[tb % 2][:, cc * TB:(cc + 1) * TB],
                                    tsub[:], psR[:], ALU.mult))
            p.VEC(v3_sec)

        def s_sec(s, tb=tb):
            s.wait_ge(p.sV.h, norm_done[(tb, NCC - 1)])
            store_done[tb] = p.sS.inc(
                s.dma_start(out=env["xn_scr_v"][:, :, tb * TB:(tb + 1) * TB],
                            in_=xn_t[tb % 2].rearrange("p (cc t) -> p cc t", cc=NCC)), 16)
        p.SYN(s_sec)

    env["xn_ready"] = dict(store_done)
    p.pop()


def emit_l0_attn(p, env):
    nc = p.nc
    x_fm = env["x_fm"]
    xn_scr_v = env["xn_scr_v"]
    ident, pswap = env["ident"], env["pswap"]
    xn_ready = env["xn_ready"]
    m_d = p.inp("cmask", [128, 4 * TB], BF16)

    wts = {nm: p.inp(f"l0{nm}", [C, C], F32) for nm in ("q1", "k1", "q2", "k2", "v", "o")}
    wview = {nm: a.rearrange("(cc p) o -> p cc o", p=128) for nm, a in wts.items()}
    ctab_d = p.inp("rope_c", [128, T], BF16)
    stab_d = p.inp("rope_s", [128, T], BF16)

    y_scr = nc.dram_tensor("y_scr", [C, T], BF16, kind="Internal").ap()
    y_scr_g = y_scr.rearrange("(g p) t -> g p t", p=128)
    y_scr_c = y_scr.rearrange("(cc p) t -> p cc t", p=128)
    q_scr = {nm: nc.dram_tensor(f"q_scr_{nm}", [128, T], BF16, kind="Internal").ap()
             for nm in ("q1", "q2")}

    names = ("q1", "k1", "q2", "k2", "v")

    p.push()
    masks = p.sb("l0a_masks", [128, 4 * TB], BF16)
    kk = {nm: p.sb(f"l0a_{nm}", [128, T], BF16) for nm in ("k1", "k2")}
    v_tm = p.sb("l0a_vtm", [128, NKT * 128], BF16)
    wbuf = {nm: p.sb(f"l0a_w_{nm}", [128, NCC * 128], BF16) for nm in names}
    xn_t = [p.sb(f"l0a_xnt{i}", [128, NCC * TB], BF16) for i in range(2)]
    cbuf = p.sb("l0a_cbuf", [128, TB], BF16)
    sbuf_t = p.sb("l0a_sbuf", [128, TB], BF16)
    ev = [p.sb(f"l0a_ev{i}", [128, TB], BF16) for i in range(2)]
    rope_t1 = p.sb("l0a_rt1", [128, TB], BF16)
    rope_t2 = p.sb("l0a_rt2", [128, TB], BF16)
    qob = [p.sb(f"l0a_qob{i}", [128, TB], BF16) for i in range(2)]
    qw_t = {nm: p.sb(f"l0a_qw_{nm}", [128, TB], BF16) for nm in ("q1", "q2")}
    s1m = [p.sb(f"l0a_s1m{i}", [128, TB], BF16) for i in range(4)]
    pat = [p.sb(f"l0a_pat{i}", [128, TB], BF16) for i in range(4)]
    ybuf = p.sb("l0a_ybuf", [128, TB], BF16)
    psyA = p.ps("l0a_psyA", [128, TB])
    psyB = p.ps("l0a_psyB", [128, TB])

    mk = {
        "ev_free": [0, 0],
        "ev_free_eng": ["V", "V"],
        "psp_free": [0, 0],
        "rope_dve": 0,
        "rope_tb": 0,
        "vtr_ev": 0,
        "qob_st": [0, 0],
        "s1ev_a": 0, "s1ev_v": 0,
        "patdve": 0,
        "av": 0,
        "yev": 0,
        "yst": 0,
        "qw_used": 0,
    }
    bar = p.snap()

    proj_tb_done = {}
    proj_g_done = {}
    q_store = {}

    for g in range(NCC):
        p.push()
        psp = [p.ps(f"l0a_psp_g{g}", [128, TB])]
        pssw = p.ps(f"l0a_pssw_g{g}", [128, TB])
        pstr = p.ps(f"l0a_pstr_g{g}", [128, 128], BF16)
        prev_pat_g = mk["patdve"]
        prev_yev_g = mk["yev"]

        def gw_sec(gq, g=g):
            if g == 0:
                p.wait_snap(gq, bar)
                p.sG.inc(gq.dma_start(out=masks[:], in_=m_d), 16)
            if g >= 1:
                gq.wait_ge(p.sP.h, proj_g_done[g - 1])
            for nm in names:
                p.sG.inc(gq.dma_start(out=wbuf[nm][:],
                                      in_=wview[nm][:, :, g * 128:(g + 1) * 128]), 16)
        p.GPS(gw_sec)
        wload_n = p.sG.n

        for tb in range(NTB):
            def s_sec(s, g=g, tb=tb):
                if g == 0 and tb == 0:
                    p.wait_snap(s, bar)
                if g == 0:
                    s.wait_ge(p.sS.h, xn_ready[tb])
                key = (g, tb - 2) if tb >= 2 else (g - 1, NTB - 2 + tb)
                if key in proj_tb_done:
                    s.wait_ge(p.sP.h, proj_tb_done[key])
                p.sS.inc(s.dma_start(
                    out=xn_t[tb % 2].rearrange("p (cc t) -> p cc t", cc=NCC),
                    in_=xn_scr_v[:, :, tb * TB:(tb + 1) * TB]), 16)
                if mk["rope_tb"]:
                    s.wait_ge(p.sV.h, mk["rope_tb"])
                p.sS.inc(s.dma_start(out=cbuf[:], in_=ctab_d[:, tb * TB:(tb + 1) * TB]), 16)
                p.sS.inc(s.dma_start(out=sbuf_t[:], in_=stab_d[:, tb * TB:(tb + 1) * TB]), 16)
            p.SYN(s_sec)
            xn_n = p.sS.n

            for pi, nm in enumerate(names):
                def t_sec(t, g=g, tb=tb, pi=pi, nm=nm, xn_n=xn_n, wload_n=wload_n,
                          prev_pat_g=prev_pat_g, prev_yev_g=prev_yev_g):
                    if pi == 0:
                        t.wait_ge(p.sS.h, xn_n)
                        t.wait_ge(p.sG.h, wload_n)
                    if g > 0 and tb == 0 and pi == 0:
                        if prev_pat_g:
                            t.wait_ge(p.sV.h, prev_pat_g)
                        if prev_yev_g:
                            t.wait_ge(p.sA.h, prev_yev_g)
                    if mk["psp_free"][0]:
                        t.wait_ge(p.sA.h, mk["psp_free"][0])
                    for cc in range(NCC):
                        mm = t.matmul(psp[0][:], wbuf[nm][:, cc * 128:(cc + 1) * 128],
                                      xn_t[tb % 2][:, cc * TB:(cc + 1) * TB],
                                      start=(cc == 0), stop=(cc == NCC - 1))
                    p.sP.inc(mm)
                p.TEN(t_sec)
                mm_n = p.sP.n

                def a_sec(s, pi=pi, mm_n=mm_n):
                    s.wait_ge(p.sP.h, mm_n)
                    eng, cnt = mk["ev_free_eng"][pi % 2], mk["ev_free"][pi % 2]
                    if cnt:
                        s.wait_ge((p.sV if eng == "V" else p.sP).h, cnt)
                    mk["psp_free"][0] = p.sA.inc(
                        s.activation(ev[pi % 2][:], psp[0][:], AF.Copy))
                p.SCA(a_sec)
                ev_n = p.sA.n

                if nm == "v":
                    for j in range(4):
                        def t3_sec(t, j=j, ev_n=ev_n, pi=pi):
                            if j == 0:
                                t.wait_ge(p.sA.h, ev_n)
                            if mk["vtr_ev"]:
                                t.wait_ge(p.sV.h, mk["vtr_ev"])
                            p.sP.inc(t.transpose(pstr[:],
                                                 ev[pi % 2][:, j * 128:(j + 1) * 128],
                                                 ident[:]))
                        p.TEN(t3_sec)
                        tr_n = p.sP.n

                        def v_sec(v, tb=tb, j=j, tr_n=tr_n):
                            v.wait_ge(p.sP.h, tr_n)
                            kt = tb * 4 + j
                            mk["vtr_ev"] = p.sV.inc(
                                v.tensor_copy(v_tm[:, kt * 128:(kt + 1) * 128], pstr[:]))
                        p.VEC(v_sec)
                    mk["ev_free"][pi % 2] = mk["vtr_ev"]
                    mk["ev_free_eng"][pi % 2] = "V"
                else:
                    def t2_sec(t, ev_n=ev_n, pi=pi):
                        t.wait_ge(p.sA.h, ev_n)
                        if mk["rope_dve"]:
                            t.wait_ge(p.sV.h, mk["rope_dve"])
                        p.sP.inc(t.matmul(pssw[:], pswap[:], ev[pi % 2][:],
                                          start=True, stop=True))
                    p.TEN(t2_sec)
                    sw_n = p.sP.n

                    is_q = nm in ("q1", "q2")
                    qi = 0 if nm == "q1" else 1

                    def v_sec(v, tb=tb, nm=nm, pi=pi, sw_n=sw_n, is_q=is_q, qi=qi):
                        v.wait_ge(p.sP.h, sw_n)
                        if is_q and mk["qob_st"][qi]:
                            v.wait_ge(p.sS.h, mk["qob_st"][qi])
                        cs = slice(tb * TB, (tb + 1) * TB)
                        v.tensor_tensor(rope_t1[:], ev[pi % 2][:], cbuf[:], ALU.mult)
                        v.tensor_tensor(rope_t2[:], pssw[:], sbuf_t[:], ALU.mult)
                        dst = qob[qi][:] if is_q else kk[nm][:, cs]
                        mk["rope_dve"] = p.sV.inc(
                            v.tensor_tensor(dst, rope_t1[:], rope_t2[:], ALU.add))
                    p.VEC(v_sec)
                    mk["ev_free"][pi % 2] = mk["rope_dve"]
                    mk["ev_free_eng"][pi % 2] = "V"

                    if is_q:
                        rn = p.sV.n

                        def sq_sec(s, g=g, tb=tb, nm=nm, qi=qi, rn=rn):
                            s.wait_ge(p.sV.h, rn)
                            mk["qob_st"][qi] = p.sS.inc(
                                s.dma_start(out=q_scr[nm][:, tb * TB:(tb + 1) * TB],
                                            in_=qob[qi][:]), 16)
                        p.SYN(sq_sec)
                        q_store[(g, nm, tb)] = p.sS.n
            mk["rope_tb"] = mk["rope_dve"]
            proj_tb_done[(g, tb)] = p.sP.n
        proj_g_done[g] = p.sP.n
        dve_fin = p.sV.n
        psp_ev_fin = mk["psp_free"][0]
        p.pop()
        p.push()
        pss = [p.ps(f"l0a_pss_g{g}_{i}", [128, TB]) for i in range(4)]

        for qw in range(NTB):
            def sw_sec(s, g=g, qw=qw):
                s.wait_ge(p.sS.h, q_store[(g, "q1", qw)])
                s.wait_ge(p.sS.h, q_store[(g, "q2", qw)])
                if mk["qw_used"]:
                    s.wait_ge(p.sP.h, mk["qw_used"])
                p.sS.inc(s.dma_start(out=qw_t["q1"][:],
                                     in_=q_scr["q1"][:, qw * TB:(qw + 1) * TB]), 16)
                p.sS.inc(s.dma_start(out=qw_t["q2"][:],
                                     in_=q_scr["q2"][:, qw * TB:(qw + 1) * TB]), 16)
            p.SYN(sw_sec)
            qw_n = p.sS.n

            nkt = 4 * qw + 4
            ndiag = 4
            noff = nkt - ndiag
            wpb = p.sP.n
            wvb = p.sV.n
            wab = p.sA.n
            patdve_pred = lambda kt: (wvb + kt + 1 if kt < noff
                                      else wvb + noff + 2 * (kt - noff) + 2)
            prev_patdve = mk["patdve"]
            prev_yst = mk["yst"]
            prev_yev = mk["yev"]

            def a_win(s, g=g, qw=qw, wpb=wpb, noff=noff, nkt=nkt,
                      prev_patdve=prev_patdve, prev_yst=prev_yst):
                for kt in range(noff):
                    pr = kt % 2
                    s.wait_ge(p.sP.h, wpb + 3 * kt + 1)
                    if kt < 2:
                        if prev_patdve:
                            s.wait_ge(p.sV.h, prev_patdve)
                    else:
                        s.wait_ge(p.sV.h, patdve_pred(kt - 2))
                    s.activation(s1m[2 * pr][:], pss[2 * pr][:], AF.Copy,
                                 scale=1.0 / 4096.0)
                    p.sA.inc(s.activation(s1m[2 * pr + 1][:], pss[2 * pr + 1][:],
                                          AF.Copy, scale=1.0 / 4096.0))
                s.wait_ge(p.sP.h, wpb + 3 * nkt)
                if prev_yst:
                    s.wait_ge(p.sS.h, prev_yst)
                s.activation(ybuf[0:64, :], psyA[0:64, :], AF.Copy)
                p.sA.inc(s.activation(ybuf[64:128, :], psyB[64:128, :], AF.Copy))
            p.SCA(a_win)
            mk["s1ev_a"] = wab + noff
            mk["yev"] = wab + noff + 1
            yev_n = mk["yev"]

            maskev_pred = lambda kt: patdve_pred(kt) - 1
            prev_av = mk["av"]

            def t_win(t, g=g, qw=qw, nkt=nkt, noff=noff, wpb=wpb, wab=wab,
                      dve_fin=dve_fin, qw_n=qw_n, prev_patdve=prev_patdve,
                      prev_yev=prev_yev, psp_ev_fin=psp_ev_fin):
                for kt in range(nkt):
                    pr = kt % 2
                    diag_r = kt - 4 * qw
                    ks = slice(kt * 128, (kt + 1) * 128)
                    if qw == 0 and kt == 0:
                        t.wait_ge(p.sV.h, dve_fin)
                        if psp_ev_fin:
                            t.wait_ge(p.sA.h, psp_ev_fin)
                    if kt == 0:
                        t.wait_ge(p.sS.h, qw_n)
                    if kt < 2:
                        if prev_patdve:
                            t.wait_ge(p.sV.h, prev_patdve)
                    else:
                        t.wait_ge(p.sV.h, patdve_pred(kt - 2))
                    t.matmul(pss[2 * pr][:], kk["k1"][0:64, ks], qw_t["q1"][0:64, :],
                             start=True, stop=True, tile_position=(0, 0))
                    mm = t.matmul(pss[2 * pr + 1][:], kk["k1"][64:128, ks],
                                  qw_t["q1"][64:128, :],
                                  start=True, stop=True, tile_position=(64, 0))
                    p.sP.inc(mm)
                    if diag_r < 0:
                        t.wait_ge(p.sA.h, wab + kt + 1)
                    else:
                        t.wait_ge(p.sV.h, maskev_pred(kt))
                    t.matmul(pss[2 * pr][:], kk["k2"][0:64, ks], qw_t["q2"][0:64, :],
                             start=True, stop=True, tile_position=(0, 0))
                    mm = t.matmul(pss[2 * pr + 1][:], kk["k2"][64:128, ks],
                                  qw_t["q2"][64:128, :],
                                  start=True, stop=True, tile_position=(64, 0))
                    mk["qw_used"] = p.sP.inc(mm)
                    t.wait_ge(p.sV.h, patdve_pred(kt))
                    if kt == 0 and prev_yev:
                        t.wait_ge(p.sA.h, prev_yev)
                    t.matmul(psyA[0:64, :], v_tm[:, kt * 128: kt * 128 + 64],
                             pat[2 * pr][:],
                             start=(kt == 0), stop=(kt == nkt - 1), tile_position=(0, 0),
                             skip_group_check=True)
                    mm = t.matmul(psyB[64:128, :],
                                  v_tm[:, kt * 128 + 64: kt * 128 + 128],
                                  pat[2 * pr + 1][:], start=(kt == 0),
                                  stop=(kt == nkt - 1),
                                  tile_position=(0, 64), skip_group_check=True)
                    mk["av"] = p.sP.inc(mm)
            p.TEN(t_win)

            def v_win(v, g=g, qw=qw, nkt=nkt, noff=noff, wpb=wpb,
                      prev_patdve=prev_patdve, prev_av=prev_av):
                for kt in range(nkt):
                    pr = kt % 2
                    diag_r = kt - 4 * qw
                    if diag_r >= 0:
                        v.wait_ge(p.sP.h, wpb + 3 * kt + 1)
                        if kt < 2 and prev_patdve:
                            v.wait_ge(p.sV.h, prev_patdve)
                        ms = masks[:, diag_r * TB:(diag_r + 1) * TB]
                        v.tensor_tensor(s1m[2 * pr][:], pss[2 * pr][:], ms, ALU.mult)
                        p.sV.inc(v.tensor_tensor(s1m[2 * pr + 1][:],
                                                 pss[2 * pr + 1][:], ms, ALU.mult))
                    v.wait_ge(p.sP.h, wpb + 3 * kt + 2)
                    if kt >= 2:
                        v.wait_ge(p.sP.h, wpb + 3 * (kt - 2) + 3)
                    elif prev_av > 0:
                        v.wait_ge(p.sP.h, prev_av)
                    v.tensor_tensor(pat[2 * pr][:], pss[2 * pr][:], s1m[2 * pr][:],
                                    ALU.mult)
                    mk["patdve"] = p.sV.inc(
                        v.tensor_tensor(pat[2 * pr + 1][:], pss[2 * pr + 1][:],
                                        s1m[2 * pr + 1][:], ALU.mult))
            p.VEC(v_win)
            y_n = p.sP.n

            def s2_sec(s, g=g, qw=qw, yev_n=yev_n):
                s.wait_ge(p.sA.h, yev_n)
                mk["yst"] = p.sS.inc(
                    s.dma_start(out=y_scr_g[g][:, qw * TB:(qw + 1) * TB], in_=ybuf[:]), 16)
            p.SYN(s2_sec)
        p.pop()

    p.pop()
    p.push()
    y_all = p.sS.n
    wout_bar = p.snap()
    wo_buf = [p.sb(f"l0a_wo{i}", [128, NCC * 128], BF16) for i in range(2)]
    y_t = [p.sb(f"l0a_yt{i}", [128, NCC * TB], BF16) for i in range(2)]
    psp2 = [p.ps(f"l0a_psp2_{i}", [128, TB]) for i in range(2)]
    wo_mm = {}
    res_add = {}
    wo_used = {}
    for oc in range(NCC):
        def g3_sec(gq, oc=oc):
            if oc == 0:
                p.wait_snap(gq, wout_bar)
            if oc >= 2:
                gq.wait_ge(p.sP.h, wo_used[oc - 2])
            p.sG.inc(gq.dma_start(out=wo_buf[oc % 2][:],
                                  in_=wview["o"][:, :, oc * 128:(oc + 1) * 128]), 16)
        p.GPS(g3_sec)
        won = p.sG.n
        for tb in range(NTB):
            idx = oc * NTB + tb

            def s_sec(s, oc=oc, tb=tb, idx=idx):
                if idx == 0:
                    p.wait_snap(s, wout_bar)
                    s.wait_ge(p.sS.h, y_all)
                if idx >= 2:
                    s.wait_ge(p.sP.h, wo_mm[idx - 2])
                p.sS.inc(s.dma_start(
                    out=y_t[tb % 2].rearrange("p (cc t) -> p cc t", cc=NCC),
                    in_=y_scr_c[:, :, tb * TB:(tb + 1) * TB]), 16)
            p.SYN(s_sec)
            yt_n = p.sS.n

            def t_sec(t, oc=oc, tb=tb, idx=idx, yt_n=yt_n, won=won):
                t.wait_ge(p.sS.h, yt_n)
                t.wait_ge(p.sG.h, won)
                if idx < 2:
                    p.wait_snap(t, wout_bar)
                else:
                    t.wait_ge(p.sV.h, res_add[idx - 2])
                for cc in range(NCC):
                    mm = t.matmul(psp2[idx % 2][:], wo_buf[oc % 2][:, cc * 128:(cc + 1) * 128],
                                  y_t[tb % 2][:, cc * TB:(cc + 1) * TB],
                                  start=(cc == 0), stop=(cc == NCC - 1))
                wo_mm[idx] = p.sP.inc(mm)
            p.TEN(t_sec)
            mm_n = p.sP.n

            def v_sec(v, oc=oc, tb=tb, idx=idx, mm_n=mm_n):
                v.wait_ge(p.sP.h, mm_n)
                xs = x_fm[:, oc * T + tb * TB: oc * T + tb * TB + TB]
                res_add[idx] = p.sV.inc(
                    v.tensor_tensor(xs, xs, psp2[idx % 2][:], ALU.add))
            p.VEC(v_sec)
        wo_used[oc] = p.sP.n
    p.pop()


def emit_l0_ffn(p, env):
    x_fm = env["x_fm"]
    xn_scr_v = env["xn_scr_v"]
    xn_ready = env["xn_ready"]

    wr_d = p.inp("l0fr", [C, C], F32).rearrange("(cc p) o -> p cc o", p=128)
    wk_d = p.inp("l0fk", [C, C], F32).rearrange("(cc p) o -> p cc o", p=128)
    wv_d = p.inp("l0fv", [C, C], F32).rearrange("(cc p) o -> p cc o", p=128)
    mur_d = p.inp("l0f_mur", [128, NCC], F32)
    muk_d = p.inp("l0f_muk", [128, NCC], F32)

    bar = p.snap()
    p.push()
    wr = p.sb("l0f_wr", [128, NCC * C], BF16)
    wk = p.sb("l0f_wk", [128, NCC * C], BF16)
    wvb = [p.sb(f"l0f_wv{i}", [128, NCC * 128], BF16) for i in range(2)]
    mur = p.sb("l0f_mur_t", [128, NCC], F32)
    muk = p.sb("l0f_muk_t", [128, NCC], F32)
    xbuf = p.sb("l0f_xbuf", [128, NCC * (TB + 1)], BF16)
    mix1 = p.sb("l0f_mix1", [128, NCC * TB], BF16)
    mix2 = p.sb("l0f_mix2", [128, NCC * TB], BF16)
    r_t = p.sb("l0f_rt", [128, NCC * TB], BF16)
    dbuf = [p.sb("l0f_d0", [128, TB], F32)]
    tbuf = [p.sb(f"l0f_t{i}", [128, TB], F32) for i in range(2)]
    krelu = p.sb("l0f_krelu", [128, TB], BF16)
    obuf = p.sb("l0f_obuf", [128, TB], F32)
    psm = [p.ps(f"l0f_ps{i}", [128, TB]) for i in range(2)]

    xb_v = xbuf.rearrange("p (cc j) -> p cc j", j=TB + 1)

    def w_sec(gq):
        p.wait_snap(gq, bar)
        p.sG.inc(gq.dma_start(out=wr.rearrange("p (cc o) -> p cc o", cc=NCC), in_=wr_d), 16)
        p.sG.inc(gq.dma_start(out=wk.rearrange("p (cc o) -> p cc o", cc=NCC), in_=wk_d), 16)
    p.GPS(w_sec)
    wres_n = p.sG.n

    def mu_sec(s):
        p.wait_snap(s, bar)
        p.sS.inc(s.dma_start(out=mur[:], in_=mur_d), 16)
        p.sS.inc(s.dma_start(out=muk[:], in_=muk_d), 16)
    p.SYN(mu_sec)
    mu_n = p.sS.n

    mk = {"ev": {}, "kv_done": {}, "wv_used": {}, "mix_done": {}, "resadd": {}}
    prev_tb_kv = 0
    prev_tb_mm = 0

    for tb in range(NTB):
        def s_sec(s, tb=tb, prev=prev_tb_kv):
            if tb == 0:
                p.wait_snap(s, bar)
                s.wait_ge(p.sS.h, mu_n)
            s.wait_ge(p.sS.h, xn_ready[tb])
            if prev:
                s.wait_ge(p.sV.h, prev)
            if tb == 0:
                p.sS.inc(s.dma_start(out=xb_v[:, :, 1:TB + 1],
                                     in_=xn_scr_v[:, :, 0:TB]), 16)
            else:
                p.sS.inc(s.dma_start(out=xb_v[:, :, 0:TB + 1],
                                     in_=xn_scr_v[:, :, tb * TB - 1: (tb + 1) * TB]), 16)
        p.SYN(s_sec)
        xb_n = p.sS.n

        if tb == 0:
            def z_sec(v):
                v.wait_ge(p.sS.h, xb_n)
                p.sV.inc(v.memset(xb_v[:, :, 0:1], 0.0))
            p.VEC(z_sec)
        zero_n = p.sV.n

        mix_marks = []
        amarks = {}
        for cc in range(NCC):
            def v_sec(v, tb=tb, cc=cc, xb_n=xb_n, zero_n=zero_n):
                if cc == 0:
                    v.wait_ge(p.sS.h, xb_n)
                    if tb == 0:
                        v.wait_ge(p.sV.h, zero_n)
                    if tb >= 1:
                        v.wait_ge(p.sP.h, prev_tb_mm)
                if (cc - 1) in amarks:
                    v.wait_ge(p.sA.h, amarks[cc - 1])
                cur = xb_v[:, cc, 1:TB + 1]
                prv = xb_v[:, cc, 0:TB]
                p.sV.inc(v.tensor_tensor(dbuf[0][:], cur, prv, ALU.subtract))
            p.VEC(v_sec)
            d_n = p.sV.n

            def a_sec(s, cc=cc, d_n=d_n):
                s.wait_ge(p.sV.h, d_n)
                s.activation(tbuf[0][:], dbuf[0][:], AF.Copy,
                             scale=mur[:, cc:cc + 1])
                amarks[cc] = p.sA.inc(s.activation(tbuf[1][:], dbuf[0][:], AF.Copy,
                                      scale=muk[:, cc:cc + 1]))
            p.SCA(a_sec)
            t_n = p.sA.n

            def v2_sec(v, cc=cc, t_n=t_n):
                v.wait_ge(p.sA.h, t_n)
                cs = slice(cc * TB, (cc + 1) * TB)
                prv = xb_v[:, cc, 0:TB]
                v.tensor_tensor(mix1[:, cs], tbuf[0][:], prv, ALU.add)
                p.sV.inc(v.tensor_tensor(mix2[:, cs], tbuf[1][:], prv, ALU.add))
            p.VEC(v2_sec)
            mix_marks.append(p.sV.n)
        mix_n = p.sV.n

        ev_mark = {}
        for oc in range(NCC):
            def t_sec(t, tb=tb, oc=oc, mix_n=mix_n):
                if oc == 0:
                    t.wait_ge(p.sV.h, mix_n)
                    if tb == 0:
                        t.wait_ge(p.sG.h, wres_n)
                if (oc - 2) in ev_mark:
                    t.wait_ge(p.sA.h, ev_mark[oc - 2])
                for cc in range(NCC):
                    mm = t.matmul(psm[oc % 2][:], wr[:, cc * C + oc * 128: cc * C + oc * 128 + 128],
                                  mix1[:, cc * TB:(cc + 1) * TB],
                                  start=(cc == 0), stop=(cc == NCC - 1))
                p.sP.inc(mm)
            p.TEN(t_sec)
            rmm_n = p.sP.n

            def a_sec(s, oc=oc, rmm_n=rmm_n):
                s.wait_ge(p.sP.h, rmm_n)
                ev_mark[oc] = p.sA.inc(
                    s.activation(r_t[:, oc * TB:(oc + 1) * TB], psm[oc % 2][:], AF.Sigmoid))
            p.SCA(a_sec)
        r_mm_all = p.sP.n

        k_ev = {}
        r_last_ev = max(ev_mark.values())
        for oc in range(NCC):
            def t_sec(t, oc=oc):
                if oc < 2:
                    t.wait_ge(p.sA.h, r_last_ev)
                if (oc - 2) in k_ev:
                    t.wait_ge(p.sA.h, k_ev[oc - 2])
                for cc in range(NCC):
                    mm = t.matmul(psm[oc % 2][:], wk[:, cc * C + oc * 128: cc * C + oc * 128 + 128],
                                  mix2[:, cc * TB:(cc + 1) * TB],
                                  start=(cc == 0), stop=(cc == NCC - 1))
                p.sP.inc(mm)
            p.TEN(t_sec)
            kmm_n = p.sP.n

            def a_sec(s, oc=oc, kmm_n=kmm_n):
                s.wait_ge(p.sP.h, kmm_n)
                k_ev[oc] = p.sA.inc(s.activation(krelu[:], psm[oc % 2][:], AF.Relu))
            p.SCA(a_sec)
            kr_n = p.sA.n

            def v_sec(v, oc=oc, kr_n=kr_n):
                v.wait_ge(p.sA.h, kr_n)
                v.wait_ge(p.sP.h, r_mm_all)
                p.sV.inc(v.tensor_tensor(mix1[:, oc * TB:(oc + 1) * TB],
                                         krelu[:], krelu[:], ALU.mult))
            p.VEC(v_sec)
        kact_n = p.sV.n

        kv_mm = {}
        for oc in range(NCC):
            def g_sec(gq, tb=tb, oc=oc):
                if (tb * NCC + oc) >= 2:
                    gq.wait_ge(p.sP.h, mk["wv_used"][(tb * NCC + oc) - 2])
                elif tb == 0 and oc == 0:
                    p.wait_snap(gq, bar)
                p.sG.inc(gq.dma_start(out=wvb[oc % 2].rearrange("p (cc o) -> p cc o", cc=NCC),
                                      in_=wv_d[:, :, oc * 128:(oc + 1) * 128]), 16)
            p.GPS(g_sec)
            wv_n = p.sG.n

            def t_sec(t, tb=tb, oc=oc, wv_n=wv_n, kact_n=kact_n):
                if oc == 0:
                    t.wait_ge(p.sV.h, kact_n)
                t.wait_ge(p.sG.h, wv_n)
                if (oc - 2) in kv_mm:
                    t.wait_ge(p.sV.h, mk["resadd"][(tb, oc - 2)])
                for cc in range(NCC):
                    mm = t.matmul(psm[oc % 2][:], wvb[oc % 2][:, cc * 128:(cc + 1) * 128],
                                  mix1[:, cc * TB:(cc + 1) * TB],
                                  start=(cc == 0), stop=(cc == NCC - 1))
                kv_mm[oc] = p.sP.inc(mm)
                mk["wv_used"][tb * NCC + oc] = kv_mm[oc]
            p.TEN(t_sec)
            kvm_n = p.sP.n

            def v_sec(v, tb=tb, oc=oc, kvm_n=kvm_n):
                v.wait_ge(p.sP.h, kvm_n)
                v.tensor_tensor(obuf[:], psm[oc % 2][:], r_t[:, oc * TB:(oc + 1) * TB],
                                ALU.mult)
                xs = x_fm[:, oc * T + tb * TB: oc * T + tb * TB + TB]
                mk["resadd"][(tb, oc)] = p.sV.inc(
                    v.tensor_tensor(xs, xs, obuf[:], ALU.add))
            p.VEC(v_sec)
        prev_tb_kv = p.sV.n
        prev_tb_mm = p.sP.n
    p.pop()


def emit_l1_attn(p, env):
    x_fm = env["x_fm"]
    xn_scr_v = env["xn_scr_v"]
    xn_ready = env["xn_ready"]
    ident = env["ident"]
    idsh = env["idsh"]
    ones_col = env["ones_col"]
    ones_row = env["ones_row"]

    wq_d = p.inp("l1q", [C, C], F32).rearrange("(cc p) o -> p cc o", p=128)
    wk_d = p.inp("l1k", [C, C], F32).rearrange("(cc p) o -> p cc o", p=128)
    wv_d = p.inp("l1v", [C, C], F32).rearrange("(cc p) o -> p cc o", p=128)
    wview = {"q": wq_d, "k": wk_d, "v": wv_d}

    bar = p.snap()
    p.push()
    qf = p.sb("l1a_qf", [128, T], BF16)
    kf = p.sb("l1a_kf", [128, T], BF16)
    v_tm = p.sb("l1a_vtm", [128, NKT * 130], BF16)
    v_tm_v = v_tm.rearrange("p (tt c) -> p tt c", c=130)
    wbuf = {nm: p.sb(f"l1a_w_{nm}", [128, NCC * 128], BF16) for nm in ("q", "k", "v")}
    xn_t = [p.sb(f"l1a_xnt{i}", [128, NCC * TB], BF16) for i in range(2)]
    ev = [p.sb(f"l1a_ev{i}", [128, TB], BF16) for i in range(2)]
    sqb = p.sb("l1a_sq", [128, T], BF16)
    nmrow = p.sb("l1a_nmrow", [1, 16], F32)
    smalls = p.sb("l1a_smalls", [1, 16], F32)
    neg_bf = p.sb("l1a_negbf", [1, 2], BF16)
    negc = [p.sb(f"l1a_negc{i}", [128, 1], F32) for i in range(3)]
    eA = [p.sb(f"l1a_eA{i}", [128, TB], BF16) for i in range(2)]
    eB = [p.sb(f"l1a_eB{i}", [128, TB], BF16) for i in range(2)]
    rec_f = p.sb("l1a_recf", [1, TB], F32)
    rec_bf = p.sb("l1a_recbf", [1, TB], BF16)
    rb_sb = p.sb("l1a_rb", [64, TB], F32)
    ybfA = p.sb("l1a_ybfA", [64, TB], BF16)
    ybfB = p.sb("l1a_ybfB", [64, TB], BF16)
    psp = p.ps("l1a_psp", [128, TB])
    pstr = p.ps("l1a_pstr", [128, 128], BF16)
    psA = p.ps("l1a_psA", [128, TB])
    psB = p.ps("l1a_psB", [128, TB])
    psYA = p.ps("l1a_psYA", [65, TB])
    psYB = p.ps("l1a_psYB", [65, TB])
    psN = p.ps("l1a_psN", [1, TB])
    psU = p.ps("l1a_psU", [128, TB])

    mk = {"psp_free": 0, "ev_free": [0, 0], "ev_eng": ["V", "V"], "vtr_ev": 0,
          "sA_ev": 0, "sB_ev": 0, "avA": 0, "avB": 0, "y_done": 0,
          "scores_done": 0, "psN_free": 0, "psC1_free": 0}

    proj_tb_done = {}
    proj_g_done = {}

    for g in range(NCC):
        def gw_sec(gq, g=g):
            if g == 0:
                p.wait_snap(gq, bar)
            if g >= 1:
                gq.wait_ge(p.sP.h, proj_g_done[g - 1])
            for nm in ("q", "k", "v"):
                p.sG.inc(gq.dma_start(out=wbuf[nm][:],
                                      in_=wview[nm][:, :, g * 128:(g + 1) * 128]), 16)
        p.GPS(gw_sec)
        wload_n = p.sG.n

        def ones_sec(v, g=g):
            if mk["scores_done"]:
                v.wait_ge(p.sP.h, mk["scores_done"])
            p.sV.inc(v.memset(v_tm_v[:, :, 64:130:65], 1.0))
        p.VEC(ones_sec)
        vones_n = p.sV.n

        gpb = p.sP.n
        gab = p.sA.n
        prev_scores = mk["scores_done"]
        prev_pspf = mk["psp_free"]

        def a_proj(s, g=g, gpb=gpb, prev_scores=prev_scores):
            for tb in range(NTB):
                for pi in range(3):
                    s.wait_ge(p.sP.h, gpb + tb * 7 + pi + 1)
                    if tb == 0 and pi == 0 and prev_scores:
                        s.wait_ge(p.sP.h, prev_scores)
                    if pi == 0:
                        dst = qf[:, tb * TB:(tb + 1) * TB]
                    elif pi == 1:
                        dst = kf[:, tb * TB:(tb + 1) * TB]
                    else:
                        dst = ev[0][:]
                    p.sA.inc(s.activation(dst, psp[:], AF.Copy))
        p.SCA(a_proj)
        mk["psp_free"] = gab + 3 * NTB

        for tb in range(NTB):
            def s_sec(s, g=g, tb=tb):
                if g == 0 and tb == 0:
                    p.wait_snap(s, bar)
                if g == 0:
                    s.wait_ge(p.sS.h, xn_ready[tb])
                key = (g, tb - 2) if tb >= 2 else (g - 1, NTB - 2 + tb)
                if key in proj_tb_done:
                    s.wait_ge(p.sP.h, proj_tb_done[key])
                p.sS.inc(s.dma_start(
                    out=xn_t[tb % 2].rearrange("p (cc t) -> p cc t", cc=NCC),
                    in_=xn_scr_v[:, :, tb * TB:(tb + 1) * TB]), 16)
            p.SYN(s_sec)
            xn_n = p.sS.n

            for pi, nm in enumerate(("q", "k", "v")):
                def t_sec(t, tb=tb, pi=pi, nm=nm, xn_n=xn_n, wload_n=wload_n,
                          gab=gab, prev_pspf=prev_pspf):
                    if pi == 0:
                        t.wait_ge(p.sS.h, xn_n)
                        t.wait_ge(p.sG.h, wload_n)
                    idx = tb * 3 + pi
                    if idx >= 1:
                        t.wait_ge(p.sA.h, gab + idx)
                    elif prev_pspf:
                        t.wait_ge(p.sA.h, prev_pspf)
                    for cc in range(NCC):
                        mm = t.matmul(psp[:], wbuf[nm][:, cc * 128:(cc + 1) * 128],
                                      xn_t[tb % 2][:, cc * TB:(cc + 1) * TB],
                                      start=(cc == 0), stop=(cc == NCC - 1))
                    p.sP.inc(mm)
                p.TEN(t_sec)

                if nm == "v":
                    ev_n = gab + tb * 3 + 3
                    for j in range(4):
                        def t3_sec(t, j=j, ev_n=ev_n, tb=tb):
                            if j == 0:
                                t.wait_ge(p.sA.h, ev_n)
                            if mk["vtr_ev"]:
                                t.wait_ge(p.sV.h, mk["vtr_ev"])
                            p.sP.inc(t.transpose(pstr[:],
                                                 ev[0][:, j * 128:(j + 1) * 128],
                                                 ident[:]))
                        p.TEN(t3_sec)
                        tr_n = p.sP.n

                        def v_sec(v, tb=tb, j=j, tr_n=tr_n, vones_n=vones_n):
                            v.wait_ge(p.sP.h, tr_n)
                            if tb == 0 and j == 0:
                                v.wait_ge(p.sV.h, vones_n)
                            kt = tb * 4 + j
                            dst = v_tm_v[:, kt, :].rearrange(
                                "p (two r) -> p two r", r=65)[:, :, 0:64]
                            mk["vtr_ev"] = p.sV.inc(
                                v.tensor_copy(dst,
                                    pstr[:].rearrange("p (two r) -> p two r", two=2)))
                        p.VEC(v_sec)
            proj_tb_done[(g, tb)] = p.sP.n
        proj_g_done[g] = p.sP.n

        def sq_sec(s, g=g):
            s.wait_ge(p.sA.h, mk["psp_free"])
            s.wait_ge(p.sP.h, proj_g_done[g])
            p.sA.inc(s.activation(sqb[:], qf[:], AF.Square))
        p.SCA(sq_sec)
        sqq_n = p.sA.n

        norm_res = {}
        for which, src_n in (("q", sqq_n), ("k", None)):
            if which == "k":
                def sq2_sec(s, g=g):
                    s.wait_ge(p.sV.h, norm_res[("q", 1)])
                    p.sA.inc(s.activation(sqb[:], kf[:], AF.Square))
                p.SCA(sq2_sec)
                src_n = p.sA.n
            for hi, base in ((0, 0), (1, 64)):
                for tb in range(NTB):
                    def tn_sec(t, hi=hi, base=base, tb=tb, src_n=src_n):
                        t.wait_ge(p.sA.h, src_n)
                        if mk["psN_free"]:
                            t.wait_ge(p.sV.h, mk["psN_free"])
                        p.sP.inc(t.matmul(psN[:], ones_col[base:base + 64, 0:1],
                                          sqb[base:base + 64, tb * TB:(tb + 1) * TB],
                                          start=True, stop=True,
                                          tile_position=(base, 0)))
                    p.TEN(tn_sec)
                    tn_n = p.sP.n

                    def vr_sec(v, which=which, hi=hi, tb=tb, tn_n=tn_n):
                        v.wait_ge(p.sP.h, tn_n)
                        mk["psN_free"] = p.sV.inc(
                            v.reduce_max(nmrow[:, tb:tb + 1], psN[:], axis=AX.X))
                    p.VEC(vr_sec)

                def vm_sec(v, which=which, hi=hi):
                    idx = {"q": 0, "k": 2}[which] + hi
                    norm_res[(which, hi)] = p.sV.inc(
                        v.reduce_max(smalls[:, idx:idx + 1], nmrow[:, 0:NTB], axis=AX.X))
                p.VEC(vm_sec)
        qkmax_n = p.sV.n

        def vc_sec(v, qkmax_n=qkmax_n):
            v.wait_ge(p.sV.h, qkmax_n)
            v.tensor_tensor(smalls[:, 4:5], smalls[:, 0:1], smalls[:, 2:3], ALU.mult)
            p.sV.inc(v.tensor_tensor(smalls[:, 5:6], smalls[:, 1:2], smalls[:, 3:4],
                                     ALU.mult))
        p.VEC(vc_sec)
        prod_n = p.sV.n

        def ac_sec(s, prod_n=prod_n):
            s.wait_ge(p.sV.h, prod_n)
            p.sA.inc(s.activation(smalls[:, 6:8], smalls[:, 4:6], AF.Sqrt))
        p.SCA(ac_sec)
        csr_n = p.sA.n

        def vc2_sec(v, csr_n=csr_n):
            v.wait_ge(p.sA.h, csr_n)
            v.tensor_scalar_mul(smalls[:, 8:10], smalls[:, 6:8], -0.125)
            p.sV.inc(v.tensor_copy(neg_bf[:], smalls[:, 8:10]))
        p.VEC(vc2_sec)
        nbf_n = p.sV.n

        def tc_sec(t, nbf_n=nbf_n):
            t.wait_ge(p.sV.h, nbf_n)
            if mk["psC1_free"]:
                t.wait_ge(p.sV.h, mk["psC1_free"])
            p.sP.inc(t.matmul(psU[:, 0:2], ones_row[:], neg_bf[:],
                              start=True, stop=True))
        p.TEN(tc_sec)
        tc_n = p.sP.n

        def vc3_sec(v, tc_n=tc_n):
            v.wait_ge(p.sP.h, tc_n)
            v.tensor_copy(negc[0][:], psU[:, 0:1])
            v.tensor_copy(negc[1][:], psU[:, 1:2])
            mk["psC1_free"] = p.sV.inc(
                v.tensor_tensor(negc[2][:], negc[0][:], negc[1][:], ALU.min))
        p.VEC(vc3_sec)
        negc_n = p.sV.n

        for qw in range(NTB):
            pbase = p.sP.n
            abase = p.sA.n

            def a_win(s, g=g, qw=qw, pbase=pbase):
                for kt in range(NKT):
                    s.wait_ge(p.sP.h, pbase + 2 * kt + 1)
                    s.activation(eA[kt % 2][:], psA[:], AF.Exp, scale=0.125,
                                 bias=negc[0][:])
                    p.sA.inc(s.activation(eB[kt % 2][:], psB[:], AF.Exp, scale=0.125,
                                          bias=negc[1][:]))
            p.SCA(a_win)
            mk["sA_ev"] = p.sA.n

            def t_win(t, g=g, qw=qw, negc_n=negc_n, pbase=pbase, abase=abase):
                for kt in range(NKT):
                    if qw == 0 and kt == 0:
                        t.wait_ge(p.sV.h, negc_n)
                    if kt >= 1:
                        t.wait_ge(p.sA.h, abase + kt)
                    qs = slice(qw * TB, (qw + 1) * TB)
                    ks = slice(kt * 128, (kt + 1) * 128)
                    t.matmul(psA[:], kf[0:64, ks], qf[0:64, qs],
                             start=True, stop=True, tile_position=(0, 0))
                    mm = t.matmul(psB[:], kf[64:128, ks], qf[64:128, qs],
                                  start=True, stop=True, tile_position=(64, 0))
                    p.sP.inc(mm)
                    t.wait_ge(p.sA.h, abase + kt + 1)
                    if kt == 0 and mk["y_done"]:
                        t.wait_ge(p.sV.h, mk["y_done"])
                    t.matmul(psYA[:], v_tm[:, kt * 130: kt * 130 + 65], eA[kt % 2][:],
                             start=(kt == 0), stop=(kt == NKT - 1), skip_group_check=True)
                    mm = t.matmul(psYB[:], v_tm[:, kt * 130 + 65: kt * 130 + 130],
                                  eB[kt % 2][:], start=(kt == 0), stop=(kt == NKT - 1),
                                  skip_group_check=True)
                    mk["avB"] = p.sP.inc(mm)
            p.TEN(t_win)
            y_n = p.sP.n

            for hi, psY, yb in ((0, psYA, ybfA), (1, psYB, ybfB)):
                def vr_sec(v, psY=psY, y_n=y_n):
                    v.wait_ge(p.sP.h, y_n)
                    v.reciprocal(rec_f[:], psY[64:65, :])
                    p.sV.inc(v.tensor_copy(rec_bf[:], rec_f[:]))
                p.VEC(vr_sec)
                rb_n = p.sV.n

                def tb_sec(t, rb_n=rb_n):
                    t.wait_ge(p.sV.h, rb_n)
                    if mk["psC1_free"]:
                        t.wait_ge(p.sV.h, mk["psC1_free"])
                    p.sP.inc(t.matmul(psU[0:64, :], ones_row[0:1, 0:64], rec_bf[:],
                                      start=True, stop=True))
                p.TEN(tb_sec)
                tb_n = p.sP.n

                def vy_sec(v, psY=psY, yb=yb, tb_n=tb_n):
                    v.wait_ge(p.sP.h, tb_n)
                    v.tensor_copy(rb_sb[:], psU[0:64, :])
                    mk["psC1_free"] = p.sV.n + 1
                    p.sV.inc(v.tensor_tensor(yb[:], psY[0:64, :], rb_sb[:], ALU.mult))
                p.VEC(vy_sec)
            mults_n = p.sV.n

            def tsh_sec(t, mults_n=mults_n):
                t.wait_ge(p.sV.h, mults_n)
                t.matmul(psU[:], ident[0:64, :], ybfA[:], start=True, stop=False)
                p.sP.inc(t.matmul(psU[:], idsh[0:64, :], ybfB[:],
                                  start=False, stop=True))
            p.TEN(tsh_sec)
            tsh_n = p.sP.n

            def vadd_sec(v, g=g, qw=qw, tsh_n=tsh_n):
                v.wait_ge(p.sP.h, tsh_n)
                xs = x_fm[:, g * T + qw * TB: g * T + qw * TB + TB]
                mk["psC1_free"] = p.sV.inc(v.tensor_tensor(xs, xs, psU[:], ALU.add))
                mk["y_done"] = mk["psC1_free"]
            p.VEC(vadd_sec)
        mk["scores_done"] = p.sP.n
    p.pop()


def emit_spill_x(p, env):
    import os as _os
    x_scr = p.nc.dram_tensor("x_scr", [C, T], F32, kind="Internal").ap()
    env["x_scr_v"] = x_scr.rearrange("(cc p) t -> cc p t", p=128)
    x_fm = env["x_fm"]
    bar = p.snap()
    dbgx = _os.environ.get("DBGX", "0") == "1"
    if dbgx:
        x3d = p.nc.dram_tensor("x3dbg", [C, T], F32, kind="ExternalOutput").ap()
        x3v = x3d.rearrange("(cc p) t -> cc p t", p=128)

    def sec(s):
        p.wait_snap(s, bar)
        for cc in range(NCC):
            p.sS.inc(s.dma_start(out=env["x_scr_v"][cc],
                                 in_=x_fm[:, cc * T:(cc + 1) * T]), 16)
            if dbgx:
                p.sO.inc(s.dma_start(out=x3v[cc],
                                     in_=x_fm[:, cc * T:(cc + 1) * T]), 16)
    p.SYN(sec)
    env["x_spilled"] = p.sS.n


def emit_l1_ffn(p, env):
    xn_scr_v = env["xn_scr_v"]
    xn_ready = env["xn_ready"]
    x_scr_v = env["x_scr_v"]
    xout = p.nc.dram_tensor("xout", [C, T], F32, kind="ExternalOutput").ap()
    xout_v = xout.rearrange("(cc p) t -> cc p t", p=128)

    F1, F2 = 4096, 4096
    NF1, NF2 = 32, 32

    k0_d = p.inp("l1f_k0t", [32, 128, 16 * 128], U8)
    k1_d = p.inp("l1f_k1t", [32, 128, 32 * 128], U8)
    k2_d = p.inp("l1f_k2t", [8, 128, 32 * 128], U8)
    s0_d = p.inp("l1f_s0", [128, 1], F32)
    s1_d = p.inp("l1f_s1", [128, 1], F32)
    s2_d = p.inp("l1f_s2", [128, 1], F32)

    bar = p.snap()
    import os as _os
    p.serial = frozenset(_os.environ.get("SERIAL3", "s"))
    fixraw = _os.environ.get("FIXRAW", "0") == "1"
    fixact = _os.environ.get("FIXACT", "0") == "1"
    dbg3 = _os.environ.get("DBG3", "0") == "1"
    if dbg3:
        d_wab = p.nc.dram_tensor("d_wab", [NTB * 32, 128, 2048], BF16, kind="ExternalOutput").ap()
        d_w0e = p.nc.dram_tensor("d_w0e", [NTB * 32, 128, C], BF16, kind="ExternalOutput").ap()
        d_xn = p.nc.dram_tensor("d_xn", [NTB, 128, NCC * TB], BF16, kind="ExternalOutput").ap()
        d_h1 = p.nc.dram_tensor("d_h1", [NTB, 128, 32 * TB], BF16, kind="ExternalOutput").ap()
        d_h2 = p.nc.dram_tensor("d_h2", [NTB, 128, 32 * TB], BF16, kind="ExternalOutput").ap()
    p.push()
    wprobe = p.sb("l1f_wprobe", [128, 8], BF16)
    aprobe = p.sb("l1f_aprobe", [128, 8], BF16)
    h1 = p.sb("l1f_h1", [128, NF1 * TB], BF16)
    h2 = p.sb("l1f_h2", [128, NF2 * TB], BF16)
    xn_t = p.sb("l1f_xnt", [128, NCC * TB], BF16)
    wab = [p.sb(f"l1f_wab{i}", [128, 2 * NCC * 128], U8) for i in range(2)]
    w0e = [p.sb(f"l1f_w0e{i}", [128, C], BF16) for i in range(2)]
    wu = [p.sb(f"l1f_wu{i}", [128, F1], U8) for i in range(2)]
    we = [p.sb(f"l1f_we{i}", [128, F1], BF16) for i in range(2)]
    sc_t = p.sb("l1f_sc", [128, 4], F32)
    xblk = [p.sb(f"l1f_xb{i}", [128, TB], F32) for i in range(2)]
    ybuf = [p.sb(f"l1f_yb{i}", [128, TB], F32) for i in range(2)]
    psm = [p.ps(f"l1f_ps{i}", [128, TB]) for i in range(2)]

    def sld_sec(s):
        p.wait_snap(s, bar)
        p.sS.inc(s.dma_start(out=sc_t[:, 0:1], in_=s0_d), 16)
        p.sS.inc(s.dma_start(out=sc_t[:, 1:2], in_=s1_d), 16)
        p.sS.inc(s.dma_start(out=sc_t[:, 2:3], in_=s2_d), 16)
    p.SYN(sld_sec)
    sld_n = p.sS.n

    def sa_sec(s):
        s.wait_ge(p.sS.h, sld_n)
        s.activation(sc_t[:, 0:1], sc_t[:, 0:1], AF.Copy, scale=4.0)
        p.sA.inc(s.activation(sc_t[:, 1:2], sc_t[:, 1:2], AF.Copy, scale=2.0))
    p.SCA(sa_sec)
    scales_n = p.sA.n

    mk = {"conv": {}, "mm": {}, "ev": {}, "l1mm_tb": {}, "l2mm_tb": {}, "l3mm_tb": {},
          "stt": {}, "yst": {}, "xb_used": {}, "wconv": {}, "wmm": {}}
    gidx = 0
    wj = 0

    for tb in range(NTB):
        ts0 = tb * TB

        def s_sec(s, tb=tb):
            if tb == 0:
                p.wait_snap(s, bar)
                s.wait_ge(p.sS.h, env["x_spilled"])
            s.wait_ge(p.sS.h, xn_ready[tb])
            if tb >= 1:
                s.wait_ge(p.sP.h, mk["l1mm_tb"][tb - 1])
            p.sS.inc(s.dma_start(
                out=xn_t.rearrange("p (cc t) -> p cc t", cc=NCC),
                in_=xn_scr_v[:, :, ts0:ts0 + TB]), 16)
        p.SYN(s_sec)
        xn_n = p.sS.n

        if dbg3:
            def dxn_sec(s, tb=tb, xn_n=xn_n):
                s.wait_ge(p.sG.h, xn_n)
                p.sO.inc(s.dma_start(out=d_xn[tb], in_=xn_t[:]), 16)
            p.SYN(dxn_sec)

        l1_pb = p.sP.n
        l1_ab = p.sA.n
        for fc in range(NF1):
            ci = tb * NF1 + fc

            def g_sec(g, tb=tb, fc=fc, ci=ci):
                if tb == 0 and fc == 0:
                    p.wait_snap(g, bar)
                if (ci - 2) in mk["conv"]:
                    g.wait_ge(p.sV.h, mk["conv"][ci - 2])
                p.sS.inc(g.dma_start(out=wab[fc % 2][:], in_=k0_d[fc]), 16)
            p.SYN(g_sec)
            gl_n = p.sS.n

            def v_sec(v, fc=fc, ci=ci, gl_n=gl_n):
                v.wait_ge(p.sS.h, gl_n)
                if (ci - 2) in mk["mm"]:
                    v.wait_ge(p.sP.h, mk["mm"][ci - 2])
                tt = v.tensor_tensor(w0e[fc % 2][:], wab[fc % 2][:, 0:NCC * 128],
                                     wab[fc % 2][:, NCC * 128:], ALU.subtract)
                if fixraw:
                    tt = v.tensor_copy(wprobe[:], w0e[fc % 2][:, 0:8])
                mk["conv"][ci] = p.sV.inc(tt)
            p.VEC(v_sec)
            cv_n = p.sV.n

            if dbg3:
                def dw_sec(s, tb=tb, fc=fc, ci=ci, gl_n=gl_n, cv_n=cv_n):
                    s.wait_ge(p.sG.h, gl_n)
                    s.wait_ge(p.sV.h, cv_n)
                    p.sO.inc(s.dma_start(out=d_wab[ci], in_=wab[fc % 2][:]), 16)
                    p.sO.inc(s.dma_start(out=d_w0e[ci], in_=w0e[fc % 2][:]), 16)
                    mk["dw"] = p.sO.n
                p.SYN(dw_sec)

            def t_sec(t, tb=tb, fc=fc, ci=ci, cv_n=cv_n, xn_n=xn_n, gi=gidx,
                      l1_ab=l1_ab):
                t.wait_ge(p.sV.h, cv_n)
                if fc == 0:
                    t.wait_ge(p.sS.h, xn_n)
                if fc >= 2:
                    t.wait_ge(p.sA.h, l1_ab + fc - 1)
                elif (gi - 2) in mk["ev"]:
                    eng, cnt = mk["ev"][gi - 2]
                    t.wait_ge((p.sA if eng == "A" else p.sV).h, cnt)
                for cc in range(NCC):
                    mm = t.matmul(psm[gi % 2][:],
                                  w0e[fc % 2][:, cc * 128:(cc + 1) * 128],
                                  xn_t[:, cc * TB:(cc + 1) * TB],
                                  start=(cc == 0), stop=(cc == NCC - 1))
                mk["mm"][ci] = p.sP.inc(mm)
            p.TEN(t_sec)
            gidx += 1

        gi0 = gidx - NF1

        def a_l1(s, tb=tb, l1_pb=l1_pb, gi0=gi0, scales_n=scales_n):
            for fc in range(NF1):
                s.wait_ge(p.sP.h, l1_pb + fc + 1)
                if tb == 0 and fc == 0:
                    s.wait_ge(p.sA.h, scales_n)
                p.sA.inc(s.activation(h1[:, fc * TB:(fc + 1) * TB],
                                      psm[(gi0 + fc) % 2][:],
                                      AF.Relu, scale=sc_t[:, 0:1]))
        p.SCA(a_l1)
        for fc in range(NF1):
            mk["ev"][gi0 + fc] = ("A", l1_ab + fc + 1)
        l1ev_n = p.sA.n
        mk["l1mm_tb"][tb] = p.sP.n

        if dbg3:
            def dh1_sec(s, tb=tb, l1ev_n=l1ev_n):
                s.wait_ge(p.sA.h, l1ev_n)
                p.sO.inc(s.dma_start(out=d_h1[tb], in_=h1[:]), 16)
            p.SYN(dh1_sec)

        l2_pb = p.sP.n
        l2_ab = p.sA.n
        for fc in range(NF2):
            ci = 1000 + tb * NF2 + fc

            def g_sec(g, fc=fc, wj=wj):
                if (wj - 2) in mk["wconv"]:
                    g.wait_ge(p.sV.h, mk["wconv"][wj - 2])
                p.sS.inc(g.dma_start(out=wu[wj % 2][:], in_=k1_d[fc]), 16)
            p.SYN(g_sec)
            gl_n = p.sS.n

            def v_sec(v, wj=wj, gl_n=gl_n):
                v.wait_ge(p.sS.h, gl_n)
                if (wj - 2) in mk["wmm"]:
                    v.wait_ge(p.sP.h, mk["wmm"][wj - 2])
                tt = v.tensor_scalar(we[wj % 2][:], wu[wj % 2][:], 2.0, -1.0,
                                     ALU.mult, ALU.add)
                if fixraw:
                    tt = v.tensor_copy(wprobe[:], we[wj % 2][:, 0:8])
                mk["wconv"][wj] = p.sV.inc(tt)
            p.VEC(v_sec)
            cv_n = p.sV.n

            def t_sec(t, fc=fc, wj=wj, cv_n=cv_n, gi=gidx, l1ev_n=l1ev_n,
                      l2_ab=l2_ab):
                t.wait_ge(p.sV.h, cv_n)
                if fc == 0:
                    t.wait_ge(p.sA.h, l1ev_n)
                if fc >= 2:
                    t.wait_ge(p.sA.h, l2_ab + fc - 1)
                elif (gi - 2) in mk["ev"]:
                    eng, cnt = mk["ev"][gi - 2]
                    t.wait_ge((p.sA if eng == "A" else p.sV).h, cnt)
                for cc in range(NF1):
                    mm = t.matmul(psm[gi % 2][:],
                                  we[wj % 2][:, cc * 128:(cc + 1) * 128],
                                  h1[:, cc * TB:(cc + 1) * TB],
                                  start=(cc == 0), stop=(cc == NF1 - 1))
                mk["wmm"][wj] = p.sP.inc(mm)
            p.TEN(t_sec)
            gidx += 1
            wj += 1

        gi0 = gidx - NF2

        def a_l2(s, tb=tb, l2_pb=l2_pb, gi0=gi0):
            for fc in range(NF2):
                s.wait_ge(p.sP.h, l2_pb + fc + 1)
                p.sA.inc(s.activation(h2[:, fc * TB:(fc + 1) * TB],
                                      psm[(gi0 + fc) % 2][:],
                                      AF.Relu, scale=sc_t[:, 1:2]))
        p.SCA(a_l2)
        for fc in range(NF2):
            mk["ev"][gi0 + fc] = ("A", l2_ab + fc + 1)
        l2ev_n = p.sA.n
        mk["l2mm_tb"][tb] = p.sP.n

        if dbg3:
            def dh2_sec(s, tb=tb, l2ev_n=l2ev_n):
                s.wait_ge(p.sA.h, l2ev_n)
                p.sO.inc(s.dma_start(out=d_h2[tb], in_=h2[:]), 16)
            p.SYN(dh2_sec)

        for oc in range(NCC):
            ci = 2000 + tb * NCC + oc

            def g_sec(g, oc=oc, wj=wj):
                if (wj - 2) in mk["wconv"]:
                    g.wait_ge(p.sV.h, mk["wconv"][wj - 2])
                p.sS.inc(g.dma_start(out=wu[wj % 2][:], in_=k2_d[oc]), 16)
            p.SYN(g_sec)
            gl_n = p.sS.n

            def sx_sec(s, tb=tb, oc=oc, ci=ci):
                if (ci - 2) in mk["xb_used"]:
                    s.wait_ge(p.sV.h, mk["xb_used"][ci - 2])
                p.sS.inc(s.dma_start(out=xblk[oc % 2][:],
                                     in_=x_scr_v[oc][:, tb * TB:(tb + 1) * TB]), 16)
            p.SYN(sx_sec)
            xb_n = p.sS.n

            def v_sec(v, wj=wj, gl_n=gl_n):
                v.wait_ge(p.sS.h, gl_n)
                if (wj - 2) in mk["wmm"]:
                    v.wait_ge(p.sP.h, mk["wmm"][wj - 2])
                tt = v.tensor_scalar(we[wj % 2][:], wu[wj % 2][:], 2.0, -1.0,
                                     ALU.mult, ALU.add)
                if fixraw:
                    tt = v.tensor_copy(wprobe[:], we[wj % 2][:, 0:8])
                mk["wconv"][wj] = p.sV.inc(tt)
            p.VEC(v_sec)
            cv_n = p.sV.n

            def t_sec(t, oc=oc, wj=wj, cv_n=cv_n, gi=gidx, l2ev_n=l2ev_n):
                t.wait_ge(p.sV.h, cv_n)
                if oc == 0:
                    t.wait_ge(p.sA.h, l2ev_n)
                if (gi - 2) in mk["ev"]:
                    eng, cnt = mk["ev"][gi - 2]
                    t.wait_ge((p.sA if eng == "A" else p.sV).h, cnt)
                for cc in range(NF2):
                    mm = t.matmul(psm[gi % 2][:],
                                  we[wj % 2][:, cc * 128:(cc + 1) * 128],
                                  h2[:, cc * TB:(cc + 1) * TB],
                                  start=(cc == 0), stop=(cc == NF2 - 1))
                mk["wmm"][wj] = p.sP.inc(mm)
            p.TEN(t_sec)
            mm_n = p.sP.n

            def vy_sec(v, tb=tb, oc=oc, ci=ci, mm_n=mm_n, xb_n=xb_n, gi=gidx):
                v.wait_ge(p.sP.h, mm_n)
                v.wait_ge(p.sS.h, xb_n)
                if (ci - 2) in mk["yst"]:
                    v.wait_ge(p.sO.h, mk["yst"][ci - 2])
                out = p.sV.inc(
                    v.scalar_tensor_tensor(ybuf[oc % 2][:], psm[gi % 2][:],
                                           sc_t[:, 2:3], xblk[oc % 2][:],
                                           ALU.mult, ALU.add))
                mk["xb_used"][ci] = out
                mk["stt"][ci] = out
                mk["ev"][gi] = ("V", out)
            p.VEC(vy_sec)
            stt_n = p.sV.n

            def sy_sec(s, tb=tb, oc=oc, ci=ci, stt_n=stt_n):
                s.wait_ge(p.sV.h, stt_n)
                mk["yst"][ci] = p.sO.inc(
                    s.dma_start(out=xout_v[oc][:, tb * TB:(tb + 1) * TB],
                                in_=ybuf[oc % 2][:]), 16)
            p.SYN(sy_sec)
            gidx += 1
            wj += 1
    p.serial = frozenset()
    p.pop()


def build_program(blocks):
    nc = bass.Bass("TRN2", target_bir_lowering=False)
    st = ExitStack()
    with st:
        p = Prog(nc, st)
        env = {}

        ident = p.sb("ident", [128, 128], BF16)
        pswap = p.sb("pswap", [128, 128], BF16)
        ones_col = p.sb("ones_col", [128, 1], BF16)
        ones_row = p.sb("ones_row", [1, 128], BF16)
        ones_row_f = p.sb("ones_row_f", [1, 128], F32)
        env.update(ident=ident, pswap=pswap, ones_col=ones_col, ones_row=ones_row)

        id_d = p.inp("ident_in", [128, 128], BF16)
        ps_d = p.inp("pswap_in", [128, 128], BF16)
        idsh = p.sb("idsh", [128, 128], BF16)
        env["idsh"] = idsh
        ish_d = p.inp("idsh_in", [128, 128], BF16)

        def const_sec(gq):
            p.sG.inc(gq.dma_start(out=ident[:], in_=id_d), 16)
            p.sG.inc(gq.dma_start(out=pswap[:], in_=ps_d), 16)
            p.sG.inc(gq.dma_start(out=idsh[:], in_=ish_d), 16)
        p.GPS(const_sec)

        eps_t = p.sb("eps_t", [1, 1], F32)
        env["eps_t"] = eps_t

        def ones_sec(v):
            v.memset(ones_col[:], 1.0)
            v.memset(eps_t[:], EPS)
            v.memset(ones_row_f[:], 1.0)
            p.sV.inc(v.memset(ones_row[:], 1.0))
        p.VEC(ones_sec)
        env["ones_row_f"] = ones_row_f

        xn_scr = nc.dram_tensor("xn_scr", [C, T], BF16, kind="Internal").ap()
        env["xn_scr_v"] = xn_scr.rearrange("(cc p) t -> p cc t", p=128)

        p.push()
        x_fm = p.sb("x_fm", [128, NCC * T], F32)
        env["x_fm"] = x_fm
        emit_load_x(p, env)

        if 0 in blocks:
            emit_ln(p, env, "0")
            emit_l0_attn(p, env)

        if 1 in blocks:
            emit_ln(p, env, "1")
            emit_l0_ffn(p, env)

        if 2 in blocks:
            emit_ln(p, env, "2")
            emit_l1_attn(p, env)

        if 3 in blocks:
            emit_ln(p, env, "3")
            emit_spill_x(p, env)
            p.pop()
            emit_l1_ffn(p, env)
        else:
            emit_store_x(p, env)
            p.pop()
        emit_final_wait(p)

    return nc, sorted(p.used_inputs)


def host_constants():
    bf = ml_dtypes.bfloat16
    ident = np.eye(128, dtype=np.float32)
    pswap = np.zeros((128, 128), np.float32)
    for j in range(128):
        pswap[j ^ 32, j] = 1.0
    inv_freq = (1.0 / (10000.0 ** (np.arange(0, HD, 2, dtype=np.float32) / HD))).astype(np.float32)
    tpos = np.arange(T, dtype=np.float32)
    freqs = tpos[:, None] * inv_freq[None, :]
    cos_t = np.cos(freqs).astype(bf).astype(np.float32)
    sin_t = np.sin(freqs).astype(bf).astype(np.float32)
    Ctab = np.zeros((128, T), np.float32)
    Stab = np.zeros((128, T), np.float32)
    for pr in range(128):
        i = pr % 32
        Ctab[pr] = cos_t[:, i]
        Stab[pr] = sin_t[:, i] * (1.0 if (pr % 64) < 32 else -1.0)
    cmask = np.zeros((128, 4 * TB), np.float32)
    qf = np.arange(TB)
    for r in range(4):
        for kp in range(128):
            cmask[kp, r * TB:(r + 1) * TB] = np.where(qf >= r * 128 + kp, 1.0 / 4096.0, 0.0)
    idsh = np.zeros((128, 128), np.float32)
    for i in range(64):
        idsh[i, i + 64] = 1.0
    return {"ident_in": ident.astype(bf), "pswap_in": pswap.astype(bf),
            "idsh_in": idsh.astype(bf),
            "rope_c": Ctab.astype(bf), "rope_s": Stab.astype(bf),
            "cmask": cmask.astype(bf)}


def host_prep(inputs):
    f32 = np.float32
    shared = dict(host_constants())
    tr = lambda a: np.ascontiguousarray(np.asarray(a, dtype=f32).T)
    shared["l0q1"] = tr(inputs["L0_attn_Wq1"])
    shared["l0k1"] = tr(inputs["L0_attn_Wk1"])
    shared["l0q2"] = tr(inputs["L0_attn_Wq2"])
    shared["l0k2"] = tr(inputs["L0_attn_Wk2"])
    shared["l0v"] = tr(inputs["L0_attn_Wv"])
    shared["l0o"] = tr(inputs["L0_attn_Wout"])
    shared["l0fr"] = tr(inputs["L0_ffn_W_r"])
    shared["l0fk"] = tr(inputs["L0_ffn_W_k"])
    shared["l0fv"] = tr(inputs["L0_ffn_W_v"])
    shared["l0f_mur"] = np.ascontiguousarray(
        np.asarray(inputs["L0_ffn_mu_r"], f32).reshape(NCC, 128).T)
    shared["l0f_muk"] = np.ascontiguousarray(
        np.asarray(inputs["L0_ffn_mu_k"], f32).reshape(NCC, 128).T)
    cat = lambda w: np.ascontiguousarray(
        np.asarray(w, f32).transpose(1, 0, 2).reshape(C, C))
    shared["l1q"] = cat(inputs["L1_attn_W_Q"])
    shared["l1k"] = cat(inputs["L1_attn_W_K"])
    shared["l1v"] = cat(inputs["L1_attn_W_V"])
    k0 = np.ascontiguousarray(np.asarray(inputs["L1_ffn_k0"])).view(np.uint8)
    k1_ = np.ascontiguousarray(np.asarray(inputs["L1_ffn_k1"])).view(np.uint8)
    k2_ = np.ascontiguousarray(np.asarray(inputs["L1_ffn_k2"])).view(np.uint8)

    def wtile(k, ng, nf):
        return np.ascontiguousarray(
            k.reshape(ng, 128, nf, 128).transpose(2, 1, 0, 3).reshape(nf, 128, ng * 128))
    shared["l1f_k0t"] = wtile(k0, 16, 32)
    shared["l1f_k1t"] = wtile(k1_, 32, 32)
    shared["l1f_k2t"] = wtile(k2_, 32, 8)
    for i in (0, 1, 2):
        shared[f"l1f_s{i}"] = np.ascontiguousarray(np.broadcast_to(
            np.asarray(inputs[f"L1_ffn_s{i}"], f32).reshape(1, 1), (128, 1)))
    return shared


LAUNCH_GROUPS = [[0, 1, 2, 3]]


def kernel(**inputs):
    x = np.asarray(inputs["L0_attn_x"], dtype=np.float32)
    shared = host_prep(inputs)
    state = [np.ascontiguousarray(x[b].T) for b in range(8)]

    for grp in LAUNCH_GROUPS:
        nc, needed = build_program(grp)
        in_maps = []
        for b in range(8):
            m = {}
            for k in needed:
                m[k] = state[b] if k == "x" else shared[k]
            in_maps.append(m)
        res = run_bass_kernel_spmd(nc, in_maps, core_ids=list(range(8)))
        state = [np.asarray(res.results[b]["xout"]) for b in range(8)]

    out = np.stack([s.T for s in state]).astype(np.float32)
    return np.ascontiguousarray(out)


# version 16 (final)
# speedup vs baseline: 1.0202x; 1.0094x over previous
import numpy as np
import ml_dtypes
from contextlib import ExitStack

import concourse.bass as bass
import concourse.mybir as mybir
from concourse.bass_utils import run_bass_kernel_spmd

F32 = mybir.dt.float32
BF16 = mybir.dt.bfloat16
U8 = mybir.dt.uint8
AF = mybir.ActivationFunctionType
ALU = mybir.AluOpType
AX = mybir.AxisListType

C = 1024
T = 4096
NCC = 8
NTB = 8
TB = 512
NKT = 32
NH = 16
HD = 64
EPS = 1e-5


class Sem:
    __slots__ = ("h", "n")

    def __init__(self, h):
        self.h = h
        self.n = 0

    def inc(self, instr, k=1):
        instr.then_inc(self.h, k)
        self.n += k
        return self.n


class Prog:
    def __init__(self, nc, st):
        self.nc = nc
        self.stacks = [st]
        self.blk = st.enter_context(nc.Block())
        self.used_inputs = set()
        for name in ("sS", "sG", "sQ", "sP", "sV", "sA", "sO"):
            setattr(self, name, Sem(st.enter_context(nc.semaphore(name))))

    serial = frozenset()

    def _wrap(self, f, kind):
        if kind not in self.serial:
            return f
        snapshot = self.snap()

        def g(eng):
            self.wait_snap(eng, snapshot)
            f(eng)
        return g

    def TEN(self, f): self.blk.tensor(self._wrap(f, "t"))
    def VEC(self, f): self.blk.vector(self._wrap(f, "v"))
    def SCA(self, f): self.blk.scalar(self._wrap(f, "a"))
    def SYN(self, f): self.blk.sync(self._wrap(f, "s"))
    def GPS(self, f): self.blk.gpsimd(self._wrap(f, "g"))

    def inp(self, name, shape, dtype):
        self.used_inputs.add(name)
        return self.nc.dram_tensor(name, shape, dtype, kind="ExternalInput").ap()

    def sb(self, name, shape, dtype):
        return self.stacks[-1].enter_context(self.nc.sbuf_tensor(name, shape, dtype))

    def ps(self, name, shape, dtype=F32):
        return self.stacks[-1].enter_context(self.nc.psum_tensor(name, shape, dtype))

    def push(self):
        st = ExitStack()
        self.stacks.append(st)
        return st

    def pop(self):
        st = self.stacks.pop()
        st.close()

    def snap(self):
        return {k: getattr(self, k).n
                for k in ("sS", "sG", "sQ", "sP", "sV", "sA")}

    def wait_snap(self, eng, snapshot):
        for k, v in snapshot.items():
            if v > 0:
                eng.wait_ge(getattr(self, k).h, v)


def emit_load_x(p, env):
    x = p.inp("x", [C, T], F32)
    xv = x.rearrange("(cc p) t -> cc p t", p=128)
    x_fm = env["x_fm"]

    def sec(s):
        for cc in range(NCC):
            p.sS.inc(s.dma_start(out=x_fm[:, cc * T:(cc + 1) * T], in_=xv[cc]), 16)
    p.SYN(sec)


def emit_store_x(p, env):
    xout = p.nc.dram_tensor("xout", [C, T], F32, kind="ExternalOutput").ap()
    xov = xout.rearrange("(cc p) t -> cc p t", p=128)
    x_fm = env["x_fm"]
    bar = p.snap()

    def sec(s):
        p.wait_snap(s, bar)
        for cc in range(NCC):
            p.sO.inc(s.dma_start(out=xov[cc], in_=x_fm[:, cc * T:(cc + 1) * T]), 16)
    p.SYN(sec)


def emit_final_wait(p):
    n = p.sO.n

    def sec(s):
        s.wait_ge(p.sO.h, n)
    p.SYN(sec)


def emit_ln(p, env, tag):
    x_fm = env["x_fm"]
    ones_col = env["ones_col"]
    ones_row = env["ones_row"]
    xn_scr_v = env["xn_scr_v"]
    bar = p.snap()

    p.push()
    xbf = [p.sb(f"ln{tag}_xbf{i}", [128, TB], BF16) for i in range(3)]
    sq = [p.sb(f"ln{tag}_sq{i}", [128, TB], BF16) for i in range(3)]
    rows = p.sb(f"ln{tag}_rows", [1, TB * 6], F32)
    mean_f, m2_f, var_f, sd_f, rstd_f, lo_f = (rows[:, i * TB:(i + 1) * TB] for i in range(6))
    mean_bf = p.sb(f"ln{tag}_meanbf", [1, TB], BF16)
    rstd_bf = p.sb(f"ln{tag}_rstdbf", [1, TB], BF16)
    mlo_bf = p.sb(f"ln{tag}_mlobf", [1, TB], BF16)
    rlo_bf = p.sb(f"ln{tag}_rlobf", [1, TB], BF16)
    xn_t = [p.sb(f"ln{tag}_xnt{i}", [128, NCC * TB], BF16) for i in range(2)]
    tsub = p.sb(f"ln{tag}_tsub", [128, TB], F32)
    psS = p.ps(f"ln{tag}_psS", [1, TB])
    psQ = p.ps(f"ln{tag}_psQ", [1, TB])
    psM = p.ps(f"ln{tag}_psM", [128, TB])
    psR = p.ps(f"ln{tag}_psR", [128, TB])

    stat_mark = {}
    mean_done = {}
    var_done = {}
    rows_done = {}
    bcast_done = {}
    norm_done = {}
    store_done = {}

    pb0 = p.sP.n
    pred_stat = lambda j: pb0 + (j // NCC) * 9 + (j % NCC) + 1
    cast_marks = {}

    for tb in range(NTB):
        def a_cast(s, tb=tb):
            for cc in range(NCC):
                idx = tb * NCC + cc
                if idx == 0:
                    p.wait_snap(s, bar)
                if idx >= 3:
                    s.wait_ge(p.sP.h, pred_stat(idx - 3))
                srcv = x_fm[:, cc * T + tb * TB: cc * T + tb * TB + TB]
                b = idx % 3
                s.activation(xbf[b][:], srcv, AF.Copy)
                cast_marks[idx] = p.sA.inc(s.activation(sq[b][:], srcv, AF.Square))
        p.SCA(a_cast)

        for cc in range(NCC):
            idx = tb * NCC + cc

            def t_sec(t, tb=tb, cc=cc, idx=idx):
                t.wait_ge(p.sA.h, cast_marks[idx])
                if cc == 0 and tb > 0:
                    t.wait_ge(p.sA.h, mean_done[tb - 1])
                    t.wait_ge(p.sV.h, var_done[tb - 1])
                b = idx % 3
                t.matmul(psS[:], ones_col[:], xbf[b][:],
                         start=(cc == 0), stop=(cc == NCC - 1),
                         skip_group_check=True)
                mm = t.matmul(psQ[:], ones_col[:], sq[b][:],
                              start=(cc == 0), stop=(cc == NCC - 1),
                              skip_group_check=True)
                stat_mark[idx] = p.sP.inc(mm)
            p.TEN(t_sec)

        sn = stat_mark[tb * NCC + NCC - 1]

        def a2_sec(s, tb=tb, sn=sn):
            s.wait_ge(p.sP.h, sn)
            mean_done[tb] = p.sA.inc(s.activation(mean_f, psS[:], AF.Copy, scale=1.0 / C))
        p.SCA(a2_sec)
        a2n = p.sA.n

        def v_sec(v, tb=tb, a2n=a2n, sn=sn):
            v.wait_ge(p.sA.h, a2n)
            v.wait_ge(p.sP.h, sn)
            v.tensor_tensor(m2_f, mean_f, mean_f, ALU.mult)
            var_done[tb] = p.sV.inc(
                v.scalar_tensor_tensor(var_f, psQ[:], 1.0 / C, m2_f,
                                       ALU.mult, ALU.subtract))
        p.VEC(v_sec)
        vn = p.sV.n

        def a3_sec(s, tb=tb, vn=vn):
            s.wait_ge(p.sV.h, vn)
            p.sA.inc(s.activation(sd_f, var_f, AF.Sqrt, bias=env["eps_t"][:]))
        p.SCA(a3_sec)
        a3n = p.sA.n

        def v2_sec(v, tb=tb, a3n=a3n):
            v.wait_ge(p.sA.h, a3n)
            v.reciprocal(rstd_f, sd_f)
            v.tensor_copy(mean_bf[:], mean_f)
            v.tensor_copy(rstd_bf[:], rstd_f)
            v.tensor_tensor(lo_f, mean_f, mean_bf[:], ALU.subtract)
            v.tensor_copy(mlo_bf[:], lo_f)
            v.tensor_tensor(lo_f, rstd_f, rstd_bf[:], ALU.subtract)
            rows_done[tb] = p.sV.inc(v.tensor_copy(rlo_bf[:], lo_f))
        p.VEC(v2_sec)

        def t2_sec(t, tb=tb):
            t.wait_ge(p.sV.h, rows_done[tb])
            if tb > 0:
                t.wait_ge(p.sV.h, norm_done[(tb - 1, NCC - 1)])
            t.matmul(psM[:], ones_row[:], mean_bf[:], start=True, stop=False,
                     skip_group_check=True)
            t.matmul(psM[:], ones_row[:], mlo_bf[:], start=False, stop=True,
                     skip_group_check=True)
            t.matmul(psR[:], ones_row[:], rstd_bf[:], start=True, stop=False,
                     skip_group_check=True)
            bcast_done[tb] = p.sP.inc(
                t.matmul(psR[:], ones_row[:], rlo_bf[:], start=False, stop=True,
                         skip_group_check=True))
        p.TEN(t2_sec)

        for cc in range(NCC):
            def v3_sec(v, tb=tb, cc=cc):
                if cc == 0:
                    v.wait_ge(p.sP.h, bcast_done[tb])
                    if tb >= 2:
                        v.wait_ge(p.sS.h, store_done[tb - 2])
                src = x_fm[:, cc * T + tb * TB: cc * T + tb * TB + TB]
                v.tensor_tensor(tsub[:], src, psM[:], ALU.subtract)
                norm_done[(tb, cc)] = p.sV.inc(
                    v.tensor_tensor(xn_t[tb % 2][:, cc * TB:(cc + 1) * TB],
                                    tsub[:], psR[:], ALU.mult))
            p.VEC(v3_sec)

        def s_sec(s, tb=tb):
            s.wait_ge(p.sV.h, norm_done[(tb, NCC - 1)])
            store_done[tb] = p.sS.inc(
                s.dma_start(out=env["xn_scr_v"][:, :, tb * TB:(tb + 1) * TB],
                            in_=xn_t[tb % 2].rearrange("p (cc t) -> p cc t", cc=NCC)), 16)
        p.SYN(s_sec)

    env["xn_ready"] = dict(store_done)
    p.pop()


def emit_l0_attn(p, env):
    nc = p.nc
    x_fm = env["x_fm"]
    xn_scr_v = env["xn_scr_v"]
    ident, pswap = env["ident"], env["pswap"]
    xn_ready = env["xn_ready"]
    m_d = p.inp("cmask", [128, 4 * TB], BF16)

    wts = {nm: p.inp(f"l0{nm}", [C, C], F32) for nm in ("q1", "k1", "q2", "k2", "v", "o")}
    wview = {nm: a.rearrange("(cc p) o -> p cc o", p=128) for nm, a in wts.items()}
    ctab_d = p.inp("rope_c", [128, T], BF16)
    stab_d = p.inp("rope_s", [128, T], BF16)

    y_scr = nc.dram_tensor("y_scr", [C, T], BF16, kind="Internal").ap()
    y_scr_g = y_scr.rearrange("(g p) t -> g p t", p=128)
    y_scr_c = y_scr.rearrange("(cc p) t -> p cc t", p=128)
    q_scr = {nm: nc.dram_tensor(f"q_scr_{nm}", [128, T], BF16, kind="Internal").ap()
             for nm in ("q1", "q2")}

    names = ("q1", "k1", "q2", "k2", "v")

    p.push()
    masks = p.sb("l0a_masks", [128, 4 * TB], BF16)
    kk = {nm: p.sb(f"l0a_{nm}", [128, T], BF16) for nm in ("k1", "k2")}
    v_tm = p.sb("l0a_vtm", [128, NKT * 128], BF16)
    wbuf = {nm: p.sb(f"l0a_w_{nm}", [128, NCC * 128], BF16) for nm in names}
    xn_t = [p.sb(f"l0a_xnt{i}", [128, NCC * TB], BF16) for i in range(2)]
    cbuf = p.sb("l0a_cbuf", [128, TB], BF16)
    sbuf_t = p.sb("l0a_sbuf", [128, TB], BF16)
    ev = [p.sb(f"l0a_ev{i}", [128, TB], BF16) for i in range(2)]
    rope_t1 = p.sb("l0a_rt1", [128, TB], BF16)
    rope_t2 = p.sb("l0a_rt2", [128, TB], BF16)
    qob = [p.sb(f"l0a_qob{i}", [128, TB], BF16) for i in range(2)]
    qw_t = {nm: p.sb(f"l0a_qw_{nm}", [128, TB], BF16) for nm in ("q1", "q2")}
    s1m = [p.sb(f"l0a_s1m{i}", [128, TB], BF16) for i in range(4)]
    pat = [p.sb(f"l0a_pat{i}", [128, TB], BF16) for i in range(4)]
    ybuf = p.sb("l0a_ybuf", [128, TB], BF16)
    psyA = p.ps("l0a_psyA", [128, TB])
    psyB = p.ps("l0a_psyB", [128, TB])

    mk = {
        "ev_free": [0, 0],
        "ev_free_eng": ["V", "V"],
        "psp_free": [0, 0],
        "rope_dve": 0,
        "rope_tb": 0,
        "vtr_ev": 0,
        "qob_st": [0, 0],
        "s1ev_a": 0, "s1ev_v": 0,
        "patdve": 0,
        "av": 0,
        "yev": 0,
        "yst": 0,
        "qw_used": 0,
    }
    bar = p.snap()

    proj_tb_done = {}
    proj_g_done = {}
    q_store = {}

    for g in range(NCC):
        p.push()
        psp = [p.ps(f"l0a_psp_g{g}", [128, TB])]
        pssw = p.ps(f"l0a_pssw_g{g}", [128, TB])
        pstr = p.ps(f"l0a_pstr_g{g}", [128, 128], BF16)
        prev_pat_g = mk["patdve"]
        prev_yev_g = mk["yev"]

        def gw_sec(gq, g=g):
            if g == 0:
                p.wait_snap(gq, bar)
                p.sG.inc(gq.dma_start(out=masks[:], in_=m_d), 16)
            if g >= 1:
                gq.wait_ge(p.sP.h, proj_g_done[g - 1])
            for nm in names:
                p.sG.inc(gq.dma_start(out=wbuf[nm][:],
                                      in_=wview[nm][:, :, g * 128:(g + 1) * 128]), 16)
        p.GPS(gw_sec)
        wload_n = p.sG.n

        for tb in range(NTB):
            def s_sec(s, g=g, tb=tb):
                if g == 0 and tb == 0:
                    p.wait_snap(s, bar)
                if g == 0:
                    s.wait_ge(p.sS.h, xn_ready[tb])
                key = (g, tb - 2) if tb >= 2 else (g - 1, NTB - 2 + tb)
                if key in proj_tb_done:
                    s.wait_ge(p.sP.h, proj_tb_done[key])
                p.sS.inc(s.dma_start(
                    out=xn_t[tb % 2].rearrange("p (cc t) -> p cc t", cc=NCC),
                    in_=xn_scr_v[:, :, tb * TB:(tb + 1) * TB]), 16)
                if mk["rope_tb"]:
                    s.wait_ge(p.sV.h, mk["rope_tb"])
                p.sS.inc(s.dma_start(out=cbuf[:], in_=ctab_d[:, tb * TB:(tb + 1) * TB]), 16)
                p.sS.inc(s.dma_start(out=sbuf_t[:], in_=stab_d[:, tb * TB:(tb + 1) * TB]), 16)
            p.SYN(s_sec)
            xn_n = p.sS.n

            for pi, nm in enumerate(names):
                def t_sec(t, g=g, tb=tb, pi=pi, nm=nm, xn_n=xn_n, wload_n=wload_n,
                          prev_pat_g=prev_pat_g, prev_yev_g=prev_yev_g):
                    if pi == 0:
                        t.wait_ge(p.sS.h, xn_n)
                        t.wait_ge(p.sG.h, wload_n)
                    if g > 0 and tb == 0 and pi == 0:
                        if prev_pat_g:
                            t.wait_ge(p.sV.h, prev_pat_g)
                        if prev_yev_g:
                            t.wait_ge(p.sA.h, prev_yev_g)
                    if mk["psp_free"][0]:
                        t.wait_ge(p.sA.h, mk["psp_free"][0])
                    for cc in range(NCC):
                        mm = t.matmul(psp[0][:], wbuf[nm][:, cc * 128:(cc + 1) * 128],
                                      xn_t[tb % 2][:, cc * TB:(cc + 1) * TB],
                                      start=(cc == 0), stop=(cc == NCC - 1))
                    p.sP.inc(mm)
                p.TEN(t_sec)
                mm_n = p.sP.n

                def a_sec(s, pi=pi, mm_n=mm_n):
                    s.wait_ge(p.sP.h, mm_n)
                    eng, cnt = mk["ev_free_eng"][pi % 2], mk["ev_free"][pi % 2]
                    if cnt:
                        s.wait_ge((p.sV if eng == "V" else p.sP).h, cnt)
                    mk["psp_free"][0] = p.sA.inc(
                        s.activation(ev[pi % 2][:], psp[0][:], AF.Copy))
                p.SCA(a_sec)
                ev_n = p.sA.n

                if nm == "v":
                    for j in range(4):
                        def t3_sec(t, j=j, ev_n=ev_n, pi=pi):
                            if j == 0:
                                t.wait_ge(p.sA.h, ev_n)
                            if mk["vtr_ev"]:
                                t.wait_ge(p.sV.h, mk["vtr_ev"])
                            p.sP.inc(t.transpose(pstr[:],
                                                 ev[pi % 2][:, j * 128:(j + 1) * 128],
                                                 ident[:]))
                        p.TEN(t3_sec)
                        tr_n = p.sP.n

                        def v_sec(v, tb=tb, j=j, tr_n=tr_n):
                            v.wait_ge(p.sP.h, tr_n)
                            kt = tb * 4 + j
                            mk["vtr_ev"] = p.sV.inc(
                                v.tensor_copy(v_tm[:, kt * 128:(kt + 1) * 128], pstr[:]))
                        p.VEC(v_sec)
                    mk["ev_free"][pi % 2] = mk["vtr_ev"]
                    mk["ev_free_eng"][pi % 2] = "V"
                else:
                    def t2_sec(t, ev_n=ev_n, pi=pi):
                        t.wait_ge(p.sA.h, ev_n)
                        if mk["rope_dve"]:
                            t.wait_ge(p.sV.h, mk["rope_dve"])
                        p.sP.inc(t.matmul(pssw[:], pswap[:], ev[pi % 2][:],
                                          start=True, stop=True))
                    p.TEN(t2_sec)
                    sw_n = p.sP.n

                    is_q = nm in ("q1", "q2")
                    qi = 0 if nm == "q1" else 1

                    def v_sec(v, tb=tb, nm=nm, pi=pi, sw_n=sw_n, is_q=is_q, qi=qi):
                        v.wait_ge(p.sP.h, sw_n)
                        if is_q and mk["qob_st"][qi]:
                            v.wait_ge(p.sS.h, mk["qob_st"][qi])
                        cs = slice(tb * TB, (tb + 1) * TB)
                        v.tensor_tensor(rope_t1[:], ev[pi % 2][:], cbuf[:], ALU.mult)
                        v.tensor_tensor(rope_t2[:], pssw[:], sbuf_t[:], ALU.mult)
                        dst = qob[qi][:] if is_q else kk[nm][:, cs]
                        mk["rope_dve"] = p.sV.inc(
                            v.tensor_tensor(dst, rope_t1[:], rope_t2[:], ALU.add))
                    p.VEC(v_sec)
                    mk["ev_free"][pi % 2] = mk["rope_dve"]
                    mk["ev_free_eng"][pi % 2] = "V"

                    if is_q:
                        rn = p.sV.n

                        def sq_sec(s, g=g, tb=tb, nm=nm, qi=qi, rn=rn):
                            s.wait_ge(p.sV.h, rn)
                            mk["qob_st"][qi] = p.sS.inc(
                                s.dma_start(out=q_scr[nm][:, tb * TB:(tb + 1) * TB],
                                            in_=qob[qi][:]), 16)
                        p.SYN(sq_sec)
                        q_store[(g, nm, tb)] = p.sS.n
            mk["rope_tb"] = mk["rope_dve"]
            proj_tb_done[(g, tb)] = p.sP.n
        proj_g_done[g] = p.sP.n
        dve_fin = p.sV.n
        psp_ev_fin = mk["psp_free"][0]
        p.pop()
        p.push()
        pss = [p.ps(f"l0a_pss_g{g}_{i}", [128, TB]) for i in range(4)]

        for qw in range(NTB):
            def sw_sec(s, g=g, qw=qw):
                s.wait_ge(p.sS.h, q_store[(g, "q1", qw)])
                s.wait_ge(p.sS.h, q_store[(g, "q2", qw)])
                if mk["qw_used"]:
                    s.wait_ge(p.sP.h, mk["qw_used"])
                p.sS.inc(s.dma_start(out=qw_t["q1"][:],
                                     in_=q_scr["q1"][:, qw * TB:(qw + 1) * TB]), 16)
                p.sS.inc(s.dma_start(out=qw_t["q2"][:],
                                     in_=q_scr["q2"][:, qw * TB:(qw + 1) * TB]), 16)
            p.SYN(sw_sec)
            qw_n = p.sS.n

            nkt = 4 * qw + 4
            ndiag = 4
            noff = nkt - ndiag
            wpb = p.sP.n
            wvb = p.sV.n
            wab = p.sA.n
            patdve_pred = lambda kt: (wvb + kt + 1 if kt < noff
                                      else wvb + noff + 2 * (kt - noff) + 2)
            prev_patdve = mk["patdve"]
            prev_yst = mk["yst"]
            prev_yev = mk["yev"]

            def a_win(s, g=g, qw=qw, wpb=wpb, noff=noff, nkt=nkt,
                      prev_patdve=prev_patdve, prev_yst=prev_yst):
                for kt in range(noff):
                    pr = kt % 2
                    s.wait_ge(p.sP.h, wpb + 3 * kt + 1)
                    if kt < 2:
                        if prev_patdve:
                            s.wait_ge(p.sV.h, prev_patdve)
                    else:
                        s.wait_ge(p.sV.h, patdve_pred(kt - 2))
                    s.activation(s1m[2 * pr][:], pss[2 * pr][:], AF.Copy,
                                 scale=1.0 / 4096.0)
                    p.sA.inc(s.activation(s1m[2 * pr + 1][:], pss[2 * pr + 1][:],
                                          AF.Copy, scale=1.0 / 4096.0))
                s.wait_ge(p.sP.h, wpb + 3 * nkt)
                if prev_yst:
                    s.wait_ge(p.sS.h, prev_yst)
                s.activation(ybuf[0:64, :], psyA[0:64, :], AF.Copy)
                p.sA.inc(s.activation(ybuf[64:128, :], psyB[64:128, :], AF.Copy))
            p.SCA(a_win)
            mk["s1ev_a"] = wab + noff
            mk["yev"] = wab + noff + 1
            yev_n = mk["yev"]

            maskev_pred = lambda kt: patdve_pred(kt) - 1
            prev_av = mk["av"]

            def t_win(t, g=g, qw=qw, nkt=nkt, noff=noff, wpb=wpb, wab=wab,
                      dve_fin=dve_fin, qw_n=qw_n, prev_patdve=prev_patdve,
                      prev_yev=prev_yev, psp_ev_fin=psp_ev_fin):
                for kt in range(nkt):
                    pr = kt % 2
                    diag_r = kt - 4 * qw
                    ks = slice(kt * 128, (kt + 1) * 128)
                    if qw == 0 and kt == 0:
                        t.wait_ge(p.sV.h, dve_fin)
                        if psp_ev_fin:
                            t.wait_ge(p.sA.h, psp_ev_fin)
                    if kt == 0:
                        t.wait_ge(p.sS.h, qw_n)
                    if kt < 2:
                        if prev_patdve:
                            t.wait_ge(p.sV.h, prev_patdve)
                    else:
                        t.wait_ge(p.sV.h, patdve_pred(kt - 2))
                    t.matmul(pss[2 * pr][:], kk["k1"][0:64, ks], qw_t["q1"][0:64, :],
                             start=True, stop=True, tile_position=(0, 0))
                    mm = t.matmul(pss[2 * pr + 1][:], kk["k1"][64:128, ks],
                                  qw_t["q1"][64:128, :],
                                  start=True, stop=True, tile_position=(64, 0))
                    p.sP.inc(mm)
                    if diag_r < 0:
                        t.wait_ge(p.sA.h, wab + kt + 1)
                    else:
                        t.wait_ge(p.sV.h, maskev_pred(kt))
                    t.matmul(pss[2 * pr][:], kk["k2"][0:64, ks], qw_t["q2"][0:64, :],
                             start=True, stop=True, tile_position=(0, 0))
                    mm = t.matmul(pss[2 * pr + 1][:], kk["k2"][64:128, ks],
                                  qw_t["q2"][64:128, :],
                                  start=True, stop=True, tile_position=(64, 0))
                    mk["qw_used"] = p.sP.inc(mm)
                    t.wait_ge(p.sV.h, patdve_pred(kt))
                    if kt == 0 and prev_yev:
                        t.wait_ge(p.sA.h, prev_yev)
                    t.matmul(psyA[0:64, :], v_tm[:, kt * 128: kt * 128 + 64],
                             pat[2 * pr][:],
                             start=(kt == 0), stop=(kt == nkt - 1), tile_position=(0, 0),
                             skip_group_check=True)
                    mm = t.matmul(psyB[64:128, :],
                                  v_tm[:, kt * 128 + 64: kt * 128 + 128],
                                  pat[2 * pr + 1][:], start=(kt == 0),
                                  stop=(kt == nkt - 1),
                                  tile_position=(0, 64), skip_group_check=True)
                    mk["av"] = p.sP.inc(mm)
            p.TEN(t_win)

            def v_win(v, g=g, qw=qw, nkt=nkt, noff=noff, wpb=wpb,
                      prev_patdve=prev_patdve, prev_av=prev_av):
                for kt in range(nkt):
                    pr = kt % 2
                    diag_r = kt - 4 * qw
                    if diag_r >= 0:
                        v.wait_ge(p.sP.h, wpb + 3 * kt + 1)
                        if kt < 2 and prev_patdve:
                            v.wait_ge(p.sV.h, prev_patdve)
                        ms = masks[:, diag_r * TB:(diag_r + 1) * TB]
                        v.tensor_tensor(s1m[2 * pr][:], pss[2 * pr][:], ms, ALU.mult)
                        p.sV.inc(v.tensor_tensor(s1m[2 * pr + 1][:],
                                                 pss[2 * pr + 1][:], ms, ALU.mult))
                    v.wait_ge(p.sP.h, wpb + 3 * kt + 2)
                    if kt >= 2:
                        v.wait_ge(p.sP.h, wpb + 3 * (kt - 2) + 3)
                    elif prev_av > 0:
                        v.wait_ge(p.sP.h, prev_av)
                    v.tensor_tensor(pat[2 * pr][:], pss[2 * pr][:], s1m[2 * pr][:],
                                    ALU.mult)
                    mk["patdve"] = p.sV.inc(
                        v.tensor_tensor(pat[2 * pr + 1][:], pss[2 * pr + 1][:],
                                        s1m[2 * pr + 1][:], ALU.mult))
            p.VEC(v_win)
            y_n = p.sP.n

            def s2_sec(s, g=g, qw=qw, yev_n=yev_n):
                s.wait_ge(p.sA.h, yev_n)
                mk["yst"] = p.sS.inc(
                    s.dma_start(out=y_scr_g[g][:, qw * TB:(qw + 1) * TB], in_=ybuf[:]), 16)
            p.SYN(s2_sec)
        p.pop()

    p.pop()
    p.push()
    y_all = p.sS.n
    wout_bar = p.snap()
    wo_buf = [p.sb(f"l0a_wo{i}", [128, NCC * 128], BF16) for i in range(2)]
    y_t = [p.sb(f"l0a_yt{i}", [128, NCC * TB], BF16) for i in range(2)]
    psp2 = [p.ps(f"l0a_psp2_{i}", [128, TB]) for i in range(2)]
    wo_mm = {}
    res_add = {}
    wo_used = {}
    for oc in range(NCC):
        def g3_sec(gq, oc=oc):
            if oc == 0:
                p.wait_snap(gq, wout_bar)
            if oc >= 2:
                gq.wait_ge(p.sP.h, wo_used[oc - 2])
            p.sG.inc(gq.dma_start(out=wo_buf[oc % 2][:],
                                  in_=wview["o"][:, :, oc * 128:(oc + 1) * 128]), 16)
        p.GPS(g3_sec)
        won = p.sG.n
        for tb in range(NTB):
            idx = oc * NTB + tb

            def s_sec(s, oc=oc, tb=tb, idx=idx):
                if idx == 0:
                    p.wait_snap(s, wout_bar)
                    s.wait_ge(p.sS.h, y_all)
                if idx >= 2:
                    s.wait_ge(p.sP.h, wo_mm[idx - 2])
                p.sS.inc(s.dma_start(
                    out=y_t[tb % 2].rearrange("p (cc t) -> p cc t", cc=NCC),
                    in_=y_scr_c[:, :, tb * TB:(tb + 1) * TB]), 16)
            p.SYN(s_sec)
            yt_n = p.sS.n

            def t_sec(t, oc=oc, tb=tb, idx=idx, yt_n=yt_n, won=won):
                t.wait_ge(p.sS.h, yt_n)
                t.wait_ge(p.sG.h, won)
                if idx < 2:
                    p.wait_snap(t, wout_bar)
                else:
                    t.wait_ge(p.sV.h, res_add[idx - 2])
                for cc in range(NCC):
                    mm = t.matmul(psp2[idx % 2][:], wo_buf[oc % 2][:, cc * 128:(cc + 1) * 128],
                                  y_t[tb % 2][:, cc * TB:(cc + 1) * TB],
                                  start=(cc == 0), stop=(cc == NCC - 1))
                wo_mm[idx] = p.sP.inc(mm)
            p.TEN(t_sec)
            mm_n = p.sP.n

            def v_sec(v, oc=oc, tb=tb, idx=idx, mm_n=mm_n):
                v.wait_ge(p.sP.h, mm_n)
                xs = x_fm[:, oc * T + tb * TB: oc * T + tb * TB + TB]
                res_add[idx] = p.sV.inc(
                    v.tensor_tensor(xs, xs, psp2[idx % 2][:], ALU.add))
            p.VEC(v_sec)
        wo_used[oc] = p.sP.n
    p.pop()


def emit_l0_ffn(p, env):
    x_fm = env["x_fm"]
    xn_scr_v = env["xn_scr_v"]
    xn_ready = env["xn_ready"]

    wr_d = p.inp("l0fr", [C, C], F32).rearrange("(cc p) o -> p cc o", p=128)
    wk_d = p.inp("l0fk", [C, C], F32).rearrange("(cc p) o -> p cc o", p=128)
    wv_d = p.inp("l0fv", [C, C], F32).rearrange("(cc p) o -> p cc o", p=128)
    mur_d = p.inp("l0f_mur", [128, NCC], F32)
    muk_d = p.inp("l0f_muk", [128, NCC], F32)

    bar = p.snap()
    p.push()
    wr = p.sb("l0f_wr", [128, NCC * C], BF16)
    wk = p.sb("l0f_wk", [128, NCC * C], BF16)
    wvb = [p.sb(f"l0f_wv{i}", [128, NCC * 128], BF16) for i in range(2)]
    mur = p.sb("l0f_mur_t", [128, NCC], F32)
    muk = p.sb("l0f_muk_t", [128, NCC], F32)
    xbuf = p.sb("l0f_xbuf", [128, NCC * (TB + 1)], BF16)
    mix1 = p.sb("l0f_mix1", [128, NCC * TB], BF16)
    mix2 = p.sb("l0f_mix2", [128, NCC * TB], BF16)
    r_t = p.sb("l0f_rt", [128, NCC * TB], BF16)
    dbuf = [p.sb("l0f_d0", [128, TB], F32)]
    tbuf = [p.sb(f"l0f_t{i}", [128, TB], F32) for i in range(2)]
    krelu = [p.sb(f"l0f_krelu{i}", [128, TB], BF16) for i in range(2)]
    obuf = p.sb("l0f_obuf", [128, TB], F32)
    psm = [p.ps(f"l0f_ps{i}", [128, TB]) for i in range(2)]

    xb_v = xbuf.rearrange("p (cc j) -> p cc j", j=TB + 1)

    def w_sec(gq):
        p.wait_snap(gq, bar)
        p.sG.inc(gq.dma_start(out=wr.rearrange("p (cc o) -> p cc o", cc=NCC), in_=wr_d), 16)
        p.sG.inc(gq.dma_start(out=wk.rearrange("p (cc o) -> p cc o", cc=NCC), in_=wk_d), 16)
    p.GPS(w_sec)
    wres_n = p.sG.n

    def mu_sec(s):
        p.wait_snap(s, bar)
        p.sS.inc(s.dma_start(out=mur[:], in_=mur_d), 16)
        p.sS.inc(s.dma_start(out=muk[:], in_=muk_d), 16)
    p.SYN(mu_sec)
    mu_n = p.sS.n

    mk = {"ev": {}, "kv_done": {}, "wv_used": {}, "mix_done": {}, "resadd": {}}
    prev_tb_kv = 0
    prev_tb_mm = 0

    for tb in range(NTB):
        def s_sec(s, tb=tb, prev=prev_tb_kv):
            if tb == 0:
                p.wait_snap(s, bar)
                s.wait_ge(p.sS.h, mu_n)
            s.wait_ge(p.sS.h, xn_ready[tb])
            if prev:
                s.wait_ge(p.sV.h, prev)
            if tb == 0:
                p.sS.inc(s.dma_start(out=xb_v[:, :, 1:TB + 1],
                                     in_=xn_scr_v[:, :, 0:TB]), 16)
            else:
                p.sS.inc(s.dma_start(out=xb_v[:, :, 0:TB + 1],
                                     in_=xn_scr_v[:, :, tb * TB - 1: (tb + 1) * TB]), 16)
        p.SYN(s_sec)
        xb_n = p.sS.n

        if tb == 0:
            def z_sec(v):
                v.wait_ge(p.sS.h, xb_n)
                p.sV.inc(v.memset(xb_v[:, :, 0:1], 0.0))
            p.VEC(z_sec)
        zero_n = p.sV.n

        mix_marks = []
        amarks = {}
        for cc in range(NCC):
            def v_sec(v, tb=tb, cc=cc, xb_n=xb_n, zero_n=zero_n):
                if cc == 0:
                    v.wait_ge(p.sS.h, xb_n)
                    if tb == 0:
                        v.wait_ge(p.sV.h, zero_n)
                    if tb >= 1:
                        v.wait_ge(p.sP.h, prev_tb_mm)
                if (cc - 1) in amarks:
                    v.wait_ge(p.sA.h, amarks[cc - 1])
                cur = xb_v[:, cc, 1:TB + 1]
                prv = xb_v[:, cc, 0:TB]
                p.sV.inc(v.tensor_tensor(dbuf[0][:], cur, prv, ALU.subtract))
            p.VEC(v_sec)
            d_n = p.sV.n

            def a_sec(s, cc=cc, d_n=d_n):
                s.wait_ge(p.sV.h, d_n)
                s.activation(tbuf[0][:], dbuf[0][:], AF.Copy,
                             scale=mur[:, cc:cc + 1])
                amarks[cc] = p.sA.inc(s.activation(tbuf[1][:], dbuf[0][:], AF.Copy,
                                      scale=muk[:, cc:cc + 1]))
            p.SCA(a_sec)
            t_n = p.sA.n

            def v2_sec(v, cc=cc, t_n=t_n):
                v.wait_ge(p.sA.h, t_n)
                cs = slice(cc * TB, (cc + 1) * TB)
                prv = xb_v[:, cc, 0:TB]
                v.tensor_tensor(mix1[:, cs], tbuf[0][:], prv, ALU.add)
                p.sV.inc(v.tensor_tensor(mix2[:, cs], tbuf[1][:], prv, ALU.add))
            p.VEC(v2_sec)
            mix_marks.append(p.sV.n)
        mix_n = p.sV.n

        rpb = p.sP.n
        rab = p.sA.n

        def a_r(s, tb=tb, rpb=rpb):
            for oc in range(NCC):
                s.wait_ge(p.sP.h, rpb + oc + 1)
                p.sA.inc(s.activation(r_t[:, oc * TB:(oc + 1) * TB],
                                      psm[oc % 2][:], AF.Sigmoid))
        p.SCA(a_r)

        for oc in range(NCC):
            def t_sec(t, tb=tb, oc=oc, mix_n=mix_n, rab=rab):
                if oc == 0:
                    t.wait_ge(p.sV.h, mix_n)
                    if tb == 0:
                        t.wait_ge(p.sG.h, wres_n)
                if oc >= 2:
                    t.wait_ge(p.sA.h, rab + oc - 1)
                for cc in range(NCC):
                    mm = t.matmul(psm[oc % 2][:], wr[:, cc * C + oc * 128: cc * C + oc * 128 + 128],
                                  mix1[:, cc * TB:(cc + 1) * TB],
                                  start=(cc == 0), stop=(cc == NCC - 1))
                p.sP.inc(mm)
            p.TEN(t_sec)
        r_mm_all = p.sP.n
        r_last_ev = rab + NCC

        kpb = p.sP.n
        kab = p.sA.n
        kvb = p.sV.n

        def a_k(s, tb=tb, kpb=kpb, kvb=kvb):
            for oc in range(NCC):
                s.wait_ge(p.sP.h, kpb + oc + 1)
                if oc >= 2:
                    s.wait_ge(p.sV.h, kvb + oc - 1)
                p.sA.inc(s.activation(krelu[oc % 2][:], psm[oc % 2][:], AF.Relu))
        p.SCA(a_k)

        for oc in range(NCC):
            def t_sec(t, oc=oc, r_last_ev=r_last_ev, kab=kab):
                if oc < 2:
                    t.wait_ge(p.sA.h, r_last_ev)
                if oc >= 2:
                    t.wait_ge(p.sA.h, kab + oc - 1)
                for cc in range(NCC):
                    mm = t.matmul(psm[oc % 2][:], wk[:, cc * C + oc * 128: cc * C + oc * 128 + 128],
                                  mix2[:, cc * TB:(cc + 1) * TB],
                                  start=(cc == 0), stop=(cc == NCC - 1))
                p.sP.inc(mm)
            p.TEN(t_sec)

            def v_sec(v, oc=oc, kab=kab):
                v.wait_ge(p.sA.h, kab + oc + 1)
                v.wait_ge(p.sP.h, r_mm_all)
                p.sV.inc(v.tensor_tensor(mix1[:, oc * TB:(oc + 1) * TB],
                                         krelu[oc % 2][:], krelu[oc % 2][:], ALU.mult))
            p.VEC(v_sec)
        kact_n = p.sV.n

        kv_mm = {}
        for oc in range(NCC):
            def g_sec(gq, tb=tb, oc=oc):
                if (tb * NCC + oc) >= 2:
                    gq.wait_ge(p.sP.h, mk["wv_used"][(tb * NCC + oc) - 2])
                elif tb == 0 and oc == 0:
                    p.wait_snap(gq, bar)
                p.sG.inc(gq.dma_start(out=wvb[oc % 2].rearrange("p (cc o) -> p cc o", cc=NCC),
                                      in_=wv_d[:, :, oc * 128:(oc + 1) * 128]), 16)
            p.GPS(g_sec)
            wv_n = p.sG.n

            def t_sec(t, tb=tb, oc=oc, wv_n=wv_n, kact_n=kact_n):
                if oc == 0:
                    t.wait_ge(p.sV.h, kact_n)
                t.wait_ge(p.sG.h, wv_n)
                if (oc - 2) in kv_mm:
                    t.wait_ge(p.sV.h, mk["resadd"][(tb, oc - 2)])
                for cc in range(NCC):
                    mm = t.matmul(psm[oc % 2][:], wvb[oc % 2][:, cc * 128:(cc + 1) * 128],
                                  mix1[:, cc * TB:(cc + 1) * TB],
                                  start=(cc == 0), stop=(cc == NCC - 1))
                kv_mm[oc] = p.sP.inc(mm)
                mk["wv_used"][tb * NCC + oc] = kv_mm[oc]
            p.TEN(t_sec)
            kvm_n = p.sP.n

            def v_sec(v, tb=tb, oc=oc, kvm_n=kvm_n):
                v.wait_ge(p.sP.h, kvm_n)
                v.tensor_tensor(obuf[:], psm[oc % 2][:], r_t[:, oc * TB:(oc + 1) * TB],
                                ALU.mult)
                xs = x_fm[:, oc * T + tb * TB: oc * T + tb * TB + TB]
                mk["resadd"][(tb, oc)] = p.sV.inc(
                    v.tensor_tensor(xs, xs, obuf[:], ALU.add))
            p.VEC(v_sec)
        prev_tb_kv = p.sV.n
        prev_tb_mm = p.sP.n
    p.pop()


def emit_l1_attn(p, env):
    x_fm = env["x_fm"]
    xn_scr_v = env["xn_scr_v"]
    xn_ready = env["xn_ready"]
    ident = env["ident"]
    idsh = env["idsh"]
    ones_col = env["ones_col"]
    ones_row = env["ones_row"]

    wq_d = p.inp("l1q", [C, C], F32).rearrange("(cc p) o -> p cc o", p=128)
    wk_d = p.inp("l1k", [C, C], F32).rearrange("(cc p) o -> p cc o", p=128)
    wv_d = p.inp("l1v", [C, C], F32).rearrange("(cc p) o -> p cc o", p=128)
    wview = {"q": wq_d, "k": wk_d, "v": wv_d}

    bar = p.snap()
    p.push()
    qf = p.sb("l1a_qf", [128, T], BF16)
    kf = p.sb("l1a_kf", [128, T], BF16)
    v_tm = p.sb("l1a_vtm", [128, NKT * 130], BF16)
    v_tm_v = v_tm.rearrange("p (tt c) -> p tt c", c=130)
    wbuf = {nm: p.sb(f"l1a_w_{nm}", [128, NCC * 128], BF16) for nm in ("q", "k", "v")}
    xn_t = [p.sb(f"l1a_xnt{i}", [128, NCC * TB], BF16) for i in range(2)]
    ev = [p.sb(f"l1a_ev{i}", [128, TB], BF16) for i in range(2)]
    sqb = p.sb("l1a_sq", [128, T], BF16)
    nmrow = p.sb("l1a_nmrow", [1, 16], F32)
    smalls = p.sb("l1a_smalls", [1, 16], F32)
    neg_bf = p.sb("l1a_negbf", [1, 2], BF16)
    negc = [p.sb(f"l1a_negc{i}", [128, 1], F32) for i in range(3)]
    eA = [p.sb(f"l1a_eA{i}", [128, TB], BF16) for i in range(2)]
    eB = [p.sb(f"l1a_eB{i}", [128, TB], BF16) for i in range(2)]
    rec_f = p.sb("l1a_recf", [1, TB], F32)
    rec_bf = p.sb("l1a_recbf", [1, TB], BF16)
    rb_sb = p.sb("l1a_rb", [64, TB], F32)
    ybfA = p.sb("l1a_ybfA", [64, TB], BF16)
    ybfB = p.sb("l1a_ybfB", [64, TB], BF16)
    psp = p.ps("l1a_psp", [128, TB])
    pstr = p.ps("l1a_pstr", [128, 128], BF16)
    psA = p.ps("l1a_psA", [128, TB])
    psB = p.ps("l1a_psB", [128, TB])
    psYA = p.ps("l1a_psYA", [65, TB])
    psYB = p.ps("l1a_psYB", [65, TB])
    psN = p.ps("l1a_psN", [1, TB])
    psU = p.ps("l1a_psU", [128, TB])

    mk = {"psp_free": 0, "ev_free": [0, 0], "ev_eng": ["V", "V"], "vtr_ev": 0,
          "sA_ev": 0, "sB_ev": 0, "avA": 0, "avB": 0, "y_done": 0,
          "scores_done": 0, "psN_free": 0, "psC1_free": 0}

    proj_tb_done = {}
    proj_g_done = {}

    for g in range(NCC):
        def gw_sec(gq, g=g):
            if g == 0:
                p.wait_snap(gq, bar)
            if g >= 1:
                gq.wait_ge(p.sP.h, proj_g_done[g - 1])
            for nm in ("q", "k", "v"):
                p.sG.inc(gq.dma_start(out=wbuf[nm][:],
                                      in_=wview[nm][:, :, g * 128:(g + 1) * 128]), 16)
        p.GPS(gw_sec)
        wload_n = p.sG.n

        def ones_sec(v, g=g):
            if mk["scores_done"]:
                v.wait_ge(p.sP.h, mk["scores_done"])
            p.sV.inc(v.memset(v_tm_v[:, :, 64:130:65], 1.0))
        p.VEC(ones_sec)
        vones_n = p.sV.n

        gpb = p.sP.n
        gab = p.sA.n
        prev_scores = mk["scores_done"]
        prev_pspf = mk["psp_free"]

        def a_proj(s, g=g, gpb=gpb, prev_scores=prev_scores):
            for tb in range(NTB):
                for pi in range(3):
                    s.wait_ge(p.sP.h, gpb + tb * 7 + pi + 1)
                    if tb == 0 and pi == 0 and prev_scores:
                        s.wait_ge(p.sP.h, prev_scores)
                    if pi == 0:
                        dst = qf[:, tb * TB:(tb + 1) * TB]
                    elif pi == 1:
                        dst = kf[:, tb * TB:(tb + 1) * TB]
                    else:
                        dst = ev[0][:]
                    p.sA.inc(s.activation(dst, psp[:], AF.Copy))
        p.SCA(a_proj)
        mk["psp_free"] = gab + 3 * NTB

        for tb in range(NTB):
            def s_sec(s, g=g, tb=tb):
                if g == 0 and tb == 0:
                    p.wait_snap(s, bar)
                if g == 0:
                    s.wait_ge(p.sS.h, xn_ready[tb])
                key = (g, tb - 2) if tb >= 2 else (g - 1, NTB - 2 + tb)
                if key in proj_tb_done:
                    s.wait_ge(p.sP.h, proj_tb_done[key])
                p.sS.inc(s.dma_start(
                    out=xn_t[tb % 2].rearrange("p (cc t) -> p cc t", cc=NCC),
                    in_=xn_scr_v[:, :, tb * TB:(tb + 1) * TB]), 16)
            p.SYN(s_sec)
            xn_n = p.sS.n

            for pi, nm in enumerate(("q", "k", "v")):
                def t_sec(t, tb=tb, pi=pi, nm=nm, xn_n=xn_n, wload_n=wload_n,
                          gab=gab, prev_pspf=prev_pspf):
                    if pi == 0:
                        t.wait_ge(p.sS.h, xn_n)
                        t.wait_ge(p.sG.h, wload_n)
                    idx = tb * 3 + pi
                    if idx >= 1:
                        t.wait_ge(p.sA.h, gab + idx)
                    elif prev_pspf:
                        t.wait_ge(p.sA.h, prev_pspf)
                    for cc in range(NCC):
                        mm = t.matmul(psp[:], wbuf[nm][:, cc * 128:(cc + 1) * 128],
                                      xn_t[tb % 2][:, cc * TB:(cc + 1) * TB],
                                      start=(cc == 0), stop=(cc == NCC - 1))
                    p.sP.inc(mm)
                p.TEN(t_sec)

                if nm == "v":
                    ev_n = gab + tb * 3 + 3
                    for j in range(4):
                        def t3_sec(t, j=j, ev_n=ev_n, tb=tb):
                            if j == 0:
                                t.wait_ge(p.sA.h, ev_n)
                            if mk["vtr_ev"]:
                                t.wait_ge(p.sV.h, mk["vtr_ev"])
                            p.sP.inc(t.transpose(pstr[:],
                                                 ev[0][:, j * 128:(j + 1) * 128],
                                                 ident[:]))
                        p.TEN(t3_sec)
                        tr_n = p.sP.n

                        def v_sec(v, tb=tb, j=j, tr_n=tr_n, vones_n=vones_n):
                            v.wait_ge(p.sP.h, tr_n)
                            if tb == 0 and j == 0:
                                v.wait_ge(p.sV.h, vones_n)
                            kt = tb * 4 + j
                            dst = v_tm_v[:, kt, :].rearrange(
                                "p (two r) -> p two r", r=65)[:, :, 0:64]
                            mk["vtr_ev"] = p.sV.inc(
                                v.tensor_copy(dst,
                                    pstr[:].rearrange("p (two r) -> p two r", two=2)))
                        p.VEC(v_sec)
            proj_tb_done[(g, tb)] = p.sP.n
        proj_g_done[g] = p.sP.n

        def sq_sec(s, g=g):
            s.wait_ge(p.sA.h, mk["psp_free"])
            s.wait_ge(p.sP.h, proj_g_done[g])
            p.sA.inc(s.activation(sqb[:], qf[:], AF.Square))
        p.SCA(sq_sec)
        sqq_n = p.sA.n

        norm_res = {}
        for which, src_n in (("q", sqq_n), ("k", None)):
            if which == "k":
                def sq2_sec(s, g=g):
                    s.wait_ge(p.sV.h, norm_res[("q", 1)])
                    p.sA.inc(s.activation(sqb[:], kf[:], AF.Square))
                p.SCA(sq2_sec)
                src_n = p.sA.n
            for hi, base in ((0, 0), (1, 64)):
                for tb in range(NTB):
                    def tn_sec(t, hi=hi, base=base, tb=tb, src_n=src_n):
                        t.wait_ge(p.sA.h, src_n)
                        if mk["psN_free"]:
                            t.wait_ge(p.sV.h, mk["psN_free"])
                        p.sP.inc(t.matmul(psN[:], ones_col[base:base + 64, 0:1],
                                          sqb[base:base + 64, tb * TB:(tb + 1) * TB],
                                          start=True, stop=True,
                                          tile_position=(base, 0)))
                    p.TEN(tn_sec)
                    tn_n = p.sP.n

                    def vr_sec(v, which=which, hi=hi, tb=tb, tn_n=tn_n):
                        v.wait_ge(p.sP.h, tn_n)
                        mk["psN_free"] = p.sV.inc(
                            v.reduce_max(nmrow[:, tb:tb + 1], psN[:], axis=AX.X))
                    p.VEC(vr_sec)

                def vm_sec(v, which=which, hi=hi):
                    idx = {"q": 0, "k": 2}[which] + hi
                    norm_res[(which, hi)] = p.sV.inc(
                        v.reduce_max(smalls[:, idx:idx + 1], nmrow[:, 0:NTB], axis=AX.X))
                p.VEC(vm_sec)
        qkmax_n = p.sV.n

        def vc_sec(v, qkmax_n=qkmax_n):
            v.wait_ge(p.sV.h, qkmax_n)
            v.tensor_tensor(smalls[:, 4:5], smalls[:, 0:1], smalls[:, 2:3], ALU.mult)
            p.sV.inc(v.tensor_tensor(smalls[:, 5:6], smalls[:, 1:2], smalls[:, 3:4],
                                     ALU.mult))
        p.VEC(vc_sec)
        prod_n = p.sV.n

        def ac_sec(s, prod_n=prod_n):
            s.wait_ge(p.sV.h, prod_n)
            p.sA.inc(s.activation(smalls[:, 6:8], smalls[:, 4:6], AF.Sqrt))
        p.SCA(ac_sec)
        csr_n = p.sA.n

        def vc2_sec(v, csr_n=csr_n):
            v.wait_ge(p.sA.h, csr_n)
            v.tensor_scalar_mul(smalls[:, 8:10], smalls[:, 6:8], -0.125)
            p.sV.inc(v.tensor_copy(neg_bf[:], smalls[:, 8:10]))
        p.VEC(vc2_sec)
        nbf_n = p.sV.n

        def tc_sec(t, nbf_n=nbf_n):
            t.wait_ge(p.sV.h, nbf_n)
            if mk["psC1_free"]:
                t.wait_ge(p.sV.h, mk["psC1_free"])
            p.sP.inc(t.matmul(psU[:, 0:2], ones_row[:], neg_bf[:],
                              start=True, stop=True))
        p.TEN(tc_sec)
        tc_n = p.sP.n

        def vc3_sec(v, tc_n=tc_n):
            v.wait_ge(p.sP.h, tc_n)
            v.tensor_copy(negc[0][:], psU[:, 0:1])
            v.tensor_copy(negc[1][:], psU[:, 1:2])
            mk["psC1_free"] = p.sV.inc(
                v.tensor_tensor(negc[2][:], negc[0][:], negc[1][:], ALU.min))
        p.VEC(vc3_sec)
        negc_n = p.sV.n

        for qw in range(NTB):
            pbase = p.sP.n
            abase = p.sA.n

            def a_win(s, g=g, qw=qw, pbase=pbase):
                for kt in range(NKT):
                    s.wait_ge(p.sP.h, pbase + 2 * kt + 1)
                    s.activation(eA[kt % 2][:], psA[:], AF.Exp, scale=0.125,
                                 bias=negc[0][:])
                    p.sA.inc(s.activation(eB[kt % 2][:], psB[:], AF.Exp, scale=0.125,
                                          bias=negc[1][:]))
            p.SCA(a_win)
            mk["sA_ev"] = p.sA.n

            def t_win(t, g=g, qw=qw, negc_n=negc_n, pbase=pbase, abase=abase):
                for kt in range(NKT):
                    if qw == 0 and kt == 0:
                        t.wait_ge(p.sV.h, negc_n)
                    if kt >= 1:
                        t.wait_ge(p.sA.h, abase + kt)
                    qs = slice(qw * TB, (qw + 1) * TB)
                    ks = slice(kt * 128, (kt + 1) * 128)
                    t.matmul(psA[:], kf[0:64, ks], qf[0:64, qs],
                             start=True, stop=True, tile_position=(0, 0))
                    mm = t.matmul(psB[:], kf[64:128, ks], qf[64:128, qs],
                                  start=True, stop=True, tile_position=(64, 0))
                    p.sP.inc(mm)
                    t.wait_ge(p.sA.h, abase + kt + 1)
                    if kt == 0 and mk["y_done"]:
                        t.wait_ge(p.sV.h, mk["y_done"])
                    t.matmul(psYA[:], v_tm[:, kt * 130: kt * 130 + 65], eA[kt % 2][:],
                             start=(kt == 0), stop=(kt == NKT - 1), skip_group_check=True)
                    mm = t.matmul(psYB[:], v_tm[:, kt * 130 + 65: kt * 130 + 130],
                                  eB[kt % 2][:], start=(kt == 0), stop=(kt == NKT - 1),
                                  skip_group_check=True)
                    mk["avB"] = p.sP.inc(mm)
            p.TEN(t_win)
            y_n = p.sP.n

            for hi, psY, yb in ((0, psYA, ybfA), (1, psYB, ybfB)):
                def vr_sec(v, psY=psY, y_n=y_n):
                    v.wait_ge(p.sP.h, y_n)
                    v.reciprocal(rec_f[:], psY[64:65, :])
                    p.sV.inc(v.tensor_copy(rec_bf[:], rec_f[:]))
                p.VEC(vr_sec)
                rb_n = p.sV.n

                def tb_sec(t, rb_n=rb_n):
                    t.wait_ge(p.sV.h, rb_n)
                    if mk["psC1_free"]:
                        t.wait_ge(p.sV.h, mk["psC1_free"])
                    p.sP.inc(t.matmul(psU[0:64, :], ones_row[0:1, 0:64], rec_bf[:],
                                      start=True, stop=True))
                p.TEN(tb_sec)
                tb_n = p.sP.n

                def vy_sec(v, psY=psY, yb=yb, tb_n=tb_n):
                    v.wait_ge(p.sP.h, tb_n)
                    v.tensor_copy(rb_sb[:], psU[0:64, :])
                    mk["psC1_free"] = p.sV.n + 1
                    p.sV.inc(v.tensor_tensor(yb[:], psY[0:64, :], rb_sb[:], ALU.mult))
                p.VEC(vy_sec)
            mults_n = p.sV.n

            def tsh_sec(t, mults_n=mults_n):
                t.wait_ge(p.sV.h, mults_n)
                t.matmul(psU[:], ident[0:64, :], ybfA[:], start=True, stop=False)
                p.sP.inc(t.matmul(psU[:], idsh[0:64, :], ybfB[:],
                                  start=False, stop=True))
            p.TEN(tsh_sec)
            tsh_n = p.sP.n

            def vadd_sec(v, g=g, qw=qw, tsh_n=tsh_n):
                v.wait_ge(p.sP.h, tsh_n)
                xs = x_fm[:, g * T + qw * TB: g * T + qw * TB + TB]
                mk["psC1_free"] = p.sV.inc(v.tensor_tensor(xs, xs, psU[:], ALU.add))
                mk["y_done"] = mk["psC1_free"]
            p.VEC(vadd_sec)
        mk["scores_done"] = p.sP.n
    p.pop()


def emit_spill_x(p, env):
    import os as _os
    x_scr = p.nc.dram_tensor("x_scr", [C, T], F32, kind="Internal").ap()
    env["x_scr_v"] = x_scr.rearrange("(cc p) t -> cc p t", p=128)
    x_fm = env["x_fm"]
    bar = p.snap()
    dbgx = _os.environ.get("DBGX", "0") == "1"
    if dbgx:
        x3d = p.nc.dram_tensor("x3dbg", [C, T], F32, kind="ExternalOutput").ap()
        x3v = x3d.rearrange("(cc p) t -> cc p t", p=128)

    def sec(s):
        p.wait_snap(s, bar)
        for cc in range(NCC):
            p.sS.inc(s.dma_start(out=env["x_scr_v"][cc],
                                 in_=x_fm[:, cc * T:(cc + 1) * T]), 16)
            if dbgx:
                p.sO.inc(s.dma_start(out=x3v[cc],
                                     in_=x_fm[:, cc * T:(cc + 1) * T]), 16)
    p.SYN(sec)
    env["x_spilled"] = p.sS.n


def emit_l1_ffn(p, env):
    xn_scr_v = env["xn_scr_v"]
    xn_ready = env["xn_ready"]
    x_scr_v = env["x_scr_v"]
    xout = p.nc.dram_tensor("xout", [C, T], F32, kind="ExternalOutput").ap()
    xout_v = xout.rearrange("(cc p) t -> cc p t", p=128)

    F1, F2 = 4096, 4096
    NF1, NF2 = 32, 32

    k0_d = p.inp("l1f_k0t", [32, 128, 16 * 128], U8)
    k1_d = p.inp("l1f_k1t", [32, 128, 32 * 128], U8)
    k2_d = p.inp("l1f_k2t", [8, 128, 32 * 128], U8)
    s0_d = p.inp("l1f_s0", [128, 1], F32)
    s1_d = p.inp("l1f_s1", [128, 1], F32)
    s2_d = p.inp("l1f_s2", [128, 1], F32)

    bar = p.snap()
    import os as _os
    p.serial = frozenset(_os.environ.get("SERIAL3", "s"))
    fixraw = _os.environ.get("FIXRAW", "0") == "1"
    fixact = _os.environ.get("FIXACT", "0") == "1"
    dbg3 = _os.environ.get("DBG3", "0") == "1"
    if dbg3:
        d_wab = p.nc.dram_tensor("d_wab", [NTB * 32, 128, 2048], BF16, kind="ExternalOutput").ap()
        d_w0e = p.nc.dram_tensor("d_w0e", [NTB * 32, 128, C], BF16, kind="ExternalOutput").ap()
        d_xn = p.nc.dram_tensor("d_xn", [NTB, 128, NCC * TB], BF16, kind="ExternalOutput").ap()
        d_h1 = p.nc.dram_tensor("d_h1", [NTB, 128, 32 * TB], BF16, kind="ExternalOutput").ap()
        d_h2 = p.nc.dram_tensor("d_h2", [NTB, 128, 32 * TB], BF16, kind="ExternalOutput").ap()
    p.push()
    wprobe = p.sb("l1f_wprobe", [128, 8], BF16)
    aprobe = p.sb("l1f_aprobe", [128, 8], BF16)
    h1 = p.sb("l1f_h1", [128, NF1 * TB], BF16)
    h2 = p.sb("l1f_h2", [128, NF2 * TB], BF16)
    xn_t = p.sb("l1f_xnt", [128, NCC * TB], BF16)
    wab = [p.sb(f"l1f_wab{i}", [128, 2 * NCC * 128], U8) for i in range(2)]
    w0e = [p.sb(f"l1f_w0e{i}", [128, C], BF16) for i in range(2)]
    wu = [p.sb(f"l1f_wu{i}", [128, F1], U8) for i in range(2)]
    we = [p.sb(f"l1f_we{i}", [128, F1], BF16) for i in range(2)]
    sc_t = p.sb("l1f_sc", [128, 4], F32)
    xblk = [p.sb(f"l1f_xb{i}", [128, TB], F32) for i in range(2)]
    ybuf = [p.sb(f"l1f_yb{i}", [128, TB], F32) for i in range(2)]
    psm = [p.ps(f"l1f_ps{i}", [128, TB]) for i in range(2)]

    def sld_sec(s):
        p.wait_snap(s, bar)
        p.sS.inc(s.dma_start(out=sc_t[:, 0:1], in_=s0_d), 16)
        p.sS.inc(s.dma_start(out=sc_t[:, 1:2], in_=s1_d), 16)
        p.sS.inc(s.dma_start(out=sc_t[:, 2:3], in_=s2_d), 16)
    p.SYN(sld_sec)
    sld_n = p.sS.n

    def sa_sec(s):
        s.wait_ge(p.sS.h, sld_n)
        s.activation(sc_t[:, 0:1], sc_t[:, 0:1], AF.Copy, scale=4.0)
        p.sA.inc(s.activation(sc_t[:, 1:2], sc_t[:, 1:2], AF.Copy, scale=2.0))
    p.SCA(sa_sec)
    scales_n = p.sA.n

    mk = {"conv": {}, "mm": {}, "ev": {}, "l1mm_tb": {}, "l2mm_tb": {}, "l3mm_tb": {},
          "stt": {}, "yst": {}, "xb_used": {}, "wconv": {}, "wmm": {}}
    gidx = 0
    wj = 0

    for tb in range(NTB):
        ts0 = tb * TB

        def s_sec(s, tb=tb):
            if tb == 0:
                p.wait_snap(s, bar)
                s.wait_ge(p.sS.h, env["x_spilled"])
            s.wait_ge(p.sS.h, xn_ready[tb])
            if tb >= 1:
                s.wait_ge(p.sP.h, mk["l1mm_tb"][tb - 1])
            p.sS.inc(s.dma_start(
                out=xn_t.rearrange("p (cc t) -> p cc t", cc=NCC),
                in_=xn_scr_v[:, :, ts0:ts0 + TB]), 16)
        p.SYN(s_sec)
        xn_n = p.sS.n

        if dbg3:
            def dxn_sec(s, tb=tb, xn_n=xn_n):
                s.wait_ge(p.sG.h, xn_n)
                p.sO.inc(s.dma_start(out=d_xn[tb], in_=xn_t[:]), 16)
            p.SYN(dxn_sec)

        l1_pb = p.sP.n
        l1_ab = p.sA.n
        for fc in range(NF1):
            ci = tb * NF1 + fc

            def g_sec(g, tb=tb, fc=fc, ci=ci):
                if tb == 0 and fc == 0:
                    p.wait_snap(g, bar)
                if (ci - 2) in mk["conv"]:
                    g.wait_ge(p.sV.h, mk["conv"][ci - 2])
                p.sS.inc(g.dma_start(out=wab[fc % 2][:], in_=k0_d[fc]), 16)
            p.SYN(g_sec)
            gl_n = p.sS.n

            def v_sec(v, fc=fc, ci=ci, gl_n=gl_n):
                v.wait_ge(p.sS.h, gl_n)
                if (ci - 2) in mk["mm"]:
                    v.wait_ge(p.sP.h, mk["mm"][ci - 2])
                tt = v.tensor_tensor(w0e[fc % 2][:], wab[fc % 2][:, 0:NCC * 128],
                                     wab[fc % 2][:, NCC * 128:], ALU.subtract)
                if fixraw:
                    tt = v.tensor_copy(wprobe[:], w0e[fc % 2][:, 0:8])
                mk["conv"][ci] = p.sV.inc(tt)
            p.VEC(v_sec)
            cv_n = p.sV.n

            if dbg3:
                def dw_sec(s, tb=tb, fc=fc, ci=ci, gl_n=gl_n, cv_n=cv_n):
                    s.wait_ge(p.sG.h, gl_n)
                    s.wait_ge(p.sV.h, cv_n)
                    p.sO.inc(s.dma_start(out=d_wab[ci], in_=wab[fc % 2][:]), 16)
                    p.sO.inc(s.dma_start(out=d_w0e[ci], in_=w0e[fc % 2][:]), 16)
                    mk["dw"] = p.sO.n
                p.SYN(dw_sec)

            def t_sec(t, tb=tb, fc=fc, ci=ci, cv_n=cv_n, xn_n=xn_n, gi=gidx,
                      l1_ab=l1_ab):
                t.wait_ge(p.sV.h, cv_n)
                if fc == 0:
                    t.wait_ge(p.sS.h, xn_n)
                if fc >= 2:
                    t.wait_ge(p.sA.h, l1_ab + fc - 1)
                elif (gi - 2) in mk["ev"]:
                    eng, cnt = mk["ev"][gi - 2]
                    t.wait_ge((p.sA if eng == "A" else p.sV).h, cnt)
                for cc in range(NCC):
                    mm = t.matmul(psm[gi % 2][:],
                                  w0e[fc % 2][:, cc * 128:(cc + 1) * 128],
                                  xn_t[:, cc * TB:(cc + 1) * TB],
                                  start=(cc == 0), stop=(cc == NCC - 1))
                mk["mm"][ci] = p.sP.inc(mm)
            p.TEN(t_sec)
            gidx += 1

        gi0 = gidx - NF1

        def a_l1(s, tb=tb, l1_pb=l1_pb, gi0=gi0, scales_n=scales_n):
            for fc in range(NF1):
                s.wait_ge(p.sP.h, l1_pb + fc + 1)
                if tb == 0 and fc == 0:
                    s.wait_ge(p.sA.h, scales_n)
                p.sA.inc(s.activation(h1[:, fc * TB:(fc + 1) * TB],
                                      psm[(gi0 + fc) % 2][:],
                                      AF.Relu, scale=sc_t[:, 0:1]))
        p.SCA(a_l1)
        for fc in range(NF1):
            mk["ev"][gi0 + fc] = ("A", l1_ab + fc + 1)
        l1ev_n = p.sA.n
        mk["l1mm_tb"][tb] = p.sP.n

        if dbg3:
            def dh1_sec(s, tb=tb, l1ev_n=l1ev_n):
                s.wait_ge(p.sA.h, l1ev_n)
                p.sO.inc(s.dma_start(out=d_h1[tb], in_=h1[:]), 16)
            p.SYN(dh1_sec)

        l2_pb = p.sP.n
        l2_ab = p.sA.n
        for fc in range(NF2):
            ci = 1000 + tb * NF2 + fc

            def g_sec(g, fc=fc, wj=wj):
                if (wj - 2) in mk["wconv"]:
                    g.wait_ge(p.sV.h, mk["wconv"][wj - 2])
                p.sS.inc(g.dma_start(out=wu[wj % 2][:], in_=k1_d[fc]), 16)
            p.SYN(g_sec)
            gl_n = p.sS.n

            def v_sec(v, wj=wj, gl_n=gl_n):
                v.wait_ge(p.sS.h, gl_n)
                if (wj - 2) in mk["wmm"]:
                    v.wait_ge(p.sP.h, mk["wmm"][wj - 2])
                tt = v.tensor_scalar(we[wj % 2][:], wu[wj % 2][:], 2.0, -1.0,
                                     ALU.mult, ALU.add)
                if fixraw:
                    tt = v.tensor_copy(wprobe[:], we[wj % 2][:, 0:8])
                mk["wconv"][wj] = p.sV.inc(tt)
            p.VEC(v_sec)
            cv_n = p.sV.n

            def t_sec(t, fc=fc, wj=wj, cv_n=cv_n, gi=gidx, l1ev_n=l1ev_n,
                      l2_ab=l2_ab):
                t.wait_ge(p.sV.h, cv_n)
                if fc == 0:
                    t.wait_ge(p.sA.h, l1ev_n)
                if fc >= 2:
                    t.wait_ge(p.sA.h, l2_ab + fc - 1)
                elif (gi - 2) in mk["ev"]:
                    eng, cnt = mk["ev"][gi - 2]
                    t.wait_ge((p.sA if eng == "A" else p.sV).h, cnt)
                for cc in range(NF1):
                    mm = t.matmul(psm[gi % 2][:],
                                  we[wj % 2][:, cc * 128:(cc + 1) * 128],
                                  h1[:, cc * TB:(cc + 1) * TB],
                                  start=(cc == 0), stop=(cc == NF1 - 1))
                mk["wmm"][wj] = p.sP.inc(mm)
            p.TEN(t_sec)
            gidx += 1
            wj += 1

        gi0 = gidx - NF2

        def a_l2(s, tb=tb, l2_pb=l2_pb, gi0=gi0):
            for fc in range(NF2):
                s.wait_ge(p.sP.h, l2_pb + fc + 1)
                p.sA.inc(s.activation(h2[:, fc * TB:(fc + 1) * TB],
                                      psm[(gi0 + fc) % 2][:],
                                      AF.Relu, scale=sc_t[:, 1:2]))
        p.SCA(a_l2)
        for fc in range(NF2):
            mk["ev"][gi0 + fc] = ("A", l2_ab + fc + 1)
        l2ev_n = p.sA.n
        mk["l2mm_tb"][tb] = p.sP.n

        if dbg3:
            def dh2_sec(s, tb=tb, l2ev_n=l2ev_n):
                s.wait_ge(p.sA.h, l2ev_n)
                p.sO.inc(s.dma_start(out=d_h2[tb], in_=h2[:]), 16)
            p.SYN(dh2_sec)

        for oc in range(NCC):
            ci = 2000 + tb * NCC + oc

            def g_sec(g, oc=oc, wj=wj):
                if (wj - 2) in mk["wconv"]:
                    g.wait_ge(p.sV.h, mk["wconv"][wj - 2])
                p.sS.inc(g.dma_start(out=wu[wj % 2][:], in_=k2_d[oc]), 16)
            p.SYN(g_sec)
            gl_n = p.sS.n

            def sx_sec(s, tb=tb, oc=oc, ci=ci):
                if (ci - 2) in mk["xb_used"]:
                    s.wait_ge(p.sV.h, mk["xb_used"][ci - 2])
                p.sS.inc(s.dma_start(out=xblk[oc % 2][:],
                                     in_=x_scr_v[oc][:, tb * TB:(tb + 1) * TB]), 16)
            p.SYN(sx_sec)
            xb_n = p.sS.n

            def v_sec(v, wj=wj, gl_n=gl_n):
                v.wait_ge(p.sS.h, gl_n)
                if (wj - 2) in mk["wmm"]:
                    v.wait_ge(p.sP.h, mk["wmm"][wj - 2])
                tt = v.tensor_scalar(we[wj % 2][:], wu[wj % 2][:], 2.0, -1.0,
                                     ALU.mult, ALU.add)
                if fixraw:
                    tt = v.tensor_copy(wprobe[:], we[wj % 2][:, 0:8])
                mk["wconv"][wj] = p.sV.inc(tt)
            p.VEC(v_sec)
            cv_n = p.sV.n

            def t_sec(t, oc=oc, wj=wj, cv_n=cv_n, gi=gidx, l2ev_n=l2ev_n):
                t.wait_ge(p.sV.h, cv_n)
                if oc == 0:
                    t.wait_ge(p.sA.h, l2ev_n)
                if (gi - 2) in mk["ev"]:
                    eng, cnt = mk["ev"][gi - 2]
                    t.wait_ge((p.sA if eng == "A" else p.sV).h, cnt)
                for cc in range(NF2):
                    mm = t.matmul(psm[gi % 2][:],
                                  we[wj % 2][:, cc * 128:(cc + 1) * 128],
                                  h2[:, cc * TB:(cc + 1) * TB],
                                  start=(cc == 0), stop=(cc == NF2 - 1))
                mk["wmm"][wj] = p.sP.inc(mm)
            p.TEN(t_sec)
            mm_n = p.sP.n

            def vy_sec(v, tb=tb, oc=oc, ci=ci, mm_n=mm_n, xb_n=xb_n, gi=gidx):
                v.wait_ge(p.sP.h, mm_n)
                v.wait_ge(p.sS.h, xb_n)
                if (ci - 2) in mk["yst"]:
                    v.wait_ge(p.sO.h, mk["yst"][ci - 2])
                out = p.sV.inc(
                    v.scalar_tensor_tensor(ybuf[oc % 2][:], psm[gi % 2][:],
                                           sc_t[:, 2:3], xblk[oc % 2][:],
                                           ALU.mult, ALU.add))
                mk["xb_used"][ci] = out
                mk["stt"][ci] = out
                mk["ev"][gi] = ("V", out)
            p.VEC(vy_sec)
            stt_n = p.sV.n

            def sy_sec(s, tb=tb, oc=oc, ci=ci, stt_n=stt_n):
                s.wait_ge(p.sV.h, stt_n)
                mk["yst"][ci] = p.sO.inc(
                    s.dma_start(out=xout_v[oc][:, tb * TB:(tb + 1) * TB],
                                in_=ybuf[oc % 2][:]), 16)
            p.SYN(sy_sec)
            gidx += 1
            wj += 1
    p.serial = frozenset()
    p.pop()


def build_program(blocks):
    nc = bass.Bass("TRN2", target_bir_lowering=False)
    st = ExitStack()
    with st:
        p = Prog(nc, st)
        env = {}

        ident = p.sb("ident", [128, 128], BF16)
        pswap = p.sb("pswap", [128, 128], BF16)
        ones_col = p.sb("ones_col", [128, 1], BF16)
        ones_row = p.sb("ones_row", [1, 128], BF16)
        ones_row_f = p.sb("ones_row_f", [1, 128], F32)
        env.update(ident=ident, pswap=pswap, ones_col=ones_col, ones_row=ones_row)

        id_d = p.inp("ident_in", [128, 128], BF16)
        ps_d = p.inp("pswap_in", [128, 128], BF16)
        idsh = p.sb("idsh", [128, 128], BF16)
        env["idsh"] = idsh
        ish_d = p.inp("idsh_in", [128, 128], BF16)

        def const_sec(gq):
            p.sG.inc(gq.dma_start(out=ident[:], in_=id_d), 16)
            p.sG.inc(gq.dma_start(out=pswap[:], in_=ps_d), 16)
            p.sG.inc(gq.dma_start(out=idsh[:], in_=ish_d), 16)
        p.GPS(const_sec)

        eps_t = p.sb("eps_t", [1, 1], F32)
        env["eps_t"] = eps_t

        def ones_sec(v):
            v.memset(ones_col[:], 1.0)
            v.memset(eps_t[:], EPS)
            v.memset(ones_row_f[:], 1.0)
            p.sV.inc(v.memset(ones_row[:], 1.0))
        p.VEC(ones_sec)
        env["ones_row_f"] = ones_row_f

        xn_scr = nc.dram_tensor("xn_scr", [C, T], BF16, kind="Internal").ap()
        env["xn_scr_v"] = xn_scr.rearrange("(cc p) t -> p cc t", p=128)

        p.push()
        x_fm = p.sb("x_fm", [128, NCC * T], F32)
        env["x_fm"] = x_fm
        emit_load_x(p, env)

        if 0 in blocks:
            emit_ln(p, env, "0")
            emit_l0_attn(p, env)

        if 1 in blocks:
            emit_ln(p, env, "1")
            emit_l0_ffn(p, env)

        if 2 in blocks:
            emit_ln(p, env, "2")
            emit_l1_attn(p, env)

        if 3 in blocks:
            emit_ln(p, env, "3")
            emit_spill_x(p, env)
            p.pop()
            emit_l1_ffn(p, env)
        else:
            emit_store_x(p, env)
            p.pop()
        emit_final_wait(p)

    return nc, sorted(p.used_inputs)


def host_constants():
    bf = ml_dtypes.bfloat16
    ident = np.eye(128, dtype=np.float32)
    pswap = np.zeros((128, 128), np.float32)
    for j in range(128):
        pswap[j ^ 32, j] = 1.0
    inv_freq = (1.0 / (10000.0 ** (np.arange(0, HD, 2, dtype=np.float32) / HD))).astype(np.float32)
    tpos = np.arange(T, dtype=np.float32)
    freqs = tpos[:, None] * inv_freq[None, :]
    cos_t = np.cos(freqs).astype(bf).astype(np.float32)
    sin_t = np.sin(freqs).astype(bf).astype(np.float32)
    Ctab = np.zeros((128, T), np.float32)
    Stab = np.zeros((128, T), np.float32)
    for pr in range(128):
        i = pr % 32
        Ctab[pr] = cos_t[:, i]
        Stab[pr] = sin_t[:, i] * (1.0 if (pr % 64) < 32 else -1.0)
    cmask = np.zeros((128, 4 * TB), np.float32)
    qf = np.arange(TB)
    for r in range(4):
        for kp in range(128):
            cmask[kp, r * TB:(r + 1) * TB] = np.where(qf >= r * 128 + kp, 1.0 / 4096.0, 0.0)
    idsh = np.zeros((128, 128), np.float32)
    for i in range(64):
        idsh[i, i + 64] = 1.0
    return {"ident_in": ident.astype(bf), "pswap_in": pswap.astype(bf),
            "idsh_in": idsh.astype(bf),
            "rope_c": Ctab.astype(bf), "rope_s": Stab.astype(bf),
            "cmask": cmask.astype(bf)}


def host_prep(inputs):
    f32 = np.float32
    shared = dict(host_constants())
    tr = lambda a: np.ascontiguousarray(np.asarray(a, dtype=f32).T)
    shared["l0q1"] = tr(inputs["L0_attn_Wq1"])
    shared["l0k1"] = tr(inputs["L0_attn_Wk1"])
    shared["l0q2"] = tr(inputs["L0_attn_Wq2"])
    shared["l0k2"] = tr(inputs["L0_attn_Wk2"])
    shared["l0v"] = tr(inputs["L0_attn_Wv"])
    shared["l0o"] = tr(inputs["L0_attn_Wout"])
    shared["l0fr"] = tr(inputs["L0_ffn_W_r"])
    shared["l0fk"] = tr(inputs["L0_ffn_W_k"])
    shared["l0fv"] = tr(inputs["L0_ffn_W_v"])
    shared["l0f_mur"] = np.ascontiguousarray(
        np.asarray(inputs["L0_ffn_mu_r"], f32).reshape(NCC, 128).T)
    shared["l0f_muk"] = np.ascontiguousarray(
        np.asarray(inputs["L0_ffn_mu_k"], f32).reshape(NCC, 128).T)
    cat = lambda w: np.ascontiguousarray(
        np.asarray(w, f32).transpose(1, 0, 2).reshape(C, C))
    shared["l1q"] = cat(inputs["L1_attn_W_Q"])
    shared["l1k"] = cat(inputs["L1_attn_W_K"])
    shared["l1v"] = cat(inputs["L1_attn_W_V"])
    k0 = np.ascontiguousarray(np.asarray(inputs["L1_ffn_k0"])).view(np.uint8)
    k1_ = np.ascontiguousarray(np.asarray(inputs["L1_ffn_k1"])).view(np.uint8)
    k2_ = np.ascontiguousarray(np.asarray(inputs["L1_ffn_k2"])).view(np.uint8)

    def wtile(k, ng, nf):
        return np.ascontiguousarray(
            k.reshape(ng, 128, nf, 128).transpose(2, 1, 0, 3).reshape(nf, 128, ng * 128))
    shared["l1f_k0t"] = wtile(k0, 16, 32)
    shared["l1f_k1t"] = wtile(k1_, 32, 32)
    shared["l1f_k2t"] = wtile(k2_, 32, 8)
    for i in (0, 1, 2):
        shared[f"l1f_s{i}"] = np.ascontiguousarray(np.broadcast_to(
            np.asarray(inputs[f"L1_ffn_s{i}"], f32).reshape(1, 1), (128, 1)))
    return shared


LAUNCH_GROUPS = [[0, 1, 2, 3]]


def kernel(**inputs):
    x = np.asarray(inputs["L0_attn_x"], dtype=np.float32)
    shared = host_prep(inputs)
    state = [np.ascontiguousarray(x[b].T) for b in range(8)]

    for grp in LAUNCH_GROUPS:
        nc, needed = build_program(grp)
        in_maps = []
        for b in range(8):
            m = {}
            for k in needed:
                m[k] = state[b] if k == "x" else shared[k]
            in_maps.append(m)
        res = run_bass_kernel_spmd(nc, in_maps, core_ids=list(range(8)))
        state = [np.asarray(res.results[b]["xout"]) for b in range(8)]

    out = np.stack([s.T for s in state]).astype(np.float32)
    return np.ascontiguousarray(out)
